# version 08
# baseline (speedup 1.0000x reference)
import contextlib
import os
import numpy as np

MAMBA_PHASES = int(os.environ.get("MAMBA_PHASES", "2"))

import concourse.bass as bass
import concourse.mybir as mybir
from concourse.bass_utils import run_bass_kernel_spmd

F32 = mybir.dt.float32
BF16 = mybir.dt.bfloat16
AF = mybir.ActivationFunctionType
ALU = mybir.AluOpType
AX = mybir.AxisListType

D_MODEL = 1024
T_FULL = 2048
NDT = 8
N_CORES = 8

ENGINES = ("sync", "tensor", "vector", "scalar", "gpsimd")

ALL_STAGES = ["l0a", "l0f1", "l0b", "l0f2", "l1a", "l1f1", "l1b", "l1f2"]
LAUNCH_GROUPS = [ALL_STAGES]


def ts(i, n=128):
    return slice(i * n, (i + 1) * n)


class Sched:

    NPOOL = {"sync": 6, "tensor": 12, "vector": 12, "scalar": 12, "gpsimd": 6}
    NDMA = {"sync": 8, "gpsimd": 8, "scalar": 4}

    def __init__(self, nc, block, sems):
        self.nc = nc
        self.block = block
        self.sems = sems
        self.cnt = {}
        for e in ENGINES:
            for p in range(self.NPOOL[e]):
                self.cnt[f"{e}:c{p}"] = 0
            for p in range(self.NDMA.get(e, 0)):
                self.cnt[f"{e}:d{p}"] = 0
        self.gi = {e: 0 for e in ENGINES}
        self.dma_i = {e: 0 for e in ENGINES}
        self.eclock = {e: {} for e in ENGINES}
        self.gclock = {}
        self.bufs = {}
        self.pending = {e: [] for e in ENGINES}
        self.nbar = 0

    def emit(self, eng, fn, reads=(), writes=(), dma=False, extra_waits=()):
        need = dict(extra_waits)
        ec = self.eclock[eng]

        def add_clock(c):
            for f, v in c.items():
                if ec.get(f, 0) < v and need.get(f, 0) < v:
                    need[f] = v

        for k in reads:
            b = self.bufs.get(k)
            if b is not None:
                add_clock(b["w"])
        for k in writes:
            b = self.bufs.get(k)
            if b is not None:
                add_clock(b["w"])
                add_clock(b["r"])
        if dma:
            p = self.dma_i[eng] % self.NDMA[eng]
            self.dma_i[eng] += 1
            tag = f"{eng}:d{p}"
            inc = 16
        else:
            p = self.gi[eng] % self.NPOOL[eng]
            self.gi[eng] += 1
            tag = f"{eng}:c{p}"
            inc = 1
        prev = self.cnt[tag]
        if prev and ec.get(tag, 0) < prev and need.get(tag, 0) < prev:
            need[tag] = prev
        self.cnt[tag] += inc
        mycnt = self.cnt[tag]
        waits = sorted(need.items())
        for f, v in waits:
            if ec.get(f, 0) < v:
                ec[f] = v
            pc = self.gclock.get((f, v))
            if pc:
                for f2, v2 in pc.items():
                    if ec.get(f2, 0) < v2:
                        ec[f2] = v2
        myclock = dict(ec)
        myclock[tag] = mycnt
        self.gclock[(tag, mycnt)] = myclock
        for k in writes:
            self.bufs[k] = {"w": {tag: mycnt}, "r": {}}
        for k in reads:
            b = self.bufs.setdefault(k, {"w": {}, "r": {}})
            b["r"][tag] = mycnt
        sems = self.sems

        def run(h, waits=waits, fn=fn, tag=tag, inc=inc):
            for f, v in waits:
                h.wait_ge(sems[f], v)
            fn(h).then_inc(sems[tag], inc)

        self.pending[eng].append(run)
        return (tag, mycnt)

    def flush(self):
        for eng in ENGINES:
            lst = self.pending[eng]
            if not lst:
                continue

            def body(h, lst=tuple(lst)):
                for r in lst:
                    r(h)

            getattr(self.block, eng)(body)
            self.pending[eng] = []

    def drain_waits(self, eng):
        ec = self.eclock[eng]
        out = []
        for p in range(self.NPOOL[eng]):
            tag = f"{eng}:c{p}"
            if self.cnt[tag] and ec.get(tag, 0) < self.cnt[tag]:
                out.append((tag, self.cnt[tag]))
        for p in range(self.NDMA.get(eng, 0)):
            tag = f"{eng}:d{p}"
            if self.cnt[tag] and ec.get(tag, 0) < self.cnt[tag]:
                out.append((tag, self.cnt[tag]))
        return tuple(out)

    def barrier(self):
        n = self.nbar
        self.nbar += 1
        keys = tuple(f"__bar{n}_{e}" for e in ENGINES)
        for e, k in zip(ENGINES, keys):
            self.emit(e, lambda h: h.nop(), writes=(k,),
                      extra_waits=self.drain_waits(e))
        for e in ENGINES:
            self.emit(e, lambda h: h.nop(), reads=keys)


class Ctx:
    pass


def build_consts(C, es):
    nc, S = C.nc, C.S
    C.ident_f = es.enter_context(nc.sbuf_tensor("ident_f", [128, 128], F32))
    C.ident_b = es.enter_context(nc.sbuf_tensor("ident_b", [128, 128], BF16))
    C.ones_col = es.enter_context(nc.sbuf_tensor("ones_col", [128, 1], F32))
    C.ones_row = es.enter_context(nc.sbuf_tensor("ones_row", [1, 128], F32))
    C.cpool = es.enter_context(nc.sbuf_tensor("cpool", [128, 4], F32))

    for t, key in ((C.ident_f, "ident_f"), (C.ident_b, "ident_b")):
        S.emit("gpsimd", lambda h, t=t: h.memset(t[:], 0.0), writes=(key,))
        S.emit("gpsimd", lambda h, t=t: h.affine_select(
            out=t[:], in_=t[:], compare_op=ALU.not_equal, fill=1.0, base=0,
            pattern=[[-1, 128]], channel_multiplier=1),
            reads=(key,), writes=(key,))
    S.emit("gpsimd", lambda h: h.memset(C.ones_col[:], 1.0), writes=("ones_col",))
    S.emit("gpsimd", lambda h: h.memset(C.ones_row[:], 1.0), writes=("ones_row",))
    for col, val in ((0, 1e-5), (1, 1e-6), (2, 1.0), (3, 0.0)):
        S.emit("gpsimd", lambda h, col=col, val=val: h.memset(
            C.cpool[:, col:col + 1], val), writes=("cpool",))
    nc.const_aps.register(F32, 1e-5, C.cpool[:, 0:1])
    nc.const_aps.register(F32, 1e-6, C.cpool[:, 1:2])


def alloc_esel(C, es, tag):
    nc, S = C.nc, C.S
    C.esel = es.enter_context(nc.sbuf_tensor(f"esel_{tag}", [16, 16, 128], F32))
    S.emit("gpsimd", lambda h: h.memset(C.esel[:], 0.0), writes=("esel",))
    S.emit("gpsimd", lambda h: h.affine_select(
        out=C.esel[:], in_=C.esel[:], compare_op=ALU.not_equal, fill=1.0, base=0,
        pattern=[[-1, 16], [0, 128]], channel_multiplier=1),
        reads=("esel",), writes=("esel",))


def stage_pro(C, es):
    nc, S = C.nc, C.S
    x_tm = C.dram["x_tm"]
    xt = es.enter_context(nc.sbuf_tensor("pro_xt", [128, 2, 1024], F32))
    ps = es.enter_context(nc.psum_tensor("pro_ps", [128, 8, 512], F32))
    for tt in range(16):
        sl = tt % 2
        S.emit("sync", lambda h, tt=tt, sl=sl: h.dma_start(
            out=xt[:, sl, :], in_=x_tm[ts(tt), :]),
            writes=(f"pro_xt{sl}",), dma=True)
        for dt in range(NDT):
            pb = dt % 8
            S.emit("tensor", lambda h, sl=sl, dt=dt, pb=pb: h.transpose(
                ps[:, pb, 0:128], xt[:, sl, ts(dt)], C.ident_f[:]),
                reads=(f"pro_xt{sl}", "ident_f"), writes=(f"pro_ps{pb}",))
            S.emit("scalar", lambda h, tt=tt, dt=dt, pb=pb: h.copy(
                out=C.x_res[:, dt, ts(tt)], in_=ps[:, pb, 0:128]),
                reads=(f"pro_ps{pb}",), writes=("x_res",))


def stage_epi(C, es):
    nc, S = C.nc, C.S
    out_tm = C.dram["out_tm"]
    ot = es.enter_context(nc.sbuf_tensor("epi_ot", [128, 2, 1024], F32))
    ps = es.enter_context(nc.psum_tensor("epi_ps", [128, 8, 512], F32))
    for tt in range(16):
        sl = tt % 2
        for dt in range(NDT):
            pb = dt % 8
            S.emit("tensor", lambda h, tt=tt, dt=dt, pb=pb: h.transpose(
                ps[:, pb, 0:128], C.x_res[:, dt, ts(tt)], C.ident_f[:]),
                reads=("x_res", "ident_f"), writes=(f"epi_ps{pb}",))
            S.emit("scalar", lambda h, dt=dt, sl=sl, pb=pb: h.copy(
                out=ot[:, sl, ts(dt)], in_=ps[:, pb, 0:128]),
                reads=(f"epi_ps{pb}",), writes=(f"epi_ot{sl}",))
        S.emit("sync", lambda h, tt=tt, sl=sl: h.dma_start(
            out=out_tm[ts(tt), :], in_=ot[:, sl, :]),
            reads=(f"epi_ot{sl}",), writes=("out_done",), dma=True)


def stage_load_res(C, es):
    S = C.S
    xin = C.dram["xin_fm"]
    S.emit("sync", lambda h: h.dma_start(
        out=C.x_res[:], in_=xin.rearrange("(a p) t -> p a t", p=128)),
        writes=("x_res",), dma=True)


def stage_store_res(C, es):
    S = C.S
    xout = C.dram["xout_fm"]
    S.emit("sync", lambda h: h.dma_start(
        out=xout.rearrange("(a p) t -> p a t", p=128), in_=C.x_res[:]),
        reads=("x_res",), writes=("out_done",), dma=True)


def emit_ln(C, xnorm, xnk, t0, tc, tag, ps, x2, rows, banks=(4, 5, 6, 7),
            keep_ab=None, keep_off=0, keep_key=None, out_t0=None):
    S = C.S
    ba, bb, b1, b2 = banks
    k = "ln" + tag
    kx2 = "buf_" + x2.name
    krows = "buf_" + rows.name
    sl = (slice(None), slice(None), slice(t0, t0 + tc))
    S.emit("scalar", lambda h: h.activation(out=x2[:, :, :tc], in_=C.x_res[sl],
                                            func=AF.Square),
           reads=("x_res",), writes=(kx2,))
    for j in range(NDT):
        S.emit("tensor", lambda h, j=j: h.matmul(
            ps[0:1, b1, :tc], C.ones_col[:], C.x_res[:, j, t0:t0 + tc],
            start=(j == 0), stop=(j == NDT - 1)),
            reads=("x_res", "ones_col"), writes=(f"psb{b1}",))
    for j in range(NDT):
        S.emit("tensor", lambda h, j=j: h.matmul(
            ps[0:1, b2, :tc], C.ones_col[:], x2[:, j, :tc],
            start=(j == 0), stop=(j == NDT - 1)),
            reads=(kx2, "ones_col"), writes=(f"psb{b2}",))
    S.emit("vector", lambda h: h.tensor_scalar_mul(
        out=rows[:, 2, :tc], in0=ps[0:1, b1, :tc], scalar1=1.0 / D_MODEL),
        reads=(f"psb{b1}",), writes=(krows,))
    S.emit("vector", lambda h: h.tensor_scalar_mul(
        out=rows[:, 3, :tc], in0=ps[0:1, b2, :tc], scalar1=1.0 / D_MODEL),
        reads=(f"psb{b2}", krows), writes=(krows,))
    S.emit("vector", lambda h: h.tensor_mul(
        out=rows[:, 0, :tc], in0=rows[:, 2, :tc], in1=rows[:, 2, :tc]),
        reads=(krows,), writes=(krows,))
    S.emit("vector", lambda h: h.tensor_sub(
        out=rows[:, 3, :tc], in0=rows[:, 3, :tc], in1=rows[:, 0, :tc]),
        reads=(krows,), writes=(krows,))
    S.emit("scalar", lambda h: h.activation(
        out=rows[:, 1, :tc], in_=rows[:, 3, :tc], func=AF.Sqrt,
        bias=C.cpool[0:1, 0:1]),
        reads=(krows, "cpool"), writes=(krows,))
    S.emit("vector", lambda h: h.reciprocal(
        out=rows[:, 4, :tc], in_=rows[:, 1, :tc]),
        reads=(krows,), writes=(krows,))
    S.emit("vector", lambda h: h.tensor_mul(
        out=rows[:, 5, :tc], in0=rows[:, 2, :tc], in1=rows[:, 4, :tc]),
        reads=(krows,), writes=(krows,))
    S.emit("vector", lambda h: h.tensor_scalar_mul(
        out=rows[:, 5, :tc], in0=rows[:, 5, :tc], scalar1=-1.0),
        reads=(krows,), writes=(krows,))
    S.emit("tensor", lambda h: h.matmul(
        ps[:, ba, :tc], C.ones_row[:], rows[:, 4, :tc], start=True, stop=True),
        reads=(krows, "ones_row"), writes=(f"psb{ba}",))
    S.emit("tensor", lambda h: h.matmul(
        ps[:, bb, :tc], C.ones_row[:], rows[:, 5, :tc], start=True, stop=True),
        reads=(krows, "ones_row"), writes=(f"psb{bb}",))
    S.emit("vector", lambda h: h.tensor_tensor(
        out=x2[:, :, :tc], in0=C.x_res[sl],
        in1=ps.ap()[:, ba:ba + 1, :tc].broadcast_to([128, NDT, tc]), op=ALU.mult),
        reads=("x_res", f"psb{ba}"), writes=(kx2,))
    o0 = t0 if out_t0 is None else out_t0
    S.emit("vector", lambda h: h.tensor_tensor(
        out=xnorm[:, :, o0:o0 + tc], in0=x2[:, :, :tc],
        in1=ps.ap()[:, bb:bb + 1, :tc].broadcast_to([128, NDT, tc]), op=ALU.add),
        reads=(kx2, f"psb{bb}"), writes=(xnk,))
    if keep_ab is not None:
        S.emit("vector", lambda h: h.tensor_copy(
            out=keep_ab[:, 0:2, keep_off:keep_off + tc],
            in_=rows[:, 4:6, :tc]),
            reads=(krows,), writes=(keep_key,))


def emit_mamba1(C, tag, n_mamba, t_sub):
    nc, S = C.nc, C.S
    W = C.dram_w[tag]
    xz_dram = C.dram[f"{tag}_xz"]
    with contextlib.ExitStack() as es:
        ps = es.enter_context(nc.psum_tensor(f"{tag}_psA", [128, 8, 512], F32))
        x2 = es.enter_context(nc.sbuf_tensor(f"{tag}_x2", [128, 8, 512], F32))
        rows = es.enter_context(nc.sbuf_tensor(f"{tag}_rows", [1, 6, 512], F32))
        xnorm = es.enter_context(nc.sbuf_tensor(f"{tag}_xn", [128, 8, T_FULL], BF16))
        wblk = es.enter_context(nc.sbuf_tensor(f"{tag}_wblk", [128, 2, 8, 512], BF16))
        evb = es.enter_context(nc.sbuf_tensor(f"{tag}_evb", [128, 2, 4, 512], BF16))
        if n_mamba > 1:
            xnsub = es.enter_context(nc.sbuf_tensor(f"{tag}_xns", [128, 8, t_sub], BF16))
        for ch in range(4):
            emit_ln(C, xnorm, "xnorm", ch * 512, 512, tag + str(ch), ps, x2, rows)
        if MAMBA_PHASES < 1:
            S.barrier()
            S.flush()
            return
        tcA = min(512, t_sub)
        ntc = max(1, t_sub // 512)
        for m in range(n_mamba):
            if n_mamba > 1:
                S.emit("vector", lambda h, m=m: h.tensor_copy(
                    out=xnsub[:], in_=xnorm[:, :, m::n_mamba]),
                    reads=("xnorm",), writes=("xnsub",))
                xn, xnk = xnsub, "xnsub"
            else:
                xn, xnk = xnorm, "xnorm"
            for ib in range(int(os.environ.get("MAMBA_IB", "8"))):
                wsl = ib % 2
                if True:
                    S.emit("gpsimd", lambda h, m=m, ib=ib, wsl=wsl: h.dma_start(
                        out=wblk[:, wsl, :, :],
                        in_=W["w_inT"][m].rearrange("(kt p) i -> p kt i", p=128)[:, :, ts(ib, 512)]),
                        writes=(f"{tag}_wblk{wsl}",), dma=True)
                if os.environ.get("MAMBA_NOMM"):
                    continue
                for ch in range(ntc):
                    esl = (ib * ntc + ch) % 2
                    for i in range(4):
                        pb = (ib * 4 + i) % 8
                        for kk in range(8):
                            S.emit("tensor", lambda h, ch=ch, i=i, kk=kk, wsl=wsl, pb=pb: h.matmul(
                                ps[:, pb, :tcA], wblk[:, wsl, kk, ts(i)],
                                xn[:, kk, ch * 512:ch * 512 + tcA],
                                start=(kk == 0), stop=(kk == 7)),
                                reads=(f"{tag}_wblk{wsl}", xnk), writes=(f"psb{pb}",))
                        S.emit("scalar", lambda h, i=i, esl=esl, pb=pb: h.copy(
                            out=evb[:, esl, i, :tcA], in_=ps[:, pb, :tcA]),
                            reads=(f"psb{pb}",), writes=(f"{tag}_evb{esl}",))
                    if not os.environ.get("MAMBA_NOSPILL"):
                        S.emit("sync", lambda h, m=m, ib=ib, ch=ch, esl=esl: h.dma_start(
                            out=xz_dram[m, ts(ib, 512), ch * 512:ch * 512 + tcA].rearrange(
                                "(i p) t -> p i t", p=128),
                            in_=evb[:, esl, :, :tcA]),
                            reads=(f"{tag}_evb{esl}",), writes=(f"{tag}_xzd",), dma=True)
        S.barrier()
        S.flush()
    if MAMBA_PHASES < 2:
        return
    tc = 256
    nch = t_sub // tc
    with contextlib.ExitStack() as es:
        alloc_esel(C, es, tag)
        ps = es.enter_context(nc.psum_tensor(f"{tag}_psB", [128, 8, 512], F32))
        wxp = es.enter_context(nc.sbuf_tensor(f"{tag}_wxp", [128, 16, 128], BF16))
        wdt = es.enter_context(nc.sbuf_tensor(f"{tag}_wdt", [64, 2048], BF16))
        wout = es.enter_context(nc.sbuf_tensor(f"{tag}_wout", [128, 16, 512], BF16))
        convw = es.enter_context(nc.sbuf_tensor(f"{tag}_cw", [128, 16, 4], F32))
        convb = es.enter_context(nc.sbuf_tensor(f"{tag}_cb", [128, 16], F32))
        bdt = es.enter_context(nc.sbuf_tensor(f"{tag}_bdt", [128, 16], F32))
        Asb = es.enter_context(nc.sbuf_tensor(f"{tag}_A", [128, 256], F32))
        Dsb = es.enter_context(nc.sbuf_tensor(f"{tag}_Dsk", [128, 16], F32))
        carry = es.enter_context(nc.sbuf_tensor(f"{tag}_carry", [128, 256], F32))
        xh = es.enter_context(nc.sbuf_tensor(f"{tag}_xh", [128, 16, tc + 4], BF16))
        zb = es.enter_context(nc.sbuf_tensor(f"{tag}_z", [128, 16, tc], BF16))
        ub = es.enter_context(nc.sbuf_tensor(f"{tag}_u", [128, 16, tc], BF16))
        delta = es.enter_context(nc.sbuf_tensor(f"{tag}_dl", [128, 16, tc], F32))
        du = es.enter_context(nc.sbuf_tensor(f"{tag}_du", [128, 16, tc], F32))
        ybuf = es.enter_context(nc.sbuf_tensor(f"{tag}_y", [128, 16, tc], F32))
        bcrB = es.enter_context(nc.sbuf_tensor(f"{tag}_bcrB", [16, tc], F32))
        bcrC = es.enter_context(nc.sbuf_tensor(f"{tag}_bcrC", [16, tc], F32))
        dtr = es.enter_context(nc.sbuf_tensor(f"{tag}_dtr", [64, tc], BF16))
        dub = es.enter_context(nc.sbuf_tensor(f"{tag}_dub", [128, 16, tc], BF16))
        bcBs = es.enter_context(nc.sbuf_tensor(f"{tag}_bcBs", [128, tc], BF16))
        dAs = es.enter_context(nc.sbuf_tensor(f"{tag}_dAs", [128, 4, tc], F32))
        Xs = es.enter_context(nc.sbuf_tensor(f"{tag}_Xs", [128, 4, tc], BF16))
        hs = es.enter_context(nc.sbuf_tensor(f"{tag}_hs", [128, 2, 16, tc], BF16))
        bcCs = es.enter_context(nc.sbuf_tensor(f"{tag}_bcCs", [128, 2, tc], BF16))
        for m in range(n_mamba):
            mk = f"{tag}m"
            for name, key in (("conv_w", "cw"), ("conv_b", "cb"), ("b_dt", "bdt"),
                              ("D", "D"), ("A_log", "A")):
                dst = {"cw": convw, "cb": convb, "bdt": bdt, "D": Dsb, "A": Asb}[key]
                S.emit("sync", lambda h, m=m, name=name, dst=dst: h.dma_start(
                    out=dst[:], in_=W[name][m]), writes=(mk + key,), dma=True)
            S.emit("scalar", lambda h: h.activation(out=Asb[:], in_=Asb[:], func=AF.Exp),
                   reads=(mk + "A",), writes=(mk + "A",))
            S.emit("vector", lambda h: h.tensor_scalar_mul(
                out=Asb[:], in0=Asb[:], scalar1=-1.0),
                reads=(mk + "A",), writes=(mk + "A",))
            S.emit("gpsimd", lambda h, m=m: h.dma_start(
                out=wxp[:], in_=W["w_xprojT"][m].rearrange("(kt p) i -> p kt i", p=128)),
                writes=(mk + "wxp",), dma=True)
            S.emit("gpsimd", lambda h, m=m: h.dma_start(out=wdt[:], in_=W["w_dtT"][m]),
                   writes=(mk + "wdt",), dma=True)
            S.emit("vector", lambda h: h.memset(carry[:], 0.0), writes=(mk + "carry",))
            for ch in range(min(nch, int(os.environ.get("MAMBA_NCH", "99")))):
                t0 = ch * tc
                xzr = xz_dram[m].rearrange("(j p) t -> p j t", p=128)
                if ch == 0:
                    S.emit("vector", lambda h: h.memset(xh[:, :, 0:3], 0.0),
                           writes=(mk + "xh",))
                    S.emit("sync", lambda h, xzr=xzr: h.dma_start(
                        out=xh[:, :, 3:3 + tc], in_=xzr[:, 0:16, 0:tc]),
                        reads=(f"{tag}_xzd",), writes=(mk + "xh",), dma=True)
                else:
                    S.emit("sync", lambda h, xzr=xzr, t0=t0: h.dma_start(
                        out=xh[:, :, 0:3 + tc], in_=xzr[:, 0:16, t0 - 3:t0 + tc]),
                        reads=(f"{tag}_xzd",), writes=(mk + "xh",), dma=True)
                S.emit("sync", lambda h, xzr=xzr, t0=t0: h.dma_start(
                    out=zb[:], in_=xzr[:, 16:32, t0:t0 + tc]),
                    reads=(f"{tag}_xzd",), writes=(mk + "z",), dma=True)
                BSTEP = int(os.environ.get("MAMBA_BSTEP", "99"))
                if BSTEP < 2:
                    continue
                for j in range(16):
                    S.emit("vector", lambda h, j=j: h.tensor_scalar_mul(
                        out=du[:, j, :], in0=xh[:, j, 0:tc], scalar1=convw[:, j, 0:1]),
                        reads=(mk + "xh", mk + "cw"), writes=(mk + "du",))
                    for k4 in (1, 2, 3):
                        S.emit("vector", lambda h, j=j, k4=k4: h.scalar_tensor_tensor(
                            out=du[:, j, :], in0=xh[:, j, k4:k4 + tc],
                            scalar=convw[:, j, k4:k4 + 1], in1=du[:, j, :],
                            op0=ALU.mult, op1=ALU.add),
                            reads=(mk + "xh", mk + "cw", mk + "du"), writes=(mk + "du",))
                if BSTEP < 3:
                    continue
                for j in range(16):
                    S.emit("scalar", lambda h, j=j: h.activation(
                        out=ub[:, j, :], in_=du[:, j, :], func=AF.Silu,
                        bias=convb[:, j:j + 1]),
                        reads=(mk + "du", mk + "cb"), writes=(mk + "u",))
                if BSTEP < 4:
                    continue
                def _xdbl(h):
                    for kk in range(16):
                        last = h.matmul(ps[:, 7, :tc], wxp[:, kk, :], ub[:, kk, :],
                                        start=(kk == 0), stop=(kk == 15))
                    return last
                S.emit("tensor", _xdbl, reads=(mk + "wxp", mk + "u"),
                       writes=("psb7",))
                S.emit("scalar", lambda h: h.copy(out=bcrB[:, :], in_=ps[0:16, 7, :tc]),
                       reads=("psb7",), writes=(mk + "bcr",))
                S.emit("scalar", lambda h: h.copy(out=bcrC[:, :], in_=ps[32:48, 7, :tc]),
                       reads=("psb7",), writes=(mk + "bcr",))
                S.emit("scalar", lambda h: h.copy(out=dtr[:, :], in_=ps[64:128, 7, :tc]),
                       reads=("psb7",), writes=(mk + "dtr",))
                if BSTEP < 5:
                    continue
                for half in range(2):
                    def _dmm(h, half=half):
                        for jj8 in range(8):
                            last = h.matmul(
                                ps[:, jj8, 0:tc], wdt[:, ts(half * 8 + jj8)],
                                dtr[:, :], start=True, stop=True)
                        return last
                    S.emit("tensor", _dmm, reads=(mk + "dtr", mk + "wdt"),
                           writes=tuple(f"psb{b}" for b in range(8)))

                    def _devac(h, half=half):
                        for jj8 in range(8):
                            last = h.activation(
                                out=delta[:, half * 8 + jj8, :], in_=ps[:, jj8, 0:tc],
                                func=AF.Identity,
                                bias=bdt[:, half * 8 + jj8:half * 8 + jj8 + 1])
                        return last
                    S.emit("scalar", _devac,
                           reads=tuple(f"psb{b}" for b in range(8)) + (mk + "bdt",),
                           writes=(mk + "dl",))
                S.emit("vector", lambda h: h.tensor_scalar_max(
                    out=du[:], in0=delta[:], scalar1=0.0),
                    reads=(mk + "dl",), writes=(mk + "du",))
                S.emit("scalar", lambda h: h.activation(
                    out=delta[:], in_=delta[:], func=AF.Abs),
                    reads=(mk + "dl",), writes=(mk + "dl",))
                S.emit("scalar", lambda h: h.activation(
                    out=delta[:], in_=delta[:], func=AF.Exp, scale=-1.0),
                    reads=(mk + "dl",), writes=(mk + "dl",))
                S.emit("scalar", lambda h: h.activation(
                    out=delta[:], in_=delta[:], func=AF.Ln, bias=1.0),
                    reads=(mk + "dl",), writes=(mk + "dl",))
                S.emit("vector", lambda h: h.tensor_add(
                    out=delta[:], in0=delta[:], in1=du[:]),
                    reads=(mk + "dl", mk + "du"), writes=(mk + "dl",))
                if BSTEP < 6:
                    continue
                for j in range(16):
                    S.emit("vector", lambda h, j=j: h.tensor_scalar_mul(
                        out=ybuf[:, j, :], in0=ub[:, j, :], scalar1=Dsb[:, j:j + 1]),
                        reads=(mk + "u", mk + "D"), writes=(mk + "y",))
                S.emit("vector", lambda h: h.tensor_mul(out=dub[:], in0=delta[:], in1=ub[:]),
                       reads=(mk + "dl", mk + "u"), writes=(mk + "dub",))
                for n in range(min(16, int(os.environ.get("MAMBA_NN", "16")))):
                    bb = n % 2
                    cb = 2 + n % 2
                    S.emit("tensor", lambda h, n=n, bb=bb: h.matmul(
                        ps[:, bb, :tc], C.esel[:, n, :], bcrB[:, :],
                        start=True, stop=True),
                        reads=(mk + "bcr", "esel"), writes=(f"psb{bb}",))
                    S.emit("tensor", lambda h, n=n, cb=cb: h.matmul(
                        ps[:, cb, :tc], C.esel[:, n, :], bcrC[:, :],
                        start=True, stop=True),
                        reads=(mk + "bcr", "esel"), writes=(f"psb{cb}",))
                    par = n % 2
                    S.emit("scalar", lambda h, bb=bb: h.copy(
                        out=bcBs[:], in_=ps[:, bb, :tc]),
                        reads=(f"psb{bb}",), writes=(mk + "bcBs",))
                    S.emit("scalar", lambda h, cb=cb, par=par: h.copy(
                        out=bcCs[:, par, :], in_=ps[:, cb, :tc]),
                        reads=(f"psb{cb}",), writes=(f"{mk}bcCs{par}",))
                    for j in range(16):
                        dsl = j % 4
                        S.emit("scalar", lambda h, n=n, j=j, dsl=dsl: h.activation(
                            out=dAs[:, dsl, :], in_=delta[:, j, :], func=AF.Exp,
                            scale=Asb[:, j * 16 + n:j * 16 + n + 1]),
                            reads=(mk + "dl", mk + "A"), writes=(f"{mk}dA{dsl}",))
                        S.emit("vector", lambda h, j=j, dsl=dsl: h.tensor_tensor(
                            out=Xs[:, dsl, :], in0=dub[:, j, :], in1=bcBs[:],
                            op=ALU.mult),
                            reads=(mk + "dub", mk + "bcBs"), writes=(f"{mk}X{dsl}",))
                        S.emit("vector", lambda h, n=n, j=j, dsl=dsl, par=par: h.tensor_tensor_scan(
                            out=hs[:, par, j, :], data0=dAs[:, dsl, :], data1=Xs[:, dsl, :],
                            initial=carry[:, j * 16 + n:j * 16 + n + 1],
                            op0=ALU.mult, op1=ALU.add),
                            reads=(f"{mk}dA{dsl}", f"{mk}X{dsl}", mk + "carry"),
                            writes=(f"{mk}hp{par}",)),
                    S.emit("vector", lambda h, n=n, par=par: h.tensor_copy(
                        out=carry.ap()[:, n:256:16], in_=hs.ap()[:, par, :, tc - 1]),
                        reads=(f"{mk}hp{par}",),
                        writes=(mk + "carry",))
                    S.emit("vector", lambda h, par=par: h.tensor_tensor(
                        out=hs[:, par, :, :], in0=hs[:, par, :, :],
                        in1=bcCs.ap()[:, par:par + 1, :].broadcast_to([128, 16, tc]),
                        op=ALU.mult),
                        reads=(f"{mk}hp{par}", f"{mk}bcCs{par}"),
                        writes=(f"{mk}hp{par}",))
                    S.emit("gpsimd", lambda h, par=par: h.tensor_add(
                        out=ybuf[:], in0=ybuf[:], in1=hs[:, par, :, :]),
                        reads=(f"{mk}hp{par}", mk + "y"),
                        writes=(mk + "y", f"{mk}hp{par}"))
                S.emit("scalar", lambda h: h.activation(
                    out=zb[:], in_=zb[:], func=AF.Silu),
                    reads=(mk + "z",), writes=(mk + "z",))
                S.emit("vector", lambda h: h.tensor_mul(
                    out=ub[:], in0=ybuf[:], in1=zb[:]),
                    reads=(mk + "y", mk + "z"), writes=(mk + "u",))
                for ihalf in range(2):
                    S.emit("gpsimd", lambda h, m=m, ihalf=ihalf: h.dma_start(
                        out=wout[:], in_=W["w_outT"][m].rearrange(
                            "(kt p) i -> p kt i", p=128)[:, :, ihalf * 512:(ihalf + 1) * 512]),
                        writes=(mk + "wout",), dma=True)
                    for i4 in range(4):
                        i = ihalf * 4 + i4
                        pb = 4 + i % 4

                        def _outp(h, i4=i4, pb=pb):
                            for kk in range(16):
                                last = h.matmul(
                                    ps[:, pb, :tc], wout[:, kk, ts(i4)], ub[:, kk, :],
                                    start=(kk == 0), stop=(kk == 15))
                            return last
                        S.emit("tensor", _outp,
                               reads=(mk + "wout", mk + "u"), writes=(f"psb{pb}",))
                        if n_mamba == 1:
                            xsl = (slice(None), i, slice(t0, t0 + tc))
                        else:
                            xsl = (slice(None), i,
                                   slice(m * t_sub + t0, m * t_sub + t0 + tc))
                        S.emit("vector", lambda h, xsl=xsl, pb=pb: h.tensor_tensor(
                            out=C.x_res[xsl], in0=C.x_res[xsl], in1=ps[:, pb, :tc],
                            op=ALU.add),
                            reads=(f"psb{pb}", "x_res"), writes=("x_res",))
        S.barrier()
        S.flush()


def stage_l0a(C, es):
    emit_mamba1(C, "l0a", 1, T_FULL)


def stage_l0b(C, es):
    emit_mamba1(C, "l0b", 4, 512)



def emit_l0ffn(C, tag):
    import contextlib
    nc, S = C.nc, C.S
    W = C.dram_w["l0f"]
    with contextlib.ExitStack() as es:
        alloc_esel(C, es, tag)
        ps = es.enter_context(nc.psum_tensor(f"{tag}_ps", [128, 8, 512], F32))
        x2 = es.enter_context(nc.sbuf_tensor(f"{tag}_x2", [128, 8, 256], F32))
        rows = es.enter_context(nc.sbuf_tensor(f"{tag}_rows", [1, 6, 256], F32))
        gw = es.enter_context(nc.sbuf_tensor(f"{tag}_gw", [128, 8, 16], F32))
        gbb = es.enter_context(nc.sbuf_tensor(f"{tag}_gbb", [128, 16], F32))
        gbr = es.enter_context(nc.sbuf_tensor(f"{tag}_gbr", [1, 16], F32))
        Gs = es.enter_context(nc.sbuf_tensor(f"{tag}_Gs", [1, 16], F32))
        xn = es.enter_context(nc.sbuf_tensor(f"{tag}_xn", [128, 8, 1024], BF16))
        keep = es.enter_context(nc.sbuf_tensor(f"{tag}_keep", [1, 2, 1024], F32))
        comb = es.enter_context(nc.sbuf_tensor(f"{tag}_comb", [16, 1024], F32))
        yacc = es.enter_context(nc.sbuf_tensor(f"{tag}_yacc", [128, 8, 1024], F32))
        w13b = es.enter_context(nc.sbuf_tensor(f"{tag}_w13", [128, 2, 8, 1024], BF16))
        w2b = es.enter_context(nc.sbuf_tensor(f"{tag}_w2", [128, 2, 4, 1024], BF16))
        gst = es.enter_context(nc.sbuf_tensor(f"{tag}_gst", [128, 512], F32))
        hbf = es.enter_context(nc.sbuf_tensor(f"{tag}_hbf", [128, 4, 512], BF16))
        rt = es.enter_context(nc.sbuf_tensor(f"{tag}_rt", [128, 8, 16], F32))
        rc = es.enter_context(nc.sbuf_tensor(f"{tag}_rc", [128, 4], F32))
        S.emit("sync", lambda h: h.dma_start(
            out=gw[:], in_=W["gate_wT"].rearrange("(kt p) e -> p kt e", p=128),
            allow_slow_non_contiguous=True),
            writes=(tag + "gw",), dma=True)
        S.emit("sync", lambda h: h.dma_start(out=gbr[:], in_=W["gate_b"]),
               writes=(tag + "gbr",), dma=True)
        S.emit("tensor", lambda h: h.matmul(
            ps[:, 0, 0:16], C.ones_row[:], gbr[:], start=True, stop=True),
            reads=(tag + "gbr", "ones_row"), writes=("psb0",))
        S.emit("scalar", lambda h: h.copy(out=gbb[:], in_=ps[:, 0, 0:16]),
               reads=("psb0",), writes=(tag + "gbb",))
        def _colsum(h):
            for kk in range(8):
                last = h.matmul(ps[0:1, 1, 0:16], C.ones_col[:], gw[:, kk, :],
                                start=(kk == 0), stop=(kk == 7))
            return last
        S.emit("tensor", _colsum, reads=(tag + "gw", "ones_col"), writes=("psb1",))
        S.emit("scalar", lambda h: h.copy(out=Gs[:], in_=ps[0:1, 1, 0:16]),
               reads=("psb1",), writes=(tag + "Gs",))
        for mc in range(2):
            t0mc = mc * 1024
            for half in range(4):
                emit_ln(C, xn, tag + "xn", t0mc + half * 256, 256,
                        f"{tag}{mc}{half}", ps, x2, rows,
                        keep_ab=keep, keep_off=half * 256, keep_key=tag + "keep",
                        out_t0=half * 256)
            for tt in range(8):
                gsl = slice(t0mc + tt * 128, t0mc + (tt + 1) * 128)
                lsl = slice(tt * 128, (tt + 1) * 128)
                def _s1(h, gsl=gsl):
                    for kk in range(8):
                        last = h.matmul(ps[:, 4, 0:16], C.x_res[:, kk, gsl],
                                        gw[:, kk, :],
                                        start=(kk == 0), stop=(kk == 7))
                    return last
                S.emit("tensor", _s1, reads=("x_res", tag + "gw"), writes=("psb4",))
                S.emit("scalar", lambda h: h.copy(out=rt[:, 0, :], in_=ps[:, 4, 0:16]),
                       reads=("psb4",), writes=(tag + "rt",))
                S.emit("tensor", lambda h, lsl=lsl: h.matmul(
                    ps[:, 5, 0:1], keep[0:1, 0, lsl], C.ones_row[0:1, 0:1],
                    start=True, stop=True),
                    reads=(tag + "keep", "ones_row"), writes=("psb5",))
                S.emit("scalar", lambda h: h.copy(out=rc[:, 0:1], in_=ps[:, 5, 0:1]),
                       reads=("psb5",), writes=(tag + "rc",))
                S.emit("tensor", lambda h, lsl=lsl: h.matmul(
                    ps[:, 6, 0:16], keep[0:1, 1, lsl], Gs[:],
                    start=True, stop=True),
                    reads=(tag + "keep", tag + "Gs", "ident_f"), writes=("psb6",))
                S.emit("vector", lambda h: h.scalar_tensor_tensor(
                    out=rt[:, 1, :], in0=rt[:, 0, :], scalar=rc[:, 0:1],
                    in1=ps[:, 6, 0:16], op0=ALU.mult, op1=ALU.add),
                    reads=(tag + "rt", tag + "rc", "psb6"), writes=(tag + "rt",))
                S.emit("scalar", lambda h: h.activation(
                    out=rt[:, 2, :], in_=rt[:, 1, :], func=AF.Sigmoid),
                    reads=(tag + "rt",), writes=(tag + "rt",))
                S.emit("vector", lambda h: h.tensor_add(
                    out=rt[:, 3, :], in0=rt[:, 2, :], in1=gbb[:]),
                    reads=(tag + "rt", tag + "gbb"), writes=(tag + "rt",))
                for k4 in range(4):
                    S.emit("vector", lambda h: h.reduce_max(
                        out=rc[:, 1:2], in_=rt[:, 3, :], axis=AX.X),
                        reads=(tag + "rt",), writes=(tag + "rc",))
                    S.emit("vector", lambda h: h.tensor_scalar(
                        out=rt[:, 5, :], in0=rt[:, 3, :], scalar1=rc[:, 1:2],
                        scalar2=None, op0=ALU.is_equal),
                        reads=(tag + "rt", tag + "rc"), writes=(tag + "rt",))
                    if k4 == 0:
                        S.emit("vector", lambda h: h.tensor_copy(
                            out=rt[:, 4, :], in_=rt[:, 5, :]),
                            reads=(tag + "rt",), writes=(tag + "rt",))
                    else:
                        S.emit("vector", lambda h: h.tensor_add(
                            out=rt[:, 4, :], in0=rt[:, 4, :], in1=rt[:, 5, :]),
                            reads=(tag + "rt",), writes=(tag + "rt",))
                    S.emit("vector", lambda h: h.scalar_tensor_tensor(
                        out=rt[:, 3, :], in0=rt[:, 5, :], scalar=-1e9,
                        in1=rt[:, 3, :], op0=ALU.mult, op1=ALU.add),
                        reads=(tag + "rt",), writes=(tag + "rt",))
                S.emit("vector", lambda h: h.tensor_mul(
                    out=rt[:, 6, :], in0=rt[:, 4, :], in1=rt[:, 2, :]),
                    reads=(tag + "rt",), writes=(tag + "rt",))
                S.emit("vector", lambda h: h.reduce_sum(
                    out=rc[:, 2:3], in_=rt[:, 6, :], axis=AX.X),
                    reads=(tag + "rt",), writes=(tag + "rc",))
                S.emit("vector", lambda h: h.reciprocal(
                    out=rc[:, 3:4], in_=rc[:, 2:3]),
                    reads=(tag + "rc",), writes=(tag + "rc",))
                S.emit("vector", lambda h: h.tensor_scalar(
                    out=rt[:, 7, :], in0=rt[:, 6, :], scalar1=rc[:, 3:4],
                    scalar2=2.5, op0=ALU.mult, op1=ALU.mult),
                    reads=(tag + "rt", tag + "rc"), writes=(tag + "rt",))
                S.emit("tensor", lambda h: h.transpose(
                    ps[0:16, 7, 0:128], rt[:, 7, :], C.ident_f[:]),
                    reads=(tag + "rt", "ident_f"), writes=("psb7",))
                S.emit("scalar", lambda h, lsl=lsl: h.copy(
                    out=comb[:, lsl], in_=ps[0:16, 7, 0:128]),
                    reads=("psb7",), writes=(tag + "comb",))
            for ei in range(17):
                shared = ei == 0
                e = ei - 1
                wsl = ei % 2
                if shared:
                    w13src = W["w13sT"]
                    w2src = W["w2sT"]
                else:
                    w13src = W["w13T"][e]
                    w2src = W["w2T"][e]
                S.emit("gpsimd", lambda h, w13src=w13src, wsl=wsl: h.dma_start(
                    out=w13b[:, wsl, :, :],
                    in_=w13src.rearrange("(kt p) f -> p kt f", p=128)),
                    writes=(f"{tag}w13{wsl}",), dma=True)
                S.emit("gpsimd", lambda h, w2src=w2src, wsl=wsl: h.dma_start(
                    out=w2b[:, wsl, :, :],
                    in_=w2src.rearrange("(kt p) f -> p kt f", p=128)),
                    writes=(f"{tag}w2{wsl}",), dma=True)
                for sub in range(2):
                    ts0 = sub * 512
                    if not shared:
                        S.emit("tensor", lambda h, e=e, ts0=ts0: h.matmul(
                            ps[:, 6, :], C.esel[:, e, :], comb[:, ts0:ts0 + 512],
                            start=True, stop=True),
                            reads=(tag + "comb", "esel"), writes=("psb6",))
                    for ft in range(4):
                        gb = (4, 0)[ft % 2]
                        ub_ = (5, 1)[ft % 2]

                        def _gmm(h, ft=ft, wsl=wsl, ts0=ts0, gb=gb):
                            for kk in range(8):
                                last = h.matmul(
                                    ps[:, gb, :], w13b[:, wsl, kk, ts(ft)],
                                    xn[:, kk, ts0:ts0 + 512],
                                    start=(kk == 0), stop=(kk == 7))
                            return last
                        S.emit("tensor", _gmm,
                               reads=(f"{tag}w13{wsl}", tag + "xn"), writes=(f"psb{gb}",))

                        def _umm(h, ft=ft, wsl=wsl, ts0=ts0, ub_=ub_):
                            for kk in range(8):
                                last = h.matmul(
                                    ps[:, ub_, :],
                                    w13b[:, wsl, kk, 512 + ft * 128:512 + (ft + 1) * 128],
                                    xn[:, kk, ts0:ts0 + 512],
                                    start=(kk == 0), stop=(kk == 7))
                            return last
                        S.emit("tensor", _umm,
                               reads=(f"{tag}w13{wsl}", tag + "xn"), writes=(f"psb{ub_}",))
                        S.emit("scalar", lambda h, gb=gb: h.activation(
                            out=gst[:], in_=ps[:, gb, :], func=AF.Silu),
                            reads=(f"psb{gb}",), writes=(tag + "gst",))
                        S.emit("vector", lambda h, ft=ft, ub_=ub_: h.tensor_tensor(
                            out=hbf[:, ft, :], in0=gst[:], in1=ps[:, ub_, :],
                            op=ALU.mult),
                            reads=(tag + "gst", f"psb{ub_}"), writes=(tag + "hbf",))
                        if not shared:
                            S.emit("vector", lambda h, ft=ft: h.tensor_tensor(
                                out=hbf[:, ft, :], in0=hbf[:, ft, :], in1=ps[:, 6, :],
                                op=ALU.mult),
                                reads=(tag + "hbf", "psb6"), writes=(tag + "hbf",))
                    for i in range(8):
                        mb2 = (7, 3)[i % 2]

                        def _mm2(h, i=i, wsl=wsl, mb2=mb2):
                            for kk in range(4):
                                last = h.matmul(
                                    ps[:, mb2, :], w2b[:, wsl, kk, ts(i)], hbf[:, kk, :],
                                    start=(kk == 0), stop=(kk == 3))
                            return last
                        S.emit("tensor", _mm2,
                               reads=(f"{tag}w2{wsl}", tag + "hbf"), writes=(f"psb{mb2}",))
                        if shared:
                            S.emit("vector", lambda h, i=i, ts0=ts0, mb2=mb2: h.tensor_copy(
                                out=yacc[:, i, ts0:ts0 + 512], in_=ps[:, mb2, :]),
                                reads=(f"psb{mb2}",), writes=(tag + "yacc",))
                        else:
                            S.emit("vector", lambda h, i=i, ts0=ts0, mb2=mb2: h.tensor_add(
                                out=yacc[:, i, ts0:ts0 + 512],
                                in0=yacc[:, i, ts0:ts0 + 512], in1=ps[:, mb2, :]),
                                reads=(tag + "yacc", f"psb{mb2}"), writes=(tag + "yacc",))
            for i in range(8):
                S.emit("vector", lambda h, i=i, t0mc=t0mc: h.tensor_add(
                    out=C.x_res[:, i, t0mc:t0mc + 1024],
                    in0=C.x_res[:, i, t0mc:t0mc + 1024], in1=yacc[:, i, :]),
                    reads=(tag + "yacc", "x_res"), writes=("x_res",))
        S.barrier()
        S.flush()



def emit_l1ffn(C, tag):
    import contextlib
    nc, S = C.nc, C.S
    W = C.dram_w["l1f"]
    with contextlib.ExitStack() as es:
        alloc_esel(C, es, tag)
        ps = es.enter_context(nc.psum_tensor(f"{tag}_ps", [128, 8, 512], F32))
        x2 = es.enter_context(nc.sbuf_tensor(f"{tag}_x2", [128, 8, 128], F32))
        rows = es.enter_context(nc.sbuf_tensor(f"{tag}_rows", [1, 6, 128], F32))
        gw = es.enter_context(nc.sbuf_tensor(f"{tag}_gw", [128, 8, 8], F32))
        gbr = es.enter_context(nc.sbuf_tensor(f"{tag}_gbr", [1, 8], F32))
        Gs = es.enter_context(nc.sbuf_tensor(f"{tag}_Gs", [1, 8], F32))
        b1s = es.enter_context(nc.sbuf_tensor(f"{tag}_b1", [128, 32, 8], F32))
        b2s = es.enter_context(nc.sbuf_tensor(f"{tag}_b2", [8, 1024], F32))
        xn = es.enter_context(nc.sbuf_tensor(f"{tag}_xn", [128, 8, 1024], BF16))
        keep = es.enter_context(nc.sbuf_tensor(f"{tag}_keep", [1, 2, 1024], F32))
        cfm = es.enter_context(nc.sbuf_tensor(f"{tag}_cfm", [8, 1024], F32))
        yacc = es.enter_context(nc.sbuf_tensor(f"{tag}_yacc", [128, 8, 1024], F32))
        wb = es.enter_context(nc.sbuf_tensor(f"{tag}_wb", [128, 8, 1024], BF16))
        wb2 = es.enter_context(nc.sbuf_tensor(f"{tag}_wb2", [128, 8, 1024], BF16))
        hblk = es.enter_context(nc.sbuf_tensor(f"{tag}_hblk", [128, 8, 1024], BF16))
        bcs = es.enter_context(nc.sbuf_tensor(f"{tag}_bcs", [128, 1024], BF16))
        rt = es.enter_context(nc.sbuf_tensor(f"{tag}_rt", [128, 8, 8], F32))
        rc = es.enter_context(nc.sbuf_tensor(f"{tag}_rc", [128, 4], F32))
        S.emit("sync", lambda h: h.dma_start(
            out=gw[:], in_=W["w_gate"].rearrange("(kt p) e -> p kt e", p=128),
            allow_slow_non_contiguous=True),
            writes=(tag + "gw",), dma=True)
        S.emit("sync", lambda h: h.dma_start(out=gbr[:], in_=W["b_gate"]),
               writes=(tag + "gbr",), dma=True)
        S.emit("sync", lambda h: h.dma_start(
            out=b1s[:], in_=W["b1p"]), writes=(tag + "b1",), dma=True)
        S.emit("sync", lambda h: h.dma_start(out=b2s[:], in_=W["b2"]),
               writes=(tag + "b2",), dma=True)
        def _colsum(h):
            for kk in range(8):
                last = h.matmul(ps[0:1, 1, 0:8], C.ones_col[:], gw[:, kk, :],
                                start=(kk == 0), stop=(kk == 7))
            return last
        S.emit("tensor", _colsum, reads=(tag + "gw", "ones_col"), writes=("psb1",))
        S.emit("scalar", lambda h: h.copy(out=Gs[:], in_=ps[0:1, 1, 0:8]),
               reads=("psb1",), writes=(tag + "Gs",))
        for mc in range(2):
            t0mc = mc * 1024
            for q in range(8):
                emit_ln(C, xn, tag + "xn", t0mc + q * 128, 128,
                        f"{tag}{mc}{q}", ps, x2, rows,
                        keep_ab=keep, keep_off=q * 128, keep_key=tag + "keep",
                        out_t0=q * 128)
            for tt in range(8):
                gsl = slice(t0mc + tt * 128, t0mc + (tt + 1) * 128)
                lsl = slice(tt * 128, (tt + 1) * 128)
                def _s1(h, gsl=gsl):
                    for kk in range(8):
                        last = h.matmul(ps[:, 4, 0:8], C.x_res[:, kk, gsl],
                                        gw[:, kk, :],
                                        start=(kk == 0), stop=(kk == 7))
                    return last
                S.emit("tensor", _s1, reads=("x_res", tag + "gw"), writes=("psb4",))
                S.emit("scalar", lambda h: h.copy(out=rt[:, 0, :], in_=ps[:, 4, 0:8]),
                       reads=("psb4",), writes=(tag + "rt",))
                S.emit("tensor", lambda h, lsl=lsl: h.matmul(
                    ps[:, 5, 0:1], keep[0:1, 0, lsl], C.ones_row[0:1, 0:1],
                    start=True, stop=True),
                    reads=(tag + "keep", "ones_row"), writes=("psb5",))
                S.emit("scalar", lambda h: h.copy(out=rc[:, 0:1], in_=ps[:, 5, 0:1]),
                       reads=("psb5",), writes=(tag + "rc",))
                S.emit("tensor", lambda h, lsl=lsl: h.matmul(
                    ps[:, 6, 0:8], keep[0:1, 1, lsl], Gs[:],
                    start=True, stop=False),
                    reads=(tag + "keep", tag + "Gs"), writes=("psb6",))
                S.emit("tensor", lambda h: h.matmul(
                    ps[:, 6, 0:8], C.ones_row[:], gbr[:],
                    start=False, stop=True),
                    reads=(tag + "gbr", "ones_row"), writes=("psb6",))
                S.emit("vector", lambda h: h.scalar_tensor_tensor(
                    out=rt[:, 1, :], in0=rt[:, 0, :], scalar=rc[:, 0:1],
                    in1=ps[:, 6, 0:8], op0=ALU.mult, op1=ALU.add),
                    reads=(tag + "rt", tag + "rc", "psb6"), writes=(tag + "rt",))
                S.emit("vector", lambda h: h.reduce_max(
                    out=rc[:, 1:2], in_=rt[:, 1, :], axis=AX.X, negate=True),
                    reads=(tag + "rt",), writes=(tag + "rc",))
                S.emit("scalar", lambda h: h.activation(
                    out=rt[:, 2, :], in_=rt[:, 1, :], func=AF.Exp,
                    bias=rc[:, 1:2]),
                    reads=(tag + "rt", tag + "rc"), writes=(tag + "rt",))
                S.emit("vector", lambda h: h.reduce_sum(
                    out=rc[:, 2:3], in_=rt[:, 2, :], axis=AX.X),
                    reads=(tag + "rt",), writes=(tag + "rc",))
                S.emit("vector", lambda h: h.reciprocal(
                    out=rc[:, 3:4], in_=rc[:, 2:3]),
                    reads=(tag + "rc",), writes=(tag + "rc",))
                S.emit("vector", lambda h: h.tensor_scalar_mul(
                    out=rt[:, 3, :], in0=rt[:, 2, :], scalar1=rc[:, 3:4]),
                    reads=(tag + "rt", tag + "rc"), writes=(tag + "rt",))
                S.emit("vector", lambda h: h.reduce_max(
                    out=rc[:, 1:2], in_=rt[:, 3, :], axis=AX.X),
                    reads=(tag + "rt", tag + "rc"), writes=(tag + "rc",))
                S.emit("vector", lambda h: h.tensor_scalar(
                    out=rt[:, 4, :], in0=rt[:, 3, :], scalar1=rc[:, 1:2],
                    scalar2=None, op0=ALU.is_equal),
                    reads=(tag + "rt", tag + "rc"), writes=(tag + "rt",))
                S.emit("vector", lambda h: h.tensor_scalar_mul(
                    out=rt[:, 5, :], in0=rt[:, 4, :], scalar1=rc[:, 1:2]),
                    reads=(tag + "rt", tag + "rc"), writes=(tag + "rt",))
                S.emit("tensor", lambda h: h.transpose(
                    ps[0:8, 7, 0:128], rt[:, 5, :], C.ident_f[:]),
                    reads=(tag + "rt", "ident_f"), writes=("psb7",))
                S.emit("scalar", lambda h, lsl=lsl: h.copy(
                    out=cfm[:, lsl], in_=ps[0:8, 7, 0:128]),
                    reads=("psb7",), writes=(tag + "cfm",))
            for i in range(8):
                for half in range(2):
                    hsl = slice(half * 512, (half + 1) * 512)
                    S.emit("tensor", lambda h, i=i, hsl=hsl: h.matmul(
                        ps[:, 2, :], b2s[:, ts(i)], cfm[:, hsl],
                        start=True, stop=True),
                        reads=(tag + "b2", tag + "cfm"), writes=("psb2",))
                    S.emit("vector", lambda h, i=i, hsl=hsl: h.tensor_copy(
                        out=yacc[:, i, hsl], in_=ps[:, 2, :]),
                        reads=("psb2",), writes=(tag + "yacc",))
            for e in range(8):
                for half in range(2):
                    hsl = slice(half * 512, (half + 1) * 512)
                    S.emit("tensor", lambda h, e=e, hsl=hsl, half=half: h.matmul(
                        ps[:, 2 + half, :], C.esel[0:8, e, :], cfm[:, hsl],
                        start=True, stop=True),
                        reads=(tag + "cfm", "esel"), writes=(f"psb{2 + half}",))
                    S.emit("scalar", lambda h, hsl=hsl, half=half: h.copy(
                        out=bcs[:, hsl], in_=ps[:, 2 + half, :]),
                        reads=(f"psb{2 + half}",), writes=(tag + "bcs",))
                for mb in range(4):
                    S.emit("gpsimd", lambda h, e=e, mb=mb: h.dma_start(
                        out=wb[:], in_=W["W1"][e].rearrange(
                            "(kt p) m -> p kt m", p=128)[:, :, ts(mb, 1024)]),
                        writes=(tag + "wb",), dma=True)
                    S.emit("gpsimd", lambda h, e=e, mb=mb: h.dma_start(
                        out=wb2[:], in_=W["W2"][e].rearrange(
                            "(kt p) d -> p kt d", p=128, kt=32)[:, mb * 8:(mb + 1) * 8, :]),
                        writes=(tag + "wb2",), dma=True)
                    for mt in range(8):
                        for half in range(2):
                            hsl = slice(half * 512, (half + 1) * 512)
                            pb = (2 * mt + half) % 3

                            def _mm1(h, mt=mt, hsl=hsl, pb=pb):
                                for kk in range(8):
                                    last = h.matmul(
                                        ps[:, pb, :], wb[:, kk, ts(mt)],
                                        xn[:, kk, hsl],
                                        start=(kk == 0), stop=(kk == 7))
                                return last
                            S.emit("tensor", _mm1,
                                   reads=(tag + "wb", tag + "xn"),
                                   writes=(f"psb{pb}",))
                            S.emit("scalar", lambda h, mt=mt, mb=mb, e=e, hsl=hsl, pb=pb: h.activation(
                                out=hblk[:, mt, hsl], in_=ps[:, pb, :], func=AF.Relu,
                                bias=b1s[:, mb * 8 + mt, e:e + 1]),
                                reads=(f"psb{pb}", tag + "b1"), writes=(tag + "hblk",))
                        S.emit("vector", lambda h, mt=mt: h.tensor_tensor(
                            out=hblk[:, mt, :], in0=hblk[:, mt, :], in1=bcs[:],
                            op=ALU.mult),
                            reads=(tag + "hblk", tag + "bcs"), writes=(tag + "hblk",))
                    for i in range(8):
                        for half in range(2):
                            hsl = slice(half * 512, (half + 1) * 512)
                            pb = 4 + (2 * i + half) % 3

                            def _mm2(h, i=i, hsl=hsl, pb=pb):
                                for kk in range(8):
                                    last = h.matmul(
                                        ps[:, pb, :], wb2[:, kk, ts(i)],
                                        hblk[:, kk, hsl],
                                        start=(kk == 0), stop=(kk == 7))
                                return last
                            S.emit("tensor", _mm2,
                                   reads=(tag + "wb2", tag + "hblk"),
                                   writes=(f"psb{pb}",))
                            S.emit("vector", lambda h, i=i, hsl=hsl, pb=pb: h.tensor_add(
                                out=yacc[:, i, hsl], in0=yacc[:, i, hsl],
                                in1=ps[:, pb, :]),
                                reads=(tag + "yacc", f"psb{pb}"), writes=(tag + "yacc",))
            for i in range(8):
                S.emit("vector", lambda h, i=i, t0mc=t0mc: h.tensor_add(
                    out=C.x_res[:, i, t0mc:t0mc + 1024],
                    in0=C.x_res[:, i, t0mc:t0mc + 1024], in1=yacc[:, i, :]),
                    reads=(tag + "yacc", "x_res"), writes=("x_res",))
        S.barrier()
        S.flush()



def emit_l1a(C, tag="l1a"):
    import contextlib
    nc, S = C.nc, C.S
    W = C.dram_w["l1a"]
    p_dram = C.dram["l1a_p"]
    r_dram = C.dram["l1a_r"]
    NIT = 41
    with contextlib.ExitStack() as es:
        ps = es.enter_context(nc.psum_tensor(f"{tag}_psA", [128, 8, 512], F32))
        x2 = es.enter_context(nc.sbuf_tensor(f"{tag}_x2", [128, 8, 512], F32))
        rows = es.enter_context(nc.sbuf_tensor(f"{tag}_rows", [1, 6, 512], F32))
        xnorm = es.enter_context(nc.sbuf_tensor(f"{tag}_xn", [128, 8, T_FULL], BF16))
        wblk = es.enter_context(nc.sbuf_tensor(f"{tag}_wblk", [128, 2, 8, 512], BF16))
        evb = es.enter_context(nc.sbuf_tensor(f"{tag}_evb", [128, 2, 4, 512], BF16))
        for ch in range(4):
            emit_ln(C, xnorm, "xnorm", ch * 512, 512, tag + str(ch), ps, x2, rows)
        for proj, src_w, n_i, dst in (("p", W["w_inT"], NIT, p_dram),
                                      ("r", W["w_resT"], 16, r_dram)):
            nblk = (n_i + 3) // 4
            for ib in range(nblk):
                i0 = ib * 4
                ni = min(4, n_i - i0)
                wsl = ib % 2
                S.emit("gpsimd", lambda h, src_w=src_w, i0=i0, ni=ni, wsl=wsl: h.dma_start(
                    out=wblk[:, wsl, :, 0:ni * 128],
                    in_=src_w.rearrange("(kt p) i -> p kt i", p=128)[:, :, i0 * 128:(i0 + ni) * 128]),
                    writes=(f"{tag}_wblk{wsl}",), dma=True)
                for ch in range(4):
                    esl = (ib * 4 + ch) % 2
                    for i in range(ni):
                        pb = (ib * 4 + i) % 8

                        def _inp(h, ch=ch, i=i, wsl=wsl, pb=pb):
                            for kk in range(8):
                                last = h.matmul(
                                    ps[:, pb, :], wblk[:, wsl, kk, ts(i)],
                                    xnorm[:, kk, ch * 512:(ch + 1) * 512],
                                    start=(kk == 0), stop=(kk == 7))
                            return last
                        S.emit("tensor", _inp,
                               reads=(f"{tag}_wblk{wsl}", "xnorm"),
                               writes=(f"psb{pb}",))
                        S.emit("scalar", lambda h, i=i, esl=esl, pb=pb: h.copy(
                            out=evb[:, esl, i, :], in_=ps[:, pb, :]),
                            reads=(f"psb{pb}",), writes=(f"{tag}_evb{esl}",))
                    S.emit("sync", lambda h, dst=dst, i0=i0, ni=ni, ch=ch, esl=esl: h.dma_start(
                        out=dst[i0 * 128:(i0 + ni) * 128, ch * 512:(ch + 1) * 512].rearrange(
                            "(i p) t -> p i t", p=128),
                        in_=evb[:, esl, 0:ni, :]),
                        reads=(f"{tag}_evb{esl}",), writes=(f"{tag}_{proj}d",), dma=True)
        S.barrier()
        S.flush()
    S.barrier()
    tc = 256
    with contextlib.ExitStack() as es:
        alloc_esel(C, es, tag)
        ps = es.enter_context(nc.psum_tensor(f"{tag}_psB", [128, 8, 512], F32))
        wout = es.enter_context(nc.sbuf_tensor(f"{tag}_wout", [128, 16, 512], BF16))
        wdt = es.enter_context(nc.sbuf_tensor(f"{tag}_wdt", [64, 8], BF16))
        convw = es.enter_context(nc.sbuf_tensor(f"{tag}_cw", [128, 16, 4], F32))
        convb = es.enter_context(nc.sbuf_tensor(f"{tag}_cb", [128, 16], F32))
        dtb = es.enter_context(nc.sbuf_tensor(f"{tag}_dtb", [8, 1], F32))
        Afl = es.enter_context(nc.sbuf_tensor(f"{tag}_Afl", [1, 128], F32))
        Abc = es.enter_context(nc.sbuf_tensor(f"{tag}_Abc", [128, 128], F32))
        Dsk = es.enter_context(nc.sbuf_tensor(f"{tag}_Dsk", [128, 16], F32))
        nrmw = es.enter_context(nc.sbuf_tensor(f"{tag}_nw", [128, 16], F32))
        carry = es.enter_context(nc.sbuf_tensor(f"{tag}_carry", [128, 256], F32))
        xh = es.enter_context(nc.sbuf_tensor(f"{tag}_xh", [128, 16, tc + 4], BF16))
        ub = es.enter_context(nc.sbuf_tensor(f"{tag}_u", [128, 16, tc], F32))
        dts = es.enter_context(nc.sbuf_tensor(f"{tag}_dts", [8, tc], F32))
        dtr = es.enter_context(nc.sbuf_tensor(f"{tag}_dtr", [64, tc], BF16))
        Bf = es.enter_context(nc.sbuf_tensor(f"{tag}_Bf", [16, 8, tc], BF16))
        eselb = es.enter_context(nc.sbuf_tensor(f"{tag}_eselb", [16, 16, 128], BF16))
        Cf = es.enter_context(nc.sbuf_tensor(f"{tag}_Cf", [16, 8, tc], BF16))
        du = es.enter_context(nc.sbuf_tensor(f"{tag}_du", [128, 16, tc], F32))
        ybuf = es.enter_context(nc.sbuf_tensor(f"{tag}_y", [128, 16, tc], F32))
        zb = es.enter_context(nc.sbuf_tensor(f"{tag}_z", [128, 16, tc], BF16))
        rbuf = es.enter_context(nc.sbuf_tensor(f"{tag}_rin", [128, 16, tc], BF16))
        gnb = es.enter_context(nc.sbuf_tensor(f"{tag}_gn", [128, 16, tc], BF16))
        dAs = es.enter_context(nc.sbuf_tensor(f"{tag}_dAs", [128, 2, tc], F32))
        Xs = es.enter_context(nc.sbuf_tensor(f"{tag}_Xs", [128, 2, tc], BF16))
        hs = es.enter_context(nc.sbuf_tensor(f"{tag}_hs", [128, 2, 2, tc], F32))
        dubuf = es.enter_context(nc.sbuf_tensor(f"{tag}_dub", [128, 16, tc], BF16))
        bcBs = es.enter_context(nc.sbuf_tensor(f"{tag}_bcBs", [128, 2, tc], BF16))
        rrow = es.enter_context(nc.sbuf_tensor(f"{tag}_rrow", [1, 2, tc], F32))
        mk = tag + "m"
        for name, dst, key in (("conv_w", convw, "cw"), ("conv_b", convb, "cb"),
                               ("dt_bias", dtb, "dtb"), ("A_flat", Afl, "Afl"),
                               ("Dskip", Dsk, "D"), ("norm_w", nrmw, "nw")):
            S.emit("sync", lambda h, name=name, dst=dst: h.dma_start(
                out=dst[:], in_=W[name]), writes=(mk + key,), dma=True)
        S.emit("gpsimd", lambda h: h.dma_start(out=wdt[:], in_=W["w_dtT"]),
               writes=(mk + "wdt",), dma=True)
        S.emit("scalar", lambda h: h.activation(out=Afl[:], in_=Afl[:], func=AF.Exp),
               reads=(mk + "Afl",), writes=(mk + "Afl",))
        S.emit("vector", lambda h: h.tensor_scalar_mul(
            out=Afl[:], in0=Afl[:], scalar1=-1.0),
            reads=(mk + "Afl",), writes=(mk + "Afl",))
        S.emit("tensor", lambda h: h.matmul(
            ps[:, 0, 0:128], C.ones_row[:], Afl[:], start=True, stop=True),
            reads=(mk + "Afl", "ones_row"), writes=("psb0",))
        S.emit("scalar", lambda h: h.copy(out=Abc[:], in_=ps[:, 0, 0:128]),
               reads=("psb0",), writes=(mk + "Abc",))
        S.emit("vector", lambda h: h.memset(carry[:], 0.0), writes=(mk + "carry",))
        S.emit("scalar", lambda h: h.copy(out=eselb[:], in_=C.esel[:]),
               reads=("esel",), writes=(mk + "eselb",))
        pr = p_dram.rearrange("(j p) t -> p j t", p=128)
        rr = r_dram.rearrange("(j p) t -> p j t", p=128)
        for ch in range(8):
            t0 = ch * tc
            if ch == 0:
                S.emit("vector", lambda h: h.memset(xh[:, :, 0:3], 0.0),
                       writes=(mk + "xh",))
                S.emit("sync", lambda h: h.dma_start(
                    out=xh[:, :, 3:3 + tc], in_=pr[:, 16:32, 0:tc]),
                    reads=(f"{tag}_pd",), writes=(mk + "xh",), dma=True)
            else:
                S.emit("sync", lambda h, t0=t0: h.dma_start(
                    out=xh[:, :, 0:3 + tc], in_=pr[:, 16:32, t0 - 3:t0 + tc]),
                    reads=(f"{tag}_pd",), writes=(mk + "xh",), dma=True)
            S.emit("sync", lambda h, t0=t0: h.dma_start(
                out=dtr[:], in_=pr[0:64, 32, t0:t0 + tc]),
                reads=(f"{tag}_pd",), writes=(mk + "dtr",), dma=True)
            S.emit("sync", lambda h, t0=t0: h.dma_start(
                out=gnb[:, 0:4, :], in_=pr[:, 33:37, t0:t0 + tc]),
                reads=(f"{tag}_pd",), writes=(mk + "gn",), dma=True)
            S.emit("sync", lambda h, t0=t0: h.dma_start(
                out=gnb[:, 4:8, :], in_=pr[:, 37:41, t0:t0 + tc]),
                reads=(f"{tag}_pd",), writes=(mk + "gn",), dma=True)
            for j in range(16):
                S.emit("vector", lambda h, j=j: h.tensor_scalar(
                    out=ub[:, j, :], in0=xh[:, j, 0:tc], scalar1=convw[:, j, 0:1],
                    scalar2=convb[:, j:j + 1], op0=ALU.mult, op1=ALU.add),
                    reads=(mk + "xh", mk + "cw", mk + "cb"), writes=(mk + "u",))
                for k4 in (1, 2, 3):
                    S.emit("vector", lambda h, j=j, k4=k4: h.scalar_tensor_tensor(
                        out=ub[:, j, :], in0=xh[:, j, k4:k4 + tc],
                        scalar=convw[:, j, k4:k4 + 1], in1=ub[:, j, :],
                        op0=ALU.mult, op1=ALU.add),
                        reads=(mk + "xh", mk + "cw", mk + "u"), writes=(mk + "u",))
            for g in range(8):
                S.emit("scalar", lambda h, g=g: h.copy(
                    out=Bf[:, g, :], in_=gnb[64 * (g % 2):64 * (g % 2) + 16, g // 2, :]),
                    reads=(mk + "gn",), writes=(mk + "Bf",))
                S.emit("scalar", lambda h, g=g: h.copy(
                    out=Cf[:, g, :], in_=gnb[64 * (g % 2):64 * (g % 2) + 16, 4 + g // 2, :]),
                    reads=(mk + "gn",), writes=(mk + "Cf",))
            S.emit("tensor", lambda h: h.matmul(
                ps[0:8, 6, :tc], wdt[:, :], dtr[:],
                start=True, stop=True),
                reads=(mk + "dtr", mk + "wdt"), writes=("psb6",))
            S.emit("scalar", lambda h: h.activation(
                out=dts[:, :],
                in_=ps[0:8, 6, :tc], func=AF.Identity, bias=dtb[:, 0:1]),
                reads=("psb6", mk + "dtb"), writes=(mk + "dts",))
            S.emit("vector", lambda h: h.tensor_scalar_max(
                out=du[0:8, 0, :], in0=dts[:, :], scalar1=0.0),
                reads=(mk + "dts",), writes=(mk + "duscr",))
            S.emit("scalar", lambda h: h.activation(
                out=dts[:, :], in_=dts[:, :], func=AF.Abs),
                reads=(mk + "dts",), writes=(mk + "dts",))
            S.emit("scalar", lambda h: h.activation(
                out=dts[:, :], in_=dts[:, :], func=AF.Exp, scale=-1.0),
                reads=(mk + "dts",), writes=(mk + "dts",))
            S.emit("scalar", lambda h: h.activation(
                out=dts[:, :], in_=dts[:, :], func=AF.Ln, bias=1.0),
                reads=(mk + "dts",), writes=(mk + "dts",))
            S.emit("vector", lambda h: h.tensor_add(
                out=dts[:, :], in0=dts[:, :], in1=du[0:8, 0, :]),
                reads=(mk + "dts", mk + "duscr"), writes=(mk + "dts",))
            S.emit("vector", lambda h: h.tensor_scalar(
                out=dts[:, :], in0=dts[:, :], scalar1=1e-4, scalar2=1.0,
                op0=ALU.max, op1=ALU.min),
                reads=(mk + "dts",), writes=(mk + "dts",))
            for j in range(16):
                S.emit("vector", lambda h, j=j: h.tensor_scalar_mul(
                    out=ybuf[:, j, :], in0=ub[:, j, :], scalar1=Dsk[:, j:j + 1]),
                    reads=(mk + "u", mk + "D"), writes=(mk + "y",))
            for hh in range(8):
                hb = hh % 2
                S.emit("tensor", lambda h, hh=hh, hb=hb: h.matmul(
                    ps[:, hb, :tc], C.esel[0:8, hh, :], dts[:, :],
                    start=True, stop=True),
                    reads=(mk + "dts", "esel"), writes=(f"psb{hb}",))
                for jo in range(2):
                    j = 2 * hh + jo
                    S.emit("vector", lambda h, j=j, hb=hb: h.tensor_tensor(
                        out=dubuf[:, j, :], in0=ub[:, j, :], in1=ps[:, hb, :tc],
                        op=ALU.mult),
                        reads=(mk + "u", f"psb{hb}"), writes=(mk + "dub",))
                for n in range(16):
                    bb = 2 + n % 2
                    cb2 = 4 + n % 2
                    par = n % 2
                    S.emit("tensor", lambda h, hh=hh, n=n, bb=bb: h.matmul(
                        ps[:, bb, :tc], eselb[:, n, :],
                        Bf[:, hh, :],
                        start=True, stop=True),
                        reads=(mk + "Bf", mk + "eselb"), writes=(f"psb{bb}",))
                    S.emit("tensor", lambda h, hh=hh, n=n, cb2=cb2: h.matmul(
                        ps[:, cb2, :tc], eselb[:, n, :],
                        Cf[:, hh, :],
                        start=True, stop=True),
                        reads=(mk + "Cf", mk + "eselb"), writes=(f"psb{cb2}",))
                    S.emit("scalar", lambda h, bb=bb, par=par: h.copy(
                        out=bcBs[:, par, :], in_=ps[:, bb, :tc]),
                        reads=(f"psb{bb}",), writes=(f"{mk}bcBs{par}",))
                    dsl = n % 2
                    S.emit("scalar", lambda h, hh=hh, n=n, dsl=dsl, hb=hb: h.activation(
                        out=dAs[:, dsl, :], in_=ps[:, hb, :tc], func=AF.Exp,
                        scale=Abc[:, hh * 16 + n:hh * 16 + n + 1]),
                        reads=(f"psb{hb}", mk + "Abc"), writes=(f"{mk}dA{dsl}",))
                    for jo in range(2):
                        j = 2 * hh + jo
                        S.emit("vector", lambda h, j=j, jo=jo, par=par, dsl=dsl: h.tensor_tensor(
                            out=Xs[:, jo, :], in0=dubuf[:, j, :], in1=bcBs[:, par, :],
                            op=ALU.mult),
                            reads=(mk + "dub", f"{mk}bcBs{par}"), writes=(f"{mk}X{jo}",))
                        S.emit("vector", lambda h, j=j, n=n, jo=jo, par=par, dsl=dsl: h.tensor_tensor_scan(
                            out=hs[:, par, jo, :], data0=dAs[:, dsl, :], data1=Xs[:, jo, :],
                            initial=carry[:, j * 16 + n:j * 16 + n + 1],
                            op0=ALU.mult, op1=ALU.add),
                            reads=(f"{mk}dA{dsl}", f"{mk}X{jo}", mk + "carry",
                                   f"{mk}hsG{par}"),
                            writes=(f"{mk}h{par}{jo}",))
                    S.emit("vector", lambda h, hh=hh, n=n, par=par: h.tensor_copy(
                        out=carry.ap()[:, 2 * hh * 16 + n:(2 * hh + 2) * 16:16],
                        in_=hs.ap()[:, par, :, tc - 1]),
                        reads=(f"{mk}h{par}0", f"{mk}h{par}1"),
                        writes=(mk + "carry",))
                    S.emit("vector", lambda h, par=par, cb2=cb2: h.tensor_tensor(
                        out=hs[:, par, :, :], in0=hs[:, par, :, :],
                        in1=ps.ap()[:, cb2:cb2 + 1, :tc].broadcast_to([128, 2, tc]),
                        op=ALU.mult),
                        reads=(f"{mk}h{par}0", f"{mk}h{par}1", f"psb{cb2}"),
                        writes=(f"{mk}h{par}0", f"{mk}h{par}1")),
                    S.emit("gpsimd", lambda h, hh=hh, par=par: h.tensor_add(
                        out=ybuf[:, 2 * hh:2 * hh + 2, :], in0=ybuf[:, 2 * hh:2 * hh + 2, :],
                        in1=hs[:, par, :, :]),
                        reads=(f"{mk}h{par}0", f"{mk}h{par}1", mk + "y"),
                        writes=(mk + "y", f"{mk}hsG{par}"))
            S.emit("sync", lambda h, t0=t0: h.dma_start(
                out=zb[:], in_=pr[:, 0:16, t0:t0 + tc]),
                reads=(f"{tag}_pd",), writes=(mk + "z",), dma=True)
            S.emit("sync", lambda h, t0=t0: h.dma_start(
                out=rbuf[:], in_=rr[:, 0:16, t0:t0 + tc]),
                reads=(f"{tag}_rd",), writes=(mk + "rb",), dma=True)
            S.emit("scalar", lambda h: h.activation(
                out=zb[:], in_=zb[:], func=AF.Silu),
                reads=(mk + "z",), writes=(mk + "z",))
            S.emit("vector", lambda h: h.tensor_mul(
                out=ybuf[:], in0=ybuf[:], in1=zb[:]),
                reads=(mk + "y", mk + "z"), writes=(mk + "y",))
            S.emit("vector", lambda h: h.tensor_add(
                out=ybuf[:], in0=ybuf[:], in1=rbuf[:]),
                reads=(mk + "y", mk + "rb"), writes=(mk + "y",))
            S.emit("scalar", lambda h: h.activation(
                out=du[:], in_=ybuf[:], func=AF.Square),
                reads=(mk + "y",), writes=(mk + "du",))
            def _rms(h):
                for j in range(16):
                    last = h.matmul(ps[0:1, 6, :tc], C.ones_col[:], du[:, j, :],
                                    start=(j == 0), stop=(j == 15))
                return last
            S.emit("tensor", _rms, reads=(mk + "du", "ones_col"), writes=("psb6",))
            S.emit("vector", lambda h: h.tensor_scalar_mul(
                out=rrow[:, 0, :], in0=ps[0:1, 6, :tc], scalar1=1.0 / 2048.0),
                reads=("psb6",), writes=(mk + "rrow",))
            S.emit("scalar", lambda h: h.activation(
                out=rrow[:, 1, :], in_=rrow[:, 0, :], func=AF.Sqrt,
                bias=C.cpool[0:1, 1:2]),
                reads=(mk + "rrow", "cpool"), writes=(mk + "rrow",))
            S.emit("vector", lambda h: h.reciprocal(
                out=rrow[:, 0, :], in_=rrow[:, 1, :]),
                reads=(mk + "rrow",), writes=(mk + "rrow",))
            S.emit("tensor", lambda h: h.matmul(
                ps[:, 7, :tc], C.ones_row[:], rrow[:, 0, :], start=True, stop=True),
                reads=(mk + "rrow", "ones_row"), writes=("psb7",))
            for j in range(16):
                S.emit("vector", lambda h, j=j: h.tensor_scalar_mul(
                    out=du[:, j, :], in0=ybuf[:, j, :], scalar1=nrmw[:, j:j + 1]),
                    reads=(mk + "y", mk + "nw"), writes=(mk + "du",))
            S.emit("vector", lambda h: h.tensor_tensor(
                out=gnb[:], in0=du[:],
                in1=ps.ap()[:, 7:8, :tc].broadcast_to([128, 16, tc]), op=ALU.mult),
                reads=(mk + "du", "psb7"), writes=(mk + "gn",))
            for ihalf in range(2):
                S.emit("gpsimd", lambda h, ihalf=ihalf: h.dma_start(
                    out=wout[:], in_=W["w_outT"].rearrange(
                        "(kt p) i -> p kt i", p=128)[:, :, ihalf * 512:(ihalf + 1) * 512]),
                    writes=(mk + "wout",), dma=True)
                for i in range(4):
                    ig = ihalf * 4 + i
                    pb = 2 + i % 4

                    def _outp(h, i=i, pb=pb):
                        for kk in range(16):
                            last = h.matmul(
                                ps[:, pb, :tc], wout[:, kk, ts(i)], gnb[:, kk, :],
                                start=(kk == 0), stop=(kk == 15))
                        return last
                    S.emit("tensor", _outp, reads=(mk + "wout", mk + "gn"),
                           writes=(f"psb{pb}",))
                    S.emit("vector", lambda h, ig=ig, t0=t0, pb=pb: h.tensor_add(
                        out=C.x_res[:, ig, t0:t0 + tc], in0=C.x_res[:, ig, t0:t0 + tc],
                        in1=ps[:, pb, :tc]),
                        reads=(f"psb{pb}", "x_res"), writes=("x_res",))
        S.barrier()
        S.flush()



def emit_l1b(C, tag="l1b"):
    import contextlib
    nc, S = C.nc, C.S
    W = C.dram_w["l1b"]
    tc = 512
    with contextlib.ExitStack() as es:
        ps_ = es.enter_context(nc.psum_tensor(f"{tag}_ps", [128, 8, 512], F32))
        ps = ps_
        psb16 = ps_.bitcast(BF16)
        x2 = es.enter_context(nc.sbuf_tensor(f"{tag}_x2", [128, 8, 128], F32))
        rows = es.enter_context(nc.sbuf_tensor(f"{tag}_rows", [1, 6, 128], F32))
        xn = es.enter_context(nc.sbuf_tensor(f"{tag}_xn", [128, 8, T_FULL], BF16))
        kb = es.enter_context(nc.sbuf_tensor(f"{tag}_kb", [128, 8, T_FULL], BF16))
        vt = es.enter_context(nc.sbuf_tensor(f"{tag}_vt", [128, 16, 1040], BF16))
        obuf = es.enter_context(nc.sbuf_tensor(f"{tag}_ob", [128, 8, 512], BF16))
        pt = es.enter_context(nc.sbuf_tensor(f"{tag}_pt", [128, 2, 512], BF16))
        wdk = es.enter_context(nc.sbuf_tensor(f"{tag}_wdksb", [128, 8, 128], BF16))
        wdv = es.enter_context(nc.sbuf_tensor(f"{tag}_wdvsb", [128, 8, 128], BF16))
        wuk = es.enter_context(nc.sbuf_tensor(f"{tag}_wuksb", [128, 1024], BF16))
        wuv = es.enter_context(nc.sbuf_tensor(f"{tag}_wuvsb", [128, 1024], BF16))
        wo = es.enter_context(nc.sbuf_tensor(f"{tag}_wosb", [128, 8, 512], BF16))
        bdk = es.enter_context(nc.sbuf_tensor(f"{tag}_bdksb", [128, 2], F32))
        buk = es.enter_context(nc.sbuf_tensor(f"{tag}_buksb", [128, 8, 2], F32))
        bo = es.enter_context(nc.sbuf_tensor(f"{tag}_bosb", [128, 8], F32))
        cosb = es.enter_context(nc.sbuf_tensor(f"{tag}_cossb", [128, 512], BF16))
        sinb = es.enter_context(nc.sbuf_tensor(f"{tag}_sinsb", [128, 512], BF16))
        perm = es.enter_context(nc.sbuf_tensor(f"{tag}_permsb", [128, 128], BF16))
        ckb = es.enter_context(nc.sbuf_tensor(f"{tag}_ckb", [128, 512], BF16))
        cvb = es.enter_context(nc.sbuf_tensor(f"{tag}_cvb", [128, 512], BF16))
        vtmp = es.enter_context(nc.sbuf_tensor(f"{tag}_vtmp", [128, 512], BF16))
        rtmp = es.enter_context(nc.sbuf_tensor(f"{tag}_rtmp", [128, 2, 512], BF16))
        osc = es.enter_context(nc.sbuf_tensor(f"{tag}_osc", [64, 512], BF16))
        den = es.enter_context(nc.sbuf_tensor(f"{tag}_den", [1, 512], F32))
        mk = tag
        for name, dst, key, cast in (
                ("w_dk", wdk, "wdk", True), ("w_dv", wdv, "wdv", True),
                ("w_uk", wuk, "wuk", True), ("w_uv", wuv, "wuv", True),
                ("perm", perm, "perm", True)):
            if name in ("w_dk", "w_dv"):
                inap = W[name].rearrange("(kt p) c -> p kt c", p=128)
            else:
                inap = W[name]
            S.emit("gpsimd", lambda h, inap=inap, dst=dst: h.dma_start(
                out=dst[:], in_=inap), writes=(mk + key,), dma=True)
        for name, dst, key in (("b_dkv", bdk, "bdk"), ("b_ukv", buk, "buk"),
                               ("b_o", bo, "bo")):
            S.emit("sync", lambda h, name=name, dst=dst: h.dma_start(
                out=dst[:], in_=W[name]), writes=(mk + key,), dma=True)
        for q in range(16):
            emit_ln(C, xn, "xnorm", q * 128, 128, tag + str(q), ps, x2, rows)
        S.emit("vector", lambda h: h.memset(vt[:, :, 64::65], 1.0),
               writes=(mk + "vt",))
        for ch in range(4):
            t0 = ch * tc
            csl = slice(t0, t0 + tc)
            def _ck(h, csl=csl):
                for kk in range(8):
                    last = h.matmul(ps[:, 0, :], wdk[:, kk, :], xn[:, kk, csl],
                                    start=(kk == 0), stop=(kk == 7))
                return last
            S.emit("tensor", _ck, reads=(mk + "wdk", "xnorm"), writes=("psb0",))
            S.emit("scalar", lambda h: h.activation(
                out=ckb[:], in_=ps[:, 0, :], func=AF.Identity, bias=bdk[:, 0:1]),
                reads=("psb0", mk + "bdk"), writes=(mk + "ckb",))
            def _cv(h, csl=csl):
                for kk in range(8):
                    last = h.matmul(ps[:, 1, :], wdv[:, kk, :], xn[:, kk, csl],
                                    start=(kk == 0), stop=(kk == 7))
                return last
            S.emit("tensor", _cv, reads=(mk + "wdv", "xnorm"), writes=("psb1",))
            S.emit("scalar", lambda h: h.activation(
                out=cvb[:], in_=ps[:, 1, :], func=AF.Identity, bias=bdk[:, 1:2]),
                reads=("psb1", mk + "bdk"), writes=(mk + "cvb",))
            for i in range(8):
                S.emit("tensor", lambda h, i=i: h.matmul(
                    ps[:, 2, :], wuk[:, ts(i)], ckb[:],
                    start=True, stop=True),
                    reads=(mk + "wuk", mk + "ckb"), writes=("psb2",))
                S.emit("scalar", lambda h, i=i, csl=csl: h.activation(
                    out=kb[:, i, csl], in_=ps[:, 2, :], func=AF.Identity,
                    bias=buk[:, i, 0:1]),
                    reads=("psb2", mk + "buk"), writes=(mk + "kb",))
                S.emit("tensor", lambda h, i=i: h.matmul(
                    ps[:, 3, :], wuv[:, ts(i)], cvb[:],
                    start=True, stop=True),
                    reads=(mk + "wuv", mk + "cvb"), writes=("psb3",))
                S.emit("scalar", lambda h, i=i: h.activation(
                    out=vtmp[:], in_=ps[:, 3, :], func=AF.Identity,
                    bias=buk[:, i, 1:2]),
                    reads=("psb3", mk + "buk"), writes=(mk + "vtmp",))
                for tt4 in range(4):
                    pb = 6 + tt4 % 2
                    S.emit("tensor", lambda h, tt4=tt4, pb=pb: h.transpose(
                        psb16[:, pb, 0:128], vtmp[:, tt4 * 128:(tt4 + 1) * 128],
                        C.ident_b[:]),
                        reads=(mk + "vtmp", "ident_b"), writes=(f"psb{pb}",))
                    ttg = ch * 4 + tt4
                    S.emit("scalar", lambda h, i=i, ttg=ttg, pb=pb: h.copy(
                        out=vt[:, ttg, (2 * i) * 65:(2 * i) * 65 + 64],
                        in_=psb16[:, pb, 0:64]),
                        reads=(f"psb{pb}",), writes=(mk + "vt",))
                    S.emit("scalar", lambda h, i=i, ttg=ttg, pb=pb: h.copy(
                        out=vt[:, ttg, (2 * i + 1) * 65:(2 * i + 1) * 65 + 64],
                        in_=psb16[:, pb, 64:128]),
                        reads=(f"psb{pb}",), writes=(mk + "vt",))
            S.emit("gpsimd", lambda h, csl=csl: h.dma_start(
                out=cosb[:], in_=W["cos"][:, csl]),
                writes=(mk + "cos",), dma=True)
            S.emit("gpsimd", lambda h, csl=csl: h.dma_start(
                out=sinb[:], in_=W["sin"][:, csl]),
                writes=(mk + "sin",), dma=True)
            for buf, bkey in ((xn, "xnorm"), (kb, mk + "kb")):
                for dt in range(8):
                    pb = 4 + dt % 2
                    S.emit("tensor", lambda h, buf=buf, dt=dt, csl=csl, pb=pb: h.matmul(
                        ps[:, pb, :], perm[:], buf[:, dt, csl],
                        start=True, stop=True),
                        reads=(bkey, mk + "perm"), writes=(f"psb{pb}",))
                    S.emit("vector", lambda h, buf=buf, dt=dt, csl=csl: h.tensor_tensor(
                        out=rtmp[:, 0, :], in0=buf[:, dt, csl], in1=cosb[:],
                        op=ALU.mult),
                        reads=(bkey, mk + "cos"), writes=(mk + "rtmp",))
                    S.emit("vector", lambda h, dt=dt, csl=csl, pb=pb: h.tensor_tensor(
                        out=rtmp[:, 1, :], in0=ps[:, pb, :], in1=sinb[:],
                        op=ALU.mult),
                        reads=(f"psb{pb}", mk + "sin"), writes=(mk + "rtmp",))
                    S.emit("vector", lambda h, buf=buf, dt=dt, csl=csl: h.tensor_add(
                        out=buf[:, dt, csl], in0=rtmp[:, 0, :], in1=rtmp[:, 1, :]),
                        reads=(mk + "rtmp",), writes=(bkey,)),
        for ch in range(4):
            for hh in range(16):
                hb = 64 * (hh % 2)
                hdt = hh // 2
                qsl = slice(ch * 512, (ch + 1) * 512)
                pvb = 4 + hh % 2
                for tkt in range(16):
                    sb = tkt % 3
                    S.emit("tensor", lambda h, hb=hb, hdt=hdt, tkt=tkt, qsl=qsl, sb=sb: h.matmul(
                        ps[:, sb, :], kb[hb:hb + 64, hdt, ts(tkt)],
                        xn[hb:hb + 64, hdt, qsl], start=True, stop=True),
                        reads=(mk + "kb", "xnorm"), writes=(f"psb{sb}",))
                    psl = tkt % 2
                    S.emit("scalar", lambda h, tkt=tkt, sb=sb, psl=psl: h.activation(
                        out=pt[:, psl, :], in_=ps[:, sb, :],
                        func=AF.Exp, scale=0.125),
                        reads=(f"psb{sb}",), writes=(f"{mk}pt{psl}",))
                    S.emit("tensor", lambda h, hh=hh, tkt=tkt, psl=psl, pvb=pvb: h.matmul(
                        ps[0:65, pvb, :], vt[:, tkt, hh * 65:(hh + 1) * 65],
                        pt[:, psl, :],
                        start=(tkt == 0), stop=(tkt == 15)),
                        reads=(mk + "vt", f"{mk}pt{psl}"), writes=(f"psb{pvb}",))
                S.emit("scalar", lambda h, pvb=pvb: h.copy(
                    out=osc[:], in_=ps[0:64, pvb, :]),
                    reads=(f"psb{pvb}",), writes=(mk + "osc",))
                S.emit("scalar", lambda h, pvb=pvb: h.copy(
                    out=den[:], in_=ps[64:65, pvb, :]),
                    reads=(f"psb{pvb}",), writes=(mk + "den",))
                S.emit("vector", lambda h: h.reciprocal(
                    out=den[:], in_=den[:]),
                    reads=(mk + "den",), writes=(mk + "den",))
                S.emit("tensor", lambda h: h.matmul(
                    ps[0:64, 6, :], C.ones_row[0:1, 0:64], den[:],
                    start=True, stop=True),
                    reads=(mk + "den", "ones_row"), writes=("psb6",))
                S.emit("vector", lambda h, hb=hb, hdt=hdt: h.tensor_tensor(
                    out=obuf[hb:hb + 64, hdt, :], in0=osc[:],
                    in1=ps[0:64, 6, :], op=ALU.mult),
                    reads=(mk + "osc", "psb6"), writes=(mk + "ob",))
            for wh in range(2):
                S.emit("gpsimd", lambda h, wh=wh: h.dma_start(
                    out=wo[:], in_=W["w_o"].rearrange(
                        "(kt p) m -> p kt m", p=128)[:, :, wh * 512:(wh + 1) * 512]),
                    writes=(mk + "wo",), dma=True)
                for i in range(4):
                    ig = wh * 4 + i
                    pb = 6 + i % 2

                    def _wo(h, i=i, pb=pb):
                        for kk in range(8):
                            last = h.matmul(
                                ps[:, pb, :], wo[:, kk, ts(i)], obuf[:, kk, :],
                                start=(kk == 0), stop=(kk == 7))
                        return last
                    S.emit("tensor", _wo, reads=(mk + "wo", mk + "ob"),
                           writes=(f"psb{pb}",))
                    S.emit("scalar", lambda h, ig=ig, pb=pb: h.activation(
                        out=rtmp[:, 0, :], in_=ps[:, pb, :], func=AF.Identity,
                        bias=bo[:, ig:ig + 1]),
                        reads=(f"psb{pb}", mk + "bo"), writes=(mk + "rtmp",))
                    S.emit("vector", lambda h, ig=ig, ch=ch: h.tensor_add(
                        out=C.x_res[:, ig, ch * 512:(ch + 1) * 512],
                        in0=C.x_res[:, ig, ch * 512:(ch + 1) * 512],
                        in1=rtmp[:, 0, :]),
                        reads=(mk + "rtmp", "x_res"), writes=("x_res",))
        S.barrier()
        S.flush()


def build_program(stages, io_mode):
    nc = bass.Bass("TRN2", target_bir_lowering=False)
    C = Ctx()
    C.nc = nc
    C.dram = {}
    C.dram_w = {}
    declare_dram(C, stages, io_mode)
    with contextlib.ExitStack() as es0:
        sems = {}
        for e in ENGINES:
            for p in range(Sched.NPOOL[e]):
                sems[f"{e}:c{p}"] = es0.enter_context(
                    nc.semaphore(f"sem_{e}_c{p}"))
            for p in range(Sched.NDMA.get(e, 0)):
                sems[f"{e}:d{p}"] = es0.enter_context(
                    nc.semaphore(f"sem_{e}_d{p}"))
        block = es0.enter_context(nc.Block())
        C.S = Sched(nc, block, sems)
        C.x_res = es0.enter_context(nc.sbuf_tensor("x_res", [128, NDT, T_FULL], F32))
        build_consts(C, es0)
        C.S.barrier()
        if io_mode == "full":
            with contextlib.ExitStack() as es:
                stage_pro(C, es)
                C.S.barrier()
                C.S.flush()
        else:
            with contextlib.ExitStack() as es:
                stage_load_res(C, es)
                C.S.barrier()
                C.S.flush()
        for st in stages:
            with contextlib.ExitStack() as es:
                STAGE_FNS[st](C, es)
                C.S.flush()
        if io_mode == "full":
            with contextlib.ExitStack() as es:
                C.S.barrier()
                stage_epi(C, es)
                C.S.flush()
        else:
            with contextlib.ExitStack() as es:
                C.S.barrier()
                stage_store_res(C, es)
                C.S.flush()
        C.S.emit("sync", lambda h: h.nop(), reads=("out_done",))
        C.S.flush()
    return C


def declare_dram(C, stages, io_mode):
    nc = C.nc
    if io_mode == "full":
        C.dram["x_tm"] = nc.dram_tensor("x_tm", [T_FULL, D_MODEL], F32,
                                        kind="ExternalInput").ap()
        C.dram["out_tm"] = nc.dram_tensor("out_tm", [T_FULL, D_MODEL], F32,
                                          kind="ExternalOutput").ap()
    else:
        C.dram["xin_fm"] = nc.dram_tensor("xin_fm", [D_MODEL, T_FULL], F32,
                                          kind="ExternalInput").ap()
        C.dram["out_tm"] = nc.dram_tensor("xout_fm", [D_MODEL, T_FULL], F32,
                                          kind="ExternalOutput").ap()
        C.dram["xout_fm"] = C.dram["out_tm"]
    for st in stages:
        if st in ("l0a", "l0b"):
            nm = 1 if st == "l0a" else 4
            tsub = T_FULL if st == "l0a" else 512
            W = {}
            W["w_inT"] = nc.dram_tensor(f"{st}_w_inT", [nm, 1024, 4096], F32,
                                        kind="ExternalInput").ap()
            W["conv_w"] = nc.dram_tensor(f"{st}_conv_w", [nm, 128, 16, 4], F32,
                                         kind="ExternalInput").ap()
            W["conv_b"] = nc.dram_tensor(f"{st}_conv_b", [nm, 128, 16], F32,
                                         kind="ExternalInput").ap()
            W["w_xprojT"] = nc.dram_tensor(f"{st}_w_xprojT", [nm, 2048, 128], F32,
                                           kind="ExternalInput").ap()
            W["w_dtT"] = nc.dram_tensor(f"{st}_w_dtT", [nm, 64, 2048], F32,
                                        kind="ExternalInput").ap()
            W["b_dt"] = nc.dram_tensor(f"{st}_b_dt", [nm, 128, 16], F32,
                                       kind="ExternalInput").ap()
            W["A_log"] = nc.dram_tensor(f"{st}_A_log", [nm, 128, 256], F32,
                                        kind="ExternalInput").ap()
            W["D"] = nc.dram_tensor(f"{st}_D", [nm, 128, 16], F32,
                                    kind="ExternalInput").ap()
            W["w_outT"] = nc.dram_tensor(f"{st}_w_outT", [nm, 2048, 1024], F32,
                                         kind="ExternalInput").ap()
            C.dram_w[st] = W
            C.dram[f"{st}_xz"] = nc.dram_tensor(f"{st}_xz", [nm, 4096, tsub], BF16,
                                                kind="Internal").ap()
        elif st in ("l0f1", "l0f2") and "l0f" not in C.dram_w:
            W = {}
            W["gate_wT"] = nc.dram_tensor("l0f_gate_wT", [1024, 16], F32,
                                          kind="ExternalInput").ap()
            W["gate_b"] = nc.dram_tensor("l0f_gate_b", [1, 16], F32,
                                         kind="ExternalInput").ap()
            W["w13T"] = nc.dram_tensor("l0f_w13T", [16, 1024, 1024], F32,
                                       kind="ExternalInput").ap()
            W["w2T"] = nc.dram_tensor("l0f_w2T", [16, 512, 1024], F32,
                                      kind="ExternalInput").ap()
            W["w13sT"] = nc.dram_tensor("l0f_w13sT", [1024, 1024], F32,
                                        kind="ExternalInput").ap()
            W["w2sT"] = nc.dram_tensor("l0f_w2sT", [512, 1024], F32,
                                       kind="ExternalInput").ap()
            C.dram_w["l0f"] = W
        elif st in ("l1f1", "l1f2") and "l1f" not in C.dram_w:
            W = {}
            W["w_gate"] = nc.dram_tensor("l1f_w_gate", [1024, 8], F32,
                                         kind="ExternalInput").ap()
            W["b_gate"] = nc.dram_tensor("l1f_b_gate", [1, 8], F32,
                                         kind="ExternalInput").ap()
            W["W1"] = nc.dram_tensor("l1f_W1", [8, 1024, 4096], F32,
                                     kind="ExternalInput").ap()
            W["W2"] = nc.dram_tensor("l1f_W2", [8, 4096, 1024], F32,
                                     kind="ExternalInput").ap()
            W["b1p"] = nc.dram_tensor("l1f_b1p", [128, 32, 8], F32,
                                      kind="ExternalInput").ap()
            W["b2"] = nc.dram_tensor("l1f_b2", [8, 1024], F32,
                                     kind="ExternalInput").ap()
            C.dram_w["l1f"] = W
        elif st == "l1a":
            W = {}
            W["w_inT"] = nc.dram_tensor("l1a_w_inT", [1024, 5248], F32,
                                        kind="ExternalInput").ap()
            W["w_resT"] = nc.dram_tensor("l1a_w_resT", [1024, 2048], F32,
                                         kind="ExternalInput").ap()
            W["w_outT"] = nc.dram_tensor("l1a_w_outT", [2048, 1024], F32,
                                         kind="ExternalInput").ap()
            W["w_dtT"] = nc.dram_tensor("l1a_w_dtT", [64, 8], F32,
                                        kind="ExternalInput").ap()
            W["conv_w"] = nc.dram_tensor("l1a_conv_w", [128, 16, 4], F32,
                                         kind="ExternalInput").ap()
            W["conv_b"] = nc.dram_tensor("l1a_conv_b", [128, 16], F32,
                                         kind="ExternalInput").ap()
            W["dt_bias"] = nc.dram_tensor("l1a_dt_bias", [8, 1], F32,
                                          kind="ExternalInput").ap()
            W["A_flat"] = nc.dram_tensor("l1a_A_flat", [1, 128], F32,
                                         kind="ExternalInput").ap()
            W["Dskip"] = nc.dram_tensor("l1a_Dskip", [128, 16], F32,
                                        kind="ExternalInput").ap()
            W["norm_w"] = nc.dram_tensor("l1a_norm_w", [128, 16], F32,
                                         kind="ExternalInput").ap()
            C.dram_w["l1a"] = W
            C.dram["l1a_p"] = nc.dram_tensor("l1a_p", [5248, 2048], BF16,
                                             kind="Internal").ap()
            C.dram["l1a_r"] = nc.dram_tensor("l1a_r", [2048, 2048], BF16,
                                             kind="Internal").ap()
        elif st == "l1b":
            W = {}
            for nm, shp in (("w_dk", [1024, 128]), ("w_dv", [1024, 128]),
                            ("w_uk", [128, 1024]), ("w_uv", [128, 1024]),
                            ("w_o", [1024, 1024]), ("cos", [128, 2048]),
                            ("sin", [128, 2048]), ("perm", [128, 128]),
                            ("b_dkv", [128, 2]), ("b_ukv", [128, 8, 2]),
                            ("b_o", [128, 8])):
                W[nm] = nc.dram_tensor(f"l1b_{nm}", shp, F32,
                                       kind="ExternalInput").ap()
            C.dram_w["l1b"] = W


def stage_l0f1(C, es):
    emit_l0ffn(C, "l0f1")


def stage_l0f2(C, es):
    emit_l0ffn(C, "l0f2")


def stage_l1f1(C, es):
    emit_l1ffn(C, "l1f1")


def stage_l1f2(C, es):
    emit_l1ffn(C, "l1f2")


def stage_l1a(C, es):
    emit_l1a(C)


def stage_l1b(C, es):
    emit_l1b(C)


STAGE_FNS = {
    "l1b": stage_l1b,
    "l1a": stage_l1a,
    "l0a": stage_l0a,
    "l0b": stage_l0b,
    "l0f1": stage_l0f1,
    "l0f2": stage_l0f2,
    "l1f1": stage_l1f1,
    "l1f2": stage_l1f2,
}


def prep_weights(I):
    out = {}

    def mamba1_pack(pfx, w_in, conv_w, conv_b, w_xproj, w_dt, b_dt, a_log, d_skip,
                    w_out):
        nm = w_in.shape[0]
        o = {}
        o["w_inT"] = np.ascontiguousarray(np.swapaxes(w_in, 1, 2))
        o["conv_w"] = np.ascontiguousarray(
            conv_w.reshape(nm, 16, 128, 4).transpose(0, 2, 1, 3))
        o["conv_b"] = np.ascontiguousarray(
            conv_b.reshape(nm, 16, 128).transpose(0, 2, 1))
        xp = np.zeros((nm, 2048, 128), np.float32)
        xp[:, :, 0:16] = np.swapaxes(w_xproj[:, 64:80, :], 1, 2)
        xp[:, :, 32:48] = np.swapaxes(w_xproj[:, 80:96, :], 1, 2)
        xp[:, :, 64:128] = np.swapaxes(w_xproj[:, 0:64, :], 1, 2)
        o["w_xprojT"] = xp
        o["w_dtT"] = np.ascontiguousarray(np.swapaxes(w_dt, 1, 2))
        o["b_dt"] = np.ascontiguousarray(b_dt.reshape(nm, 16, 128).transpose(0, 2, 1))
        o["A_log"] = np.ascontiguousarray(
            a_log.reshape(nm, 16, 128, 16).transpose(0, 2, 1, 3).reshape(nm, 128, 256))
        o["D"] = np.ascontiguousarray(d_skip.reshape(nm, 16, 128).transpose(0, 2, 1))
        o["w_outT"] = np.ascontiguousarray(np.swapaxes(w_out, 1, 2))
        return {f"{pfx}_{k}": v for k, v in o.items()}

    out.update(mamba1_pack(
        "l0a", I["L0_attn_a_W_in"][None], I["L0_attn_a_conv_w"][None],
        I["L0_attn_a_conv_b"][None], I["L0_attn_a_W_xproj"][None],
        I["L0_attn_a_W_dt"][None], I["L0_attn_a_b_dt"][None],
        I["L0_attn_a_A_log"][None], I["L0_attn_a_D_skip"][None],
        I["L0_attn_a_W_out"][None]))
    out.update(mamba1_pack(
        "l0b", I["L0_attn_b_in_w"], I["L0_attn_b_conv_w"], I["L0_attn_b_conv_b"],
        I["L0_attn_b_xproj_w"], I["L0_attn_b_dt_w"], I["L0_attn_b_dt_b"],
        I["L0_attn_b_A_log"], I["L0_attn_b_D_skip"], I["L0_attn_b_out_w"]))
    out["l0f_gate_wT"] = np.swapaxes(I["L0_ffn_gate_w"], 0, 1)
    out["l0f_gate_b"] = np.asarray(I["L0_ffn_gate_b"])[None, :]
    out["l0f_w13T"] = np.swapaxes(I["L0_ffn_w13"], 1, 2)
    out["l0f_w2T"] = np.swapaxes(I["L0_ffn_w2"], 1, 2)
    out["l0f_w13sT"] = np.swapaxes(I["L0_ffn_w13_s"], 0, 1)
    out["l0f_w2sT"] = np.swapaxes(I["L0_ffn_w2_s"], 0, 1)
    out["l1f_w_gate"] = np.asarray(I["L1_ffn_w_gate"])
    out["l1f_b_gate"] = np.asarray(I["L1_ffn_b_gate"])[None, :]
    out["l1f_W1"] = np.asarray(I["L1_ffn_W1"])
    out["l1f_W2"] = np.asarray(I["L1_ffn_W2"])
    out["l1f_b1p"] = np.asarray(I["L1_ffn_b1"]).reshape(8, 32, 128).transpose(2, 1, 0)
    out["l1f_b2"] = np.asarray(I["L1_ffn_b2"])
    win = np.asarray(I["L1_attn_a_W_in"])
    wp = np.zeros((5248, 1024), np.float32)
    wp[0:4160] = win[0:4160]
    for hh in range(8):
        wp[4224 + 64 * hh: 4224 + 64 * hh + 16] = win[4160 + 16 * hh: 4160 + 16 * (hh + 1)]
        wp[4736 + 64 * hh: 4736 + 64 * hh + 16] = win[4288 + 16 * hh: 4288 + 16 * (hh + 1)]
    out["l1a_w_inT"] = np.ascontiguousarray(wp.T)
    out["l1a_w_resT"] = np.ascontiguousarray(np.asarray(I["L1_attn_a_W_res"]).T)
    out["l1a_w_outT"] = np.ascontiguousarray(np.asarray(I["L1_attn_a_W_out"]).T)
    out["l1a_w_dtT"] = np.ascontiguousarray(np.asarray(I["L1_attn_a_W_dt"]).T)
    out["l1a_conv_w"] = np.ascontiguousarray(
        np.asarray(I["L1_attn_a_conv_w"]).reshape(16, 128, 4).transpose(1, 0, 2))
    out["l1a_conv_b"] = np.ascontiguousarray(
        np.asarray(I["L1_attn_a_conv_b"]).reshape(16, 128).T)
    out["l1a_dt_bias"] = np.asarray(I["L1_attn_a_dt_bias"])[:, None]
    out["l1a_A_flat"] = np.asarray(I["L1_attn_a_A_log"]).reshape(1, 128)
    out["l1a_Dskip"] = np.ascontiguousarray(
        np.asarray(I["L1_attn_a_Dskip"]).reshape(2048).reshape(16, 128).T)
    out["l1a_norm_w"] = np.ascontiguousarray(
        np.asarray(I["L1_attn_a_norm_w"]).reshape(16, 128).T)
    out["l1b_w_dk"] = np.asarray(I["L1_attn_b_w_dk"])
    out["l1b_w_dv"] = np.asarray(I["L1_attn_b_w_dv"])
    out["l1b_w_uk"] = np.asarray(I["L1_attn_b_w_uk"])
    out["l1b_w_uv"] = np.asarray(I["L1_attn_b_w_uv"])
    out["l1b_w_o"] = np.asarray(I["L1_attn_b_w_o"])
    out["l1b_b_dkv"] = np.stack([np.asarray(I["L1_attn_b_b_dk"]),
                                 np.asarray(I["L1_attn_b_b_dv"])], axis=1)
    out["l1b_b_ukv"] = np.stack(
        [np.asarray(I["L1_attn_b_b_uk"]).reshape(8, 128).T,
         np.asarray(I["L1_attn_b_b_uv"]).reshape(8, 128).T], axis=2)
    out["l1b_b_o"] = np.ascontiguousarray(
        np.asarray(I["L1_attn_b_b_o"]).reshape(8, 128).T)
    t = np.arange(2048, dtype=np.float64)
    invf = 10000.0 ** (-(np.arange(128) % 32) / 32.0)
    ang = t[None, :] * invf[:, None]
    out["l1b_cos"] = np.cos(ang).astype(np.float32)
    out["l1b_sin"] = np.sin(ang).astype(np.float32)
    pm = np.zeros((128, 128), np.float32)
    for i in range(128):
        j = i % 64
        if j < 32:
            pm[i + 32, i] = -1.0
        else:
            pm[i - 32, i] = 1.0
    out["l1b_perm"] = pm
    return {k: np.ascontiguousarray(np.asarray(v, np.float32)) for k, v in out.items()}


def input_names_for(stages):
    pref = {"l0a": "l0a_", "l0b": "l0b_", "l0f1": "l0f_", "l0f2": "l0f_",
            "l1a": "l1a_", "l1b": "l1b_", "l1f1": "l1f_", "l1f2": "l1f_"}
    return sorted({p for st in stages for p in [pref[st]]})


def run_chain(inputs, launch_groups=None):
    launch_groups = launch_groups or LAUNCH_GROUPS
    x = np.asarray(inputs["L0_attn_a_hidden_states"], np.float32)
    wmap = prep_weights(inputs)
    n_groups = len(launch_groups)
    cur = None
    for gi, stages in enumerate(launch_groups):
        first = gi == 0
        last = gi == n_groups - 1
        io_mode = "full" if n_groups == 1 else ("chain" if not (first or last) else
                                                ("full" if n_groups == 1 else "chain"))
        C = build_program(stages, "chain")
        names = needed_weight_names(stages)
        in_maps = []
        for c in range(N_CORES):
            m = {"xin_fm": cur[c]}
            for nmname in names:
                m[nmname] = wmap[nmname]
            in_maps.append(m)
        res = run_bass_kernel_spmd(C.nc, in_maps, core_ids=list(range(N_CORES)))
        cur = np.stack([r["xout_fm"] for r in res.results])
    return cur


def needed_weight_names(stages):
    names = []
    for st in stages:
        if st in ("l0a", "l0b"):
            names += [f"{st}_{k}" for k in ("w_inT", "conv_w", "conv_b", "w_xprojT",
                                            "w_dtT", "b_dt", "A_log", "D", "w_outT")]
        elif st in ("l0f1", "l0f2"):
            names += [f"l0f_{k}" for k in ("gate_wT", "gate_b", "w13T", "w2T",
                                           "w13sT", "w2sT")]
        elif st in ("l1f1", "l1f2"):
            names += [f"l1f_{k}" for k in ("w_gate", "b_gate", "W1", "W2",
                                           "b1p", "b2")]
        elif st == "l1a":
            names += [f"l1a_{k}" for k in ("w_inT", "w_resT", "w_outT", "w_dtT",
                                           "conv_w", "conv_b", "dt_bias",
                                           "A_flat", "Dskip", "norm_w")]
        elif st == "l1b":
            names += [f"l1b_{k}" for k in ("w_dk", "w_dv", "w_uk", "w_uv", "w_o",
                                           "cos", "sin", "perm", "b_dkv",
                                           "b_ukv", "b_o")]
    return sorted(set(names))


def kernel(**inputs):
    x = np.ascontiguousarray(np.asarray(inputs["L0_attn_a_hidden_states"],
                                        np.float32))
    wmap = prep_weights(inputs)
    C = build_program(ALL_STAGES, "full")
    names = needed_weight_names(ALL_STAGES)
    in_maps = []
    for c in range(N_CORES):
        m = {"x_tm": np.ascontiguousarray(x[c])}
        for nm in names:
            m[nm] = wmap[nm]
        in_maps.append(m)
    res = run_bass_kernel_spmd(C.nc, in_maps, core_ids=list(range(N_CORES)))
    return np.stack([np.asarray(r["out_tm"], np.float32) for r in res.results])


# version 12
# speedup vs baseline: 1.1884x; 1.1884x over previous
import contextlib
import os
import numpy as np

MAMBA_PHASES = int(os.environ.get("MAMBA_PHASES", "2"))

import concourse.bass as bass
import concourse.mybir as mybir
from concourse.bass_utils import run_bass_kernel_spmd

F32 = mybir.dt.float32
BF16 = mybir.dt.bfloat16
AF = mybir.ActivationFunctionType
ALU = mybir.AluOpType
AX = mybir.AxisListType

D_MODEL = 1024
T_FULL = 2048
NDT = 8
N_CORES = 8

ENGINES = ("sync", "tensor", "vector", "scalar", "gpsimd")

ALL_STAGES = ["l0a", "l0f1", "l0b", "l0f2", "l1a", "l1f1", "l1b", "l1f2"]
LAUNCH_GROUPS = [ALL_STAGES]


def ts(i, n=128):
    return slice(i * n, (i + 1) * n)


class Sched:

    NPOOL = {"sync": 6, "tensor": 12, "vector": 12, "scalar": 12, "gpsimd": 6}
    NDMA = {"sync": 8, "gpsimd": 8, "scalar": 4}

    def __init__(self, nc, block, sems):
        self.nc = nc
        self.block = block
        self.sems = sems
        self.cnt = {}
        for e in ENGINES:
            for p in range(self.NPOOL[e]):
                self.cnt[f"{e}:c{p}"] = 0
            for p in range(self.NDMA.get(e, 0)):
                self.cnt[f"{e}:d{p}"] = 0
        self.gi = {e: 0 for e in ENGINES}
        self.dma_i = {e: 0 for e in ENGINES}
        self.eclock = {e: {} for e in ENGINES}
        self.gclock = {}
        self.bufs = {}
        self.pending = {e: [] for e in ENGINES}
        self.nbar = 0

    def emit(self, eng, fn, reads=(), writes=(), dma=False, extra_waits=()):
        need = dict(extra_waits)
        ec = self.eclock[eng]

        def add_clock(c):
            for f, v in c.items():
                if ec.get(f, 0) < v and need.get(f, 0) < v:
                    need[f] = v

        for k in reads:
            b = self.bufs.get(k)
            if b is not None:
                add_clock(b["w"])
        for k in writes:
            b = self.bufs.get(k)
            if b is not None:
                add_clock(b["w"])
                add_clock(b["r"])
        if dma:
            p = self.dma_i[eng] % self.NDMA[eng]
            self.dma_i[eng] += 1
            tag = f"{eng}:d{p}"
            inc = 16
        else:
            p = self.gi[eng] % self.NPOOL[eng]
            self.gi[eng] += 1
            tag = f"{eng}:c{p}"
            inc = 1
        prev = self.cnt[tag]
        if prev and ec.get(tag, 0) < prev and need.get(tag, 0) < prev:
            need[tag] = prev
        self.cnt[tag] += inc
        mycnt = self.cnt[tag]
        waits = sorted(need.items())
        for f, v in waits:
            if ec.get(f, 0) < v:
                ec[f] = v
            pc = self.gclock.get((f, v))
            if pc:
                for f2, v2 in pc.items():
                    if ec.get(f2, 0) < v2:
                        ec[f2] = v2
        myclock = dict(ec)
        myclock[tag] = mycnt
        self.gclock[(tag, mycnt)] = myclock
        for k in writes:
            self.bufs[k] = {"w": {tag: mycnt}, "r": {}}
        for k in reads:
            b = self.bufs.setdefault(k, {"w": {}, "r": {}})
            b["r"][tag] = mycnt
        sems = self.sems

        def run(h, waits=waits, fn=fn, tag=tag, inc=inc):
            for f, v in waits:
                h.wait_ge(sems[f], v)
            fn(h).then_inc(sems[tag], inc)

        self.pending[eng].append(run)
        return (tag, mycnt)

    def flush(self):
        for eng in ENGINES:
            lst = self.pending[eng]
            if not lst:
                continue

            def body(h, lst=tuple(lst)):
                for r in lst:
                    r(h)

            getattr(self.block, eng)(body)
            self.pending[eng] = []

    def drain_waits(self, eng):
        ec = self.eclock[eng]
        out = []
        for p in range(self.NPOOL[eng]):
            tag = f"{eng}:c{p}"
            if self.cnt[tag] and ec.get(tag, 0) < self.cnt[tag]:
                out.append((tag, self.cnt[tag]))
        for p in range(self.NDMA.get(eng, 0)):
            tag = f"{eng}:d{p}"
            if self.cnt[tag] and ec.get(tag, 0) < self.cnt[tag]:
                out.append((tag, self.cnt[tag]))
        return tuple(out)

    def barrier(self):
        n = self.nbar
        self.nbar += 1
        keys = tuple(f"__bar{n}_{e}" for e in ENGINES)
        for e, k in zip(ENGINES, keys):
            self.emit(e, lambda h: h.nop(), writes=(k,),
                      extra_waits=self.drain_waits(e))
        for e in ENGINES:
            self.emit(e, lambda h: h.nop(), reads=keys)


class Ctx:
    pass


def build_consts(C, es):
    nc, S = C.nc, C.S
    C.ident_f = es.enter_context(nc.sbuf_tensor("ident_f", [128, 128], F32))
    C.ident_b = es.enter_context(nc.sbuf_tensor("ident_b", [128, 128], BF16))
    C.ones_col = es.enter_context(nc.sbuf_tensor("ones_col", [128, 1], F32))
    C.ones_row = es.enter_context(nc.sbuf_tensor("ones_row", [1, 128], F32))
    C.cpool = es.enter_context(nc.sbuf_tensor("cpool", [128, 4], F32))

    for t, key in ((C.ident_f, "ident_f"), (C.ident_b, "ident_b")):
        S.emit("gpsimd", lambda h, t=t: h.memset(t[:], 0.0), writes=(key,))
        S.emit("gpsimd", lambda h, t=t: h.affine_select(
            out=t[:], in_=t[:], compare_op=ALU.not_equal, fill=1.0, base=0,
            pattern=[[-1, 128]], channel_multiplier=1),
            reads=(key,), writes=(key,))
    S.emit("gpsimd", lambda h: h.memset(C.ones_col[:], 1.0), writes=("ones_col",))
    S.emit("gpsimd", lambda h: h.memset(C.ones_row[:], 1.0), writes=("ones_row",))
    for col, val in ((0, 1e-5), (1, 1e-6), (2, 1.0), (3, 0.0)):
        S.emit("gpsimd", lambda h, col=col, val=val: h.memset(
            C.cpool[:, col:col + 1], val), writes=("cpool",))
    nc.const_aps.register(F32, 1e-5, C.cpool[:, 0:1])
    nc.const_aps.register(F32, 1e-6, C.cpool[:, 1:2])


def alloc_esel(C, es, tag):
    nc, S = C.nc, C.S
    C.esel = es.enter_context(nc.sbuf_tensor(f"esel_{tag}", [16, 16, 128], F32))
    S.emit("gpsimd", lambda h: h.memset(C.esel[:], 0.0), writes=("esel",))
    S.emit("gpsimd", lambda h: h.affine_select(
        out=C.esel[:], in_=C.esel[:], compare_op=ALU.not_equal, fill=1.0, base=0,
        pattern=[[-1, 16], [0, 128]], channel_multiplier=1),
        reads=("esel",), writes=("esel",))


def stage_pro(C, es):
    nc, S = C.nc, C.S
    x_tm = C.dram["x_tm"]
    xt = es.enter_context(nc.sbuf_tensor("pro_xt", [128, 2, 1024], F32))
    ps = es.enter_context(nc.psum_tensor("pro_ps", [128, 8, 512], F32))
    for tt in range(16):
        sl = tt % 2
        S.emit("sync", lambda h, tt=tt, sl=sl: h.dma_start(
            out=xt[:, sl, :], in_=x_tm[ts(tt), :]),
            writes=(f"pro_xt{sl}",), dma=True)
        for dt in range(NDT):
            pb = dt % 8
            S.emit("tensor", lambda h, sl=sl, dt=dt, pb=pb: h.transpose(
                ps[:, pb, 0:128], xt[:, sl, ts(dt)], C.ident_f[:]),
                reads=(f"pro_xt{sl}", "ident_f"), writes=(f"pro_ps{pb}",))
            S.emit("scalar", lambda h, tt=tt, dt=dt, pb=pb: h.copy(
                out=C.x_res[:, dt, ts(tt)], in_=ps[:, pb, 0:128]),
                reads=(f"pro_ps{pb}",), writes=("x_res",))


def stage_epi(C, es):
    nc, S = C.nc, C.S
    out_tm = C.dram["out_tm"]
    ot = es.enter_context(nc.sbuf_tensor("epi_ot", [128, 2, 1024], F32))
    ps = es.enter_context(nc.psum_tensor("epi_ps", [128, 8, 512], F32))
    for tt in range(16):
        sl = tt % 2
        for dt in range(NDT):
            pb = dt % 8
            S.emit("tensor", lambda h, tt=tt, dt=dt, pb=pb: h.transpose(
                ps[:, pb, 0:128], C.x_res[:, dt, ts(tt)], C.ident_f[:]),
                reads=("x_res", "ident_f"), writes=(f"epi_ps{pb}",))
            S.emit("scalar", lambda h, dt=dt, sl=sl, pb=pb: h.copy(
                out=ot[:, sl, ts(dt)], in_=ps[:, pb, 0:128]),
                reads=(f"epi_ps{pb}",), writes=(f"epi_ot{sl}",))
        S.emit("sync", lambda h, tt=tt, sl=sl: h.dma_start(
            out=out_tm[ts(tt), :], in_=ot[:, sl, :]),
            reads=(f"epi_ot{sl}",), writes=("out_done",), dma=True)


def stage_load_res(C, es):
    S = C.S
    xin = C.dram["xin_fm"]
    S.emit("sync", lambda h: h.dma_start(
        out=C.x_res[:], in_=xin.rearrange("(a p) t -> p a t", p=128)),
        writes=("x_res",), dma=True)


def stage_store_res(C, es):
    S = C.S
    xout = C.dram["xout_fm"]
    S.emit("sync", lambda h: h.dma_start(
        out=xout.rearrange("(a p) t -> p a t", p=128), in_=C.x_res[:]),
        reads=("x_res",), writes=("out_done",), dma=True)


def emit_ln(C, xnorm, xnk, t0, tc, tag, ps, x2, rows, banks=(4, 5, 6, 7),
            keep_ab=None, keep_off=0, keep_key=None, out_t0=None):
    S = C.S
    ba, bb, b1, b2 = banks
    k = "ln" + tag
    kx2 = "buf_" + x2.name
    krows = "buf_" + rows.name
    sl = (slice(None), slice(None), slice(t0, t0 + tc))
    S.emit("scalar", lambda h: h.activation(out=x2[:, :, :tc], in_=C.x_res[sl],
                                            func=AF.Square),
           reads=("x_res",), writes=(kx2,))
    for j in range(NDT):
        S.emit("tensor", lambda h, j=j: h.matmul(
            ps[0:1, b1, :tc], C.ones_col[:], C.x_res[:, j, t0:t0 + tc],
            start=(j == 0), stop=(j == NDT - 1)),
            reads=("x_res", "ones_col"), writes=(f"psb{b1}",))
    for j in range(NDT):
        S.emit("tensor", lambda h, j=j: h.matmul(
            ps[0:1, b2, :tc], C.ones_col[:], x2[:, j, :tc],
            start=(j == 0), stop=(j == NDT - 1)),
            reads=(kx2, "ones_col"), writes=(f"psb{b2}",))
    S.emit("vector", lambda h: h.tensor_scalar_mul(
        out=rows[:, 2, :tc], in0=ps[0:1, b1, :tc], scalar1=1.0 / D_MODEL),
        reads=(f"psb{b1}",), writes=(krows,))
    S.emit("vector", lambda h: h.tensor_scalar_mul(
        out=rows[:, 3, :tc], in0=ps[0:1, b2, :tc], scalar1=1.0 / D_MODEL),
        reads=(f"psb{b2}", krows), writes=(krows,))
    S.emit("vector", lambda h: h.tensor_mul(
        out=rows[:, 0, :tc], in0=rows[:, 2, :tc], in1=rows[:, 2, :tc]),
        reads=(krows,), writes=(krows,))
    S.emit("vector", lambda h: h.tensor_sub(
        out=rows[:, 3, :tc], in0=rows[:, 3, :tc], in1=rows[:, 0, :tc]),
        reads=(krows,), writes=(krows,))
    S.emit("scalar", lambda h: h.activation(
        out=rows[:, 1, :tc], in_=rows[:, 3, :tc], func=AF.Sqrt,
        bias=C.cpool[0:1, 0:1]),
        reads=(krows, "cpool"), writes=(krows,))
    S.emit("vector", lambda h: h.reciprocal(
        out=rows[:, 4, :tc], in_=rows[:, 1, :tc]),
        reads=(krows,), writes=(krows,))
    S.emit("vector", lambda h: h.tensor_mul(
        out=rows[:, 5, :tc], in0=rows[:, 2, :tc], in1=rows[:, 4, :tc]),
        reads=(krows,), writes=(krows,))
    S.emit("vector", lambda h: h.tensor_scalar_mul(
        out=rows[:, 5, :tc], in0=rows[:, 5, :tc], scalar1=-1.0),
        reads=(krows,), writes=(krows,))
    S.emit("tensor", lambda h: h.matmul(
        ps[:, ba, :tc], C.ones_row[:], rows[:, 4, :tc], start=True, stop=True),
        reads=(krows, "ones_row"), writes=(f"psb{ba}",))
    S.emit("tensor", lambda h: h.matmul(
        ps[:, bb, :tc], C.ones_row[:], rows[:, 5, :tc], start=True, stop=True),
        reads=(krows, "ones_row"), writes=(f"psb{bb}",))
    S.emit("vector", lambda h: h.tensor_tensor(
        out=x2[:, :, :tc], in0=C.x_res[sl],
        in1=ps.ap()[:, ba:ba + 1, :tc].broadcast_to([128, NDT, tc]), op=ALU.mult),
        reads=("x_res", f"psb{ba}"), writes=(kx2,))
    o0 = t0 if out_t0 is None else out_t0
    S.emit("vector", lambda h: h.tensor_tensor(
        out=xnorm[:, :, o0:o0 + tc], in0=x2[:, :, :tc],
        in1=ps.ap()[:, bb:bb + 1, :tc].broadcast_to([128, NDT, tc]), op=ALU.add),
        reads=(kx2, f"psb{bb}"), writes=(xnk,))
    if keep_ab is not None:
        S.emit("vector", lambda h: h.tensor_copy(
            out=keep_ab[:, 0:2, keep_off:keep_off + tc],
            in_=rows[:, 4:6, :tc]),
            reads=(krows,), writes=(keep_key,))


def emit_mamba1(C, tag, n_mamba, t_sub):
    nc, S = C.nc, C.S
    W = C.dram_w[tag]
    xz_dram = C.dram[f"{tag}_xz"]
    with contextlib.ExitStack() as es:
        ps = es.enter_context(nc.psum_tensor(f"{tag}_psA", [128, 8, 512], F32))
        x2 = es.enter_context(nc.sbuf_tensor(f"{tag}_x2", [128, 8, 512], F32))
        rows = es.enter_context(nc.sbuf_tensor(f"{tag}_rows", [1, 6, 512], F32))
        xnorm = es.enter_context(nc.sbuf_tensor(f"{tag}_xn", [128, 8, T_FULL], BF16))
        wblk = es.enter_context(nc.sbuf_tensor(f"{tag}_wblk", [128, 2, 8, 512], BF16))
        evb = es.enter_context(nc.sbuf_tensor(f"{tag}_evb", [128, 2, 4, 512], BF16))
        if n_mamba > 1:
            xnsub = es.enter_context(nc.sbuf_tensor(f"{tag}_xns", [128, 8, t_sub], BF16))
        for ch in range(4):
            emit_ln(C, xnorm, "xnorm", ch * 512, 512, tag + str(ch), ps, x2, rows)
        if MAMBA_PHASES < 1:
            S.barrier()
            S.flush()
            return
        tcA = min(512, t_sub)
        ntc = max(1, t_sub // 512)
        for m in range(n_mamba):
            if n_mamba > 1:
                S.emit("vector", lambda h, m=m: h.tensor_copy(
                    out=xnsub[:], in_=xnorm[:, :, m::n_mamba]),
                    reads=("xnorm",), writes=("xnsub",))
                xn, xnk = xnsub, "xnsub"
            else:
                xn, xnk = xnorm, "xnorm"
            for ib in range(int(os.environ.get("MAMBA_IB", "8"))):
                wsl = ib % 2
                if True:
                    S.emit("gpsimd", lambda h, m=m, ib=ib, wsl=wsl: h.dma_start(
                        out=wblk[:, wsl, :, :],
                        in_=W["w_inT"][m].rearrange("(kt p) i -> p kt i", p=128)[:, :, ts(ib, 512)]),
                        writes=(f"{tag}_wblk{wsl}",), dma=True)
                if os.environ.get("MAMBA_NOMM"):
                    continue
                for ch in range(ntc):
                    esl = (ib * ntc + ch) % 2
                    for i in range(4):
                        pb = (ib * 4 + i) % 8
                        for kk in range(8):
                            S.emit("tensor", lambda h, ch=ch, i=i, kk=kk, wsl=wsl, pb=pb: h.matmul(
                                ps[:, pb, :tcA], wblk[:, wsl, kk, ts(i)],
                                xn[:, kk, ch * 512:ch * 512 + tcA],
                                start=(kk == 0), stop=(kk == 7)),
                                reads=(f"{tag}_wblk{wsl}", xnk), writes=(f"psb{pb}",))
                        S.emit("scalar", lambda h, i=i, esl=esl, pb=pb: h.copy(
                            out=evb[:, esl, i, :tcA], in_=ps[:, pb, :tcA]),
                            reads=(f"psb{pb}",), writes=(f"{tag}_evb{esl}",))
                    if not os.environ.get("MAMBA_NOSPILL"):
                        S.emit("sync", lambda h, m=m, ib=ib, ch=ch, esl=esl: h.dma_start(
                            out=xz_dram[m, ts(ib, 512), ch * 512:ch * 512 + tcA].rearrange(
                                "(i p) t -> p i t", p=128),
                            in_=evb[:, esl, :, :tcA]),
                            reads=(f"{tag}_evb{esl}",), writes=(f"{tag}_xzd",), dma=True)
        S.barrier()
        S.flush()
    if MAMBA_PHASES < 2:
        return
    tc = 256
    nch = t_sub // tc
    with contextlib.ExitStack() as es:
        alloc_esel(C, es, tag)
        ps = es.enter_context(nc.psum_tensor(f"{tag}_psB", [128, 8, 512], F32))
        wxp = es.enter_context(nc.sbuf_tensor(f"{tag}_wxp", [128, 16, 128], BF16))
        wdt = es.enter_context(nc.sbuf_tensor(f"{tag}_wdt", [64, 2048], BF16))
        wout = es.enter_context(nc.sbuf_tensor(f"{tag}_wout", [128, 16, 512], BF16))
        convw = es.enter_context(nc.sbuf_tensor(f"{tag}_cw", [128, 16, 4], F32))
        convb = es.enter_context(nc.sbuf_tensor(f"{tag}_cb", [128, 16], F32))
        bdt = es.enter_context(nc.sbuf_tensor(f"{tag}_bdt", [128, 16], F32))
        Asb = es.enter_context(nc.sbuf_tensor(f"{tag}_A", [128, 256], F32))
        Dsb = es.enter_context(nc.sbuf_tensor(f"{tag}_Dsk", [128, 16], F32))
        carry = es.enter_context(nc.sbuf_tensor(f"{tag}_carry", [128, 256], F32))
        xh = es.enter_context(nc.sbuf_tensor(f"{tag}_xh", [128, 2, 16, tc + 4], BF16))
        zb = es.enter_context(nc.sbuf_tensor(f"{tag}_z", [128, 2, 16, tc], BF16))
        ub = es.enter_context(nc.sbuf_tensor(f"{tag}_u", [128, 16, tc], BF16))
        delta = es.enter_context(nc.sbuf_tensor(f"{tag}_dl", [128, 16, tc], F32))
        ybuf = es.enter_context(nc.sbuf_tensor(f"{tag}_y", [128, 16, tc], F32))
        bcrB = es.enter_context(nc.sbuf_tensor(f"{tag}_bcrB", [16, tc], F32))
        bcrC = es.enter_context(nc.sbuf_tensor(f"{tag}_bcrC", [16, tc], F32))
        dtr = es.enter_context(nc.sbuf_tensor(f"{tag}_dtr", [64, tc], BF16))
        dub = es.enter_context(nc.sbuf_tensor(f"{tag}_dub", [128, 16, tc], BF16))
        bcBs = es.enter_context(nc.sbuf_tensor(f"{tag}_bcBs", [128, tc], BF16))
        dAs = es.enter_context(nc.sbuf_tensor(f"{tag}_dAs", [128, 4, tc], F32))
        Xs = es.enter_context(nc.sbuf_tensor(f"{tag}_Xs", [128, 4, tc], BF16))
        hs = es.enter_context(nc.sbuf_tensor(f"{tag}_hs", [128, 2, 16, tc], BF16))
        bcCs = es.enter_context(nc.sbuf_tensor(f"{tag}_bcCs", [128, 2, tc], BF16))
        for m in range(n_mamba):
            mk = f"{tag}m"
            for name, key in (("conv_w", "cw"), ("conv_b", "cb"), ("b_dt", "bdt"),
                              ("D", "D"), ("A_log", "A")):
                dst = {"cw": convw, "cb": convb, "bdt": bdt, "D": Dsb, "A": Asb}[key]
                S.emit("sync", lambda h, m=m, name=name, dst=dst: h.dma_start(
                    out=dst[:], in_=W[name][m]), writes=(mk + key,), dma=True)
            S.emit("scalar", lambda h: h.activation(out=Asb[:], in_=Asb[:], func=AF.Exp),
                   reads=(mk + "A",), writes=(mk + "A",))
            S.emit("vector", lambda h: h.tensor_scalar_mul(
                out=Asb[:], in0=Asb[:], scalar1=-1.0),
                reads=(mk + "A",), writes=(mk + "A",))
            S.emit("gpsimd", lambda h, m=m: h.dma_start(
                out=wxp[:], in_=W["w_xprojT"][m].rearrange("(kt p) i -> p kt i", p=128)),
                writes=(mk + "wxp",), dma=True)
            S.emit("gpsimd", lambda h, m=m: h.dma_start(out=wdt[:], in_=W["w_dtT"][m]),
                   writes=(mk + "wdt",), dma=True)
            S.emit("vector", lambda h: h.memset(carry[:], 0.0), writes=(mk + "carry",))
            def load_chunk(ch, m=m):
                t0 = ch * tc
                sl = ch % 2
                xzr = xz_dram[m].rearrange("(j p) t -> p j t", p=128)
                if ch == 0:
                    S.emit("vector", lambda h, sl=sl: h.memset(xh[:, sl, :, 0:3], 0.0),
                           writes=(f"{mk}xh{sl}",))
                    S.emit("sync", lambda h, xzr=xzr, sl=sl: h.dma_start(
                        out=xh[:, sl, :, 3:3 + tc], in_=xzr[:, 0:16, 0:tc]),
                        reads=(f"{tag}_xzd",), writes=(f"{mk}xh{sl}",), dma=True)
                else:
                    S.emit("sync", lambda h, xzr=xzr, t0=t0, sl=sl: h.dma_start(
                        out=xh[:, sl, :, 0:3 + tc], in_=xzr[:, 0:16, t0 - 3:t0 + tc]),
                        reads=(f"{tag}_xzd",), writes=(f"{mk}xh{sl}",), dma=True)
                S.emit("sync", lambda h, xzr=xzr, t0=t0, sl=sl: h.dma_start(
                    out=zb[:, sl, :, :], in_=xzr[:, 16:32, t0:t0 + tc]),
                    reads=(f"{tag}_xzd",), writes=(f"{mk}z{sl}",), dma=True)

            BSTEP = int(os.environ.get("MAMBA_BSTEP", "99"))
            nch_l = min(nch, int(os.environ.get("MAMBA_NCH", "99")))
            load_chunk(0)
            for ch in range(nch_l):
                t0 = ch * tc
                sl = ch % 2
                if ch + 1 < nch_l:
                    load_chunk(ch + 1)
                for j in range(16):
                    S.emit("vector", lambda h, j=j, sl=sl: h.tensor_scalar_mul(
                        out=ybuf[:, j, :], in0=xh[:, sl, j, 0:tc], scalar1=convw[:, j, 0:1]),
                        reads=(f"{mk}xh{sl}", mk + "cw"), writes=(mk + "y",))
                    for k4 in (1, 2, 3):
                        S.emit("vector", lambda h, j=j, k4=k4, sl=sl: h.scalar_tensor_tensor(
                            out=ybuf[:, j, :], in0=xh[:, sl, j, k4:k4 + tc],
                            scalar=convw[:, j, k4:k4 + 1], in1=ybuf[:, j, :],
                            op0=ALU.mult, op1=ALU.add),
                            reads=(f"{mk}xh{sl}", mk + "cw", mk + "y"), writes=(mk + "y",))
                if BSTEP < 3:
                    continue
                for j in range(16):
                    S.emit("scalar", lambda h, j=j: h.activation(
                        out=ub[:, j, :], in_=ybuf[:, j, :], func=AF.Silu,
                        bias=convb[:, j:j + 1]),
                        reads=(mk + "y", mk + "cb"), writes=(mk + "u",))
                if BSTEP < 4:
                    continue
                def _xdbl(h):
                    for kk in range(16):
                        last = h.matmul(ps[:, 7, :tc], wxp[:, kk, :], ub[:, kk, :],
                                        start=(kk == 0), stop=(kk == 15))
                    return last
                S.emit("tensor", _xdbl, reads=(mk + "wxp", mk + "u"),
                       writes=("psb7",))
                S.emit("scalar", lambda h: h.copy(out=bcrB[:, :], in_=ps[0:16, 7, :tc]),
                       reads=("psb7",), writes=(mk + "bcr",))
                S.emit("scalar", lambda h: h.copy(out=bcrC[:, :], in_=ps[32:48, 7, :tc]),
                       reads=("psb7",), writes=(mk + "bcr",))
                S.emit("scalar", lambda h: h.copy(out=dtr[:, :], in_=ps[64:128, 7, :tc]),
                       reads=("psb7",), writes=(mk + "dtr",))
                if BSTEP < 5:
                    continue
                for half in range(2):
                    def _dmm(h, half=half):
                        for jj8 in range(8):
                            last = h.matmul(
                                ps[:, jj8, 0:tc], wdt[:, ts(half * 8 + jj8)],
                                dtr[:, :], start=True, stop=True)
                        return last
                    S.emit("tensor", _dmm, reads=(mk + "dtr", mk + "wdt"),
                           writes=tuple(f"psb{b}" for b in range(8)))

                    def _devac(h, half=half):
                        for jj8 in range(8):
                            last = h.activation(
                                out=delta[:, half * 8 + jj8, :], in_=ps[:, jj8, 0:tc],
                                func=AF.Identity,
                                bias=bdt[:, half * 8 + jj8:half * 8 + jj8 + 1])
                        return last
                    S.emit("scalar", _devac,
                           reads=tuple(f"psb{b}" for b in range(8)) + (mk + "bdt",),
                           writes=(mk + "dl",))
                S.emit("vector", lambda h: h.tensor_scalar_max(
                    out=ybuf[:], in0=delta[:], scalar1=0.0),
                    reads=(mk + "dl",), writes=(mk + "y",))
                S.emit("scalar", lambda h: h.activation(
                    out=delta[:], in_=delta[:], func=AF.Abs),
                    reads=(mk + "dl",), writes=(mk + "dl",))
                S.emit("scalar", lambda h: h.activation(
                    out=delta[:], in_=delta[:], func=AF.Exp, scale=-1.0),
                    reads=(mk + "dl",), writes=(mk + "dl",))
                S.emit("scalar", lambda h: h.activation(
                    out=delta[:], in_=delta[:], func=AF.Ln, bias=1.0),
                    reads=(mk + "dl",), writes=(mk + "dl",))
                S.emit("vector", lambda h: h.tensor_add(
                    out=delta[:], in0=delta[:], in1=ybuf[:]),
                    reads=(mk + "dl", mk + "y"), writes=(mk + "dl",))
                if BSTEP < 6:
                    continue
                for j in range(16):
                    S.emit("vector", lambda h, j=j: h.tensor_scalar_mul(
                        out=ybuf[:, j, :], in0=ub[:, j, :], scalar1=Dsb[:, j:j + 1]),
                        reads=(mk + "u", mk + "D"), writes=(mk + "y",))
                S.emit("vector", lambda h: h.tensor_mul(out=dub[:], in0=delta[:], in1=ub[:]),
                       reads=(mk + "dl", mk + "u"), writes=(mk + "dub",))
                for n in range(min(16, int(os.environ.get("MAMBA_NN", "16")))):
                    bb = n % 2
                    cb = 2 + n % 2
                    S.emit("tensor", lambda h, n=n, bb=bb: h.matmul(
                        ps[:, bb, :tc], C.esel[:, n, :], bcrB[:, :],
                        start=True, stop=True),
                        reads=(mk + "bcr", "esel"), writes=(f"psb{bb}",))
                    S.emit("tensor", lambda h, n=n, cb=cb: h.matmul(
                        ps[:, cb, :tc], C.esel[:, n, :], bcrC[:, :],
                        start=True, stop=True),
                        reads=(mk + "bcr", "esel"), writes=(f"psb{cb}",))
                    par = n % 2
                    S.emit("scalar", lambda h, bb=bb: h.copy(
                        out=bcBs[:], in_=ps[:, bb, :tc]),
                        reads=(f"psb{bb}",), writes=(mk + "bcBs",))
                    S.emit("scalar", lambda h, cb=cb, par=par: h.copy(
                        out=bcCs[:, par, :], in_=ps[:, cb, :tc]),
                        reads=(f"psb{cb}",), writes=(f"{mk}bcCs{par}",))
                    for j in range(16):
                        dsl = j % 4
                        S.emit("scalar", lambda h, n=n, j=j, dsl=dsl: h.activation(
                            out=dAs[:, dsl, :], in_=delta[:, j, :], func=AF.Exp,
                            scale=Asb[:, j * 16 + n:j * 16 + n + 1]),
                            reads=(mk + "dl", mk + "A"), writes=(f"{mk}dA{dsl}",))
                        S.emit("vector", lambda h, j=j, dsl=dsl: h.tensor_tensor(
                            out=Xs[:, dsl, :], in0=dub[:, j, :], in1=bcBs[:],
                            op=ALU.mult),
                            reads=(mk + "dub", mk + "bcBs"), writes=(f"{mk}X{dsl}",))
                        S.emit("vector", lambda h, n=n, j=j, dsl=dsl, par=par: h.tensor_tensor_scan(
                            out=hs[:, par, j, :], data0=dAs[:, dsl, :], data1=Xs[:, dsl, :],
                            initial=carry[:, j * 16 + n:j * 16 + n + 1],
                            op0=ALU.mult, op1=ALU.add),
                            reads=(f"{mk}dA{dsl}", f"{mk}X{dsl}", mk + "carry"),
                            writes=(f"{mk}hp{par}",)),
                    S.emit("vector", lambda h, n=n, par=par: h.tensor_copy(
                        out=carry.ap()[:, n:256:16], in_=hs.ap()[:, par, :, tc - 1]),
                        reads=(f"{mk}hp{par}",),
                        writes=(mk + "carry",))
                    S.emit("vector", lambda h, par=par: h.tensor_tensor(
                        out=hs[:, par, :, :], in0=hs[:, par, :, :],
                        in1=bcCs.ap()[:, par:par + 1, :].broadcast_to([128, 16, tc]),
                        op=ALU.mult),
                        reads=(f"{mk}hp{par}", f"{mk}bcCs{par}"),
                        writes=(f"{mk}hp{par}",))
                    S.emit("gpsimd", lambda h, par=par: h.tensor_add(
                        out=ybuf[:], in0=ybuf[:], in1=hs[:, par, :, :]),
                        reads=(f"{mk}hp{par}", mk + "y"),
                        writes=(mk + "y", f"{mk}hp{par}"))
                S.emit("scalar", lambda h, sl=sl: h.activation(
                    out=zb[:, sl, :, :], in_=zb[:, sl, :, :], func=AF.Silu),
                    reads=(f"{mk}z{sl}",), writes=(f"{mk}z{sl}",))
                S.emit("vector", lambda h, sl=sl: h.tensor_mul(
                    out=ub[:], in0=ybuf[:], in1=zb[:, sl, :, :]),
                    reads=(mk + "y", f"{mk}z{sl}"), writes=(mk + "u",))
                for ihalf in range(2):
                    S.emit("gpsimd", lambda h, m=m, ihalf=ihalf: h.dma_start(
                        out=wout[:], in_=W["w_outT"][m].rearrange(
                            "(kt p) i -> p kt i", p=128)[:, :, ihalf * 512:(ihalf + 1) * 512]),
                        writes=(mk + "wout",), dma=True)
                    for i4 in range(4):
                        i = ihalf * 4 + i4
                        pb = 4 + i % 4

                        def _outp(h, i4=i4, pb=pb):
                            for kk in range(16):
                                last = h.matmul(
                                    ps[:, pb, :tc], wout[:, kk, ts(i4)], ub[:, kk, :],
                                    start=(kk == 0), stop=(kk == 15))
                            return last
                        S.emit("tensor", _outp,
                               reads=(mk + "wout", mk + "u"), writes=(f"psb{pb}",))
                        if n_mamba == 1:
                            xsl = (slice(None), i, slice(t0, t0 + tc))
                        else:
                            xsl = (slice(None), i,
                                   slice(m * t_sub + t0, m * t_sub + t0 + tc))
                        S.emit("vector", lambda h, xsl=xsl, pb=pb: h.tensor_tensor(
                            out=C.x_res[xsl], in0=C.x_res[xsl], in1=ps[:, pb, :tc],
                            op=ALU.add),
                            reads=(f"psb{pb}", "x_res"), writes=("x_res",))
        S.barrier()
        S.flush()


def stage_l0a(C, es):
    emit_mamba1(C, "l0a", 1, T_FULL)


def stage_l0b(C, es):
    emit_mamba1(C, "l0b", 4, 512)



def emit_l0ffn(C, tag):
    import contextlib
    nc, S = C.nc, C.S
    W = C.dram_w["l0f"]
    with contextlib.ExitStack() as es:
        alloc_esel(C, es, tag)
        ps = es.enter_context(nc.psum_tensor(f"{tag}_ps", [128, 8, 512], F32))
        x2 = es.enter_context(nc.sbuf_tensor(f"{tag}_x2", [128, 8, 256], F32))
        rows = es.enter_context(nc.sbuf_tensor(f"{tag}_rows", [1, 6, 256], F32))
        gw = es.enter_context(nc.sbuf_tensor(f"{tag}_gw", [128, 8, 16], F32))
        gbb = es.enter_context(nc.sbuf_tensor(f"{tag}_gbb", [128, 16], F32))
        gbr = es.enter_context(nc.sbuf_tensor(f"{tag}_gbr", [1, 16], F32))
        Gs = es.enter_context(nc.sbuf_tensor(f"{tag}_Gs", [1, 16], F32))
        xn = es.enter_context(nc.sbuf_tensor(f"{tag}_xn", [128, 8, 1024], BF16))
        keep = es.enter_context(nc.sbuf_tensor(f"{tag}_keep", [1, 2, 1024], F32))
        comb = es.enter_context(nc.sbuf_tensor(f"{tag}_comb", [16, 1024], F32))
        yacc = es.enter_context(nc.sbuf_tensor(f"{tag}_yacc", [128, 8, 1024], F32))
        w13b = es.enter_context(nc.sbuf_tensor(f"{tag}_w13", [128, 2, 8, 1024], BF16))
        w2b = es.enter_context(nc.sbuf_tensor(f"{tag}_w2", [128, 2, 4, 1024], BF16))
        gst = es.enter_context(nc.sbuf_tensor(f"{tag}_gst", [128, 512], F32))
        hbf = es.enter_context(nc.sbuf_tensor(f"{tag}_hbf", [128, 4, 512], BF16))
        rt = es.enter_context(nc.sbuf_tensor(f"{tag}_rt", [128, 8, 16], F32))
        rc = es.enter_context(nc.sbuf_tensor(f"{tag}_rc", [128, 4], F32))
        S.emit("sync", lambda h: h.dma_start(
            out=gw[:], in_=W["gate_wT"].rearrange("(kt p) e -> p kt e", p=128),
            allow_slow_non_contiguous=True),
            writes=(tag + "gw",), dma=True)
        S.emit("sync", lambda h: h.dma_start(out=gbr[:], in_=W["gate_b"]),
               writes=(tag + "gbr",), dma=True)
        S.emit("tensor", lambda h: h.matmul(
            ps[:, 0, 0:16], C.ones_row[:], gbr[:], start=True, stop=True),
            reads=(tag + "gbr", "ones_row"), writes=("psb0",))
        S.emit("scalar", lambda h: h.copy(out=gbb[:], in_=ps[:, 0, 0:16]),
               reads=("psb0",), writes=(tag + "gbb",))
        def _colsum(h):
            for kk in range(8):
                last = h.matmul(ps[0:1, 1, 0:16], C.ones_col[:], gw[:, kk, :],
                                start=(kk == 0), stop=(kk == 7))
            return last
        S.emit("tensor", _colsum, reads=(tag + "gw", "ones_col"), writes=("psb1",))
        S.emit("scalar", lambda h: h.copy(out=Gs[:], in_=ps[0:1, 1, 0:16]),
               reads=("psb1",), writes=(tag + "Gs",))
        for mc in range(2):
            t0mc = mc * 1024
            for half in range(4):
                emit_ln(C, xn, tag + "xn", t0mc + half * 256, 256,
                        f"{tag}{mc}{half}", ps, x2, rows,
                        keep_ab=keep, keep_off=half * 256, keep_key=tag + "keep",
                        out_t0=half * 256)
            for tt in range(8):
                gsl = slice(t0mc + tt * 128, t0mc + (tt + 1) * 128)
                lsl = slice(tt * 128, (tt + 1) * 128)
                def _s1(h, gsl=gsl):
                    for kk in range(8):
                        last = h.matmul(ps[:, 4, 0:16], C.x_res[:, kk, gsl],
                                        gw[:, kk, :],
                                        start=(kk == 0), stop=(kk == 7))
                    return last
                S.emit("tensor", _s1, reads=("x_res", tag + "gw"), writes=("psb4",))
                S.emit("scalar", lambda h: h.copy(out=rt[:, 0, :], in_=ps[:, 4, 0:16]),
                       reads=("psb4",), writes=(tag + "rt",))
                S.emit("tensor", lambda h, lsl=lsl: h.matmul(
                    ps[:, 5, 0:1], keep[0:1, 0, lsl], C.ones_row[0:1, 0:1],
                    start=True, stop=True),
                    reads=(tag + "keep", "ones_row"), writes=("psb5",))
                S.emit("scalar", lambda h: h.copy(out=rc[:, 0:1], in_=ps[:, 5, 0:1]),
                       reads=("psb5",), writes=(tag + "rc",))
                S.emit("tensor", lambda h, lsl=lsl: h.matmul(
                    ps[:, 6, 0:16], keep[0:1, 1, lsl], Gs[:],
                    start=True, stop=True),
                    reads=(tag + "keep", tag + "Gs", "ident_f"), writes=("psb6",))
                S.emit("vector", lambda h: h.scalar_tensor_tensor(
                    out=rt[:, 1, :], in0=rt[:, 0, :], scalar=rc[:, 0:1],
                    in1=ps[:, 6, 0:16], op0=ALU.mult, op1=ALU.add),
                    reads=(tag + "rt", tag + "rc", "psb6"), writes=(tag + "rt",))
                S.emit("scalar", lambda h: h.activation(
                    out=rt[:, 2, :], in_=rt[:, 1, :], func=AF.Sigmoid),
                    reads=(tag + "rt",), writes=(tag + "rt",))
                S.emit("vector", lambda h: h.tensor_add(
                    out=rt[:, 3, :], in0=rt[:, 2, :], in1=gbb[:]),
                    reads=(tag + "rt", tag + "gbb"), writes=(tag + "rt",))
                for k4 in range(4):
                    S.emit("vector", lambda h: h.reduce_max(
                        out=rc[:, 1:2], in_=rt[:, 3, :], axis=AX.X),
                        reads=(tag + "rt",), writes=(tag + "rc",))
                    S.emit("vector", lambda h: h.tensor_scalar(
                        out=rt[:, 5, :], in0=rt[:, 3, :], scalar1=rc[:, 1:2],
                        scalar2=None, op0=ALU.is_equal),
                        reads=(tag + "rt", tag + "rc"), writes=(tag + "rt",))
                    if k4 == 0:
                        S.emit("vector", lambda h: h.tensor_copy(
                            out=rt[:, 4, :], in_=rt[:, 5, :]),
                            reads=(tag + "rt",), writes=(tag + "rt",))
                    else:
                        S.emit("vector", lambda h: h.tensor_add(
                            out=rt[:, 4, :], in0=rt[:, 4, :], in1=rt[:, 5, :]),
                            reads=(tag + "rt",), writes=(tag + "rt",))
                    S.emit("vector", lambda h: h.scalar_tensor_tensor(
                        out=rt[:, 3, :], in0=rt[:, 5, :], scalar=-1e9,
                        in1=rt[:, 3, :], op0=ALU.mult, op1=ALU.add),
                        reads=(tag + "rt",), writes=(tag + "rt",))
                S.emit("vector", lambda h: h.tensor_mul(
                    out=rt[:, 6, :], in0=rt[:, 4, :], in1=rt[:, 2, :]),
                    reads=(tag + "rt",), writes=(tag + "rt",))
                S.emit("vector", lambda h: h.reduce_sum(
                    out=rc[:, 2:3], in_=rt[:, 6, :], axis=AX.X),
                    reads=(tag + "rt",), writes=(tag + "rc",))
                S.emit("vector", lambda h: h.reciprocal(
                    out=rc[:, 3:4], in_=rc[:, 2:3]),
                    reads=(tag + "rc",), writes=(tag + "rc",))
                S.emit("vector", lambda h: h.tensor_scalar(
                    out=rt[:, 7, :], in0=rt[:, 6, :], scalar1=rc[:, 3:4],
                    scalar2=2.5, op0=ALU.mult, op1=ALU.mult),
                    reads=(tag + "rt", tag + "rc"), writes=(tag + "rt",))
                S.emit("tensor", lambda h: h.transpose(
                    ps[0:16, 7, 0:128], rt[:, 7, :], C.ident_f[:]),
                    reads=(tag + "rt", "ident_f"), writes=("psb7",))
                S.emit("scalar", lambda h, lsl=lsl: h.copy(
                    out=comb[:, lsl], in_=ps[0:16, 7, 0:128]),
                    reads=("psb7",), writes=(tag + "comb",))
            for ei in range(17):
                shared = ei == 0
                e = ei - 1
                wsl = ei % 2
                if shared:
                    w13src = W["w13sT"]
                    w2src = W["w2sT"]
                else:
                    w13src = W["w13T"][e]
                    w2src = W["w2T"][e]
                S.emit("gpsimd", lambda h, w13src=w13src, wsl=wsl: h.dma_start(
                    out=w13b[:, wsl, :, :],
                    in_=w13src.rearrange("(kt p) f -> p kt f", p=128)),
                    writes=(f"{tag}w13{wsl}",), dma=True)
                S.emit("gpsimd", lambda h, w2src=w2src, wsl=wsl: h.dma_start(
                    out=w2b[:, wsl, :, :],
                    in_=w2src.rearrange("(kt p) f -> p kt f", p=128)),
                    writes=(f"{tag}w2{wsl}",), dma=True)
                for sub in range(2):
                    ts0 = sub * 512
                    if not shared:
                        S.emit("tensor", lambda h, e=e, ts0=ts0: h.matmul(
                            ps[:, 6, :], C.esel[:, e, :], comb[:, ts0:ts0 + 512],
                            start=True, stop=True),
                            reads=(tag + "comb", "esel"), writes=("psb6",))
                    for ft in range(4):
                        gb = (4, 0)[ft % 2]
                        ub_ = (5, 1)[ft % 2]

                        def _gmm(h, ft=ft, wsl=wsl, ts0=ts0, gb=gb):
                            for kk in range(8):
                                last = h.matmul(
                                    ps[:, gb, :], w13b[:, wsl, kk, ts(ft)],
                                    xn[:, kk, ts0:ts0 + 512],
                                    start=(kk == 0), stop=(kk == 7))
                            return last
                        S.emit("tensor", _gmm,
                               reads=(f"{tag}w13{wsl}", tag + "xn"), writes=(f"psb{gb}",))

                        def _umm(h, ft=ft, wsl=wsl, ts0=ts0, ub_=ub_):
                            for kk in range(8):
                                last = h.matmul(
                                    ps[:, ub_, :],
                                    w13b[:, wsl, kk, 512 + ft * 128:512 + (ft + 1) * 128],
                                    xn[:, kk, ts0:ts0 + 512],
                                    start=(kk == 0), stop=(kk == 7))
                            return last
                        S.emit("tensor", _umm,
                               reads=(f"{tag}w13{wsl}", tag + "xn"), writes=(f"psb{ub_}",))
                        S.emit("scalar", lambda h, gb=gb: h.activation(
                            out=gst[:], in_=ps[:, gb, :], func=AF.Silu),
                            reads=(f"psb{gb}",), writes=(tag + "gst",))
                        S.emit("vector", lambda h, ft=ft, ub_=ub_: h.tensor_tensor(
                            out=hbf[:, ft, :], in0=gst[:], in1=ps[:, ub_, :],
                            op=ALU.mult),
                            reads=(tag + "gst", f"psb{ub_}"), writes=(tag + "hbf",))
                        if not shared:
                            S.emit("vector", lambda h, ft=ft: h.tensor_tensor(
                                out=hbf[:, ft, :], in0=hbf[:, ft, :], in1=ps[:, 6, :],
                                op=ALU.mult),
                                reads=(tag + "hbf", "psb6"), writes=(tag + "hbf",))
                    for i in range(8):
                        mb2 = (7, 3)[i % 2]

                        def _mm2(h, i=i, wsl=wsl, mb2=mb2):
                            for kk in range(4):
                                last = h.matmul(
                                    ps[:, mb2, :], w2b[:, wsl, kk, ts(i)], hbf[:, kk, :],
                                    start=(kk == 0), stop=(kk == 3))
                            return last
                        S.emit("tensor", _mm2,
                               reads=(f"{tag}w2{wsl}", tag + "hbf"), writes=(f"psb{mb2}",))
                        if shared:
                            S.emit("vector", lambda h, i=i, ts0=ts0, mb2=mb2: h.tensor_copy(
                                out=yacc[:, i, ts0:ts0 + 512], in_=ps[:, mb2, :]),
                                reads=(f"psb{mb2}",), writes=(tag + "yacc",))
                        else:
                            S.emit("vector", lambda h, i=i, ts0=ts0, mb2=mb2: h.tensor_add(
                                out=yacc[:, i, ts0:ts0 + 512],
                                in0=yacc[:, i, ts0:ts0 + 512], in1=ps[:, mb2, :]),
                                reads=(tag + "yacc", f"psb{mb2}"), writes=(tag + "yacc",))
            for i in range(8):
                S.emit("vector", lambda h, i=i, t0mc=t0mc: h.tensor_add(
                    out=C.x_res[:, i, t0mc:t0mc + 1024],
                    in0=C.x_res[:, i, t0mc:t0mc + 1024], in1=yacc[:, i, :]),
                    reads=(tag + "yacc", "x_res"), writes=("x_res",))
        S.barrier()
        S.flush()



def emit_l1ffn(C, tag):
    import contextlib
    nc, S = C.nc, C.S
    W = C.dram_w["l1f"]
    with contextlib.ExitStack() as es:
        alloc_esel(C, es, tag)
        ps = es.enter_context(nc.psum_tensor(f"{tag}_ps", [128, 8, 512], F32))
        x2 = es.enter_context(nc.sbuf_tensor(f"{tag}_x2", [128, 8, 128], F32))
        rows = es.enter_context(nc.sbuf_tensor(f"{tag}_rows", [1, 6, 128], F32))
        gw = es.enter_context(nc.sbuf_tensor(f"{tag}_gw", [128, 8, 8], F32))
        gbr = es.enter_context(nc.sbuf_tensor(f"{tag}_gbr", [1, 8], F32))
        Gs = es.enter_context(nc.sbuf_tensor(f"{tag}_Gs", [1, 8], F32))
        b1s = es.enter_context(nc.sbuf_tensor(f"{tag}_b1", [128, 32, 8], F32))
        b2s = es.enter_context(nc.sbuf_tensor(f"{tag}_b2", [8, 1024], F32))
        xn = es.enter_context(nc.sbuf_tensor(f"{tag}_xn", [128, 8, 1024], BF16))
        keep = es.enter_context(nc.sbuf_tensor(f"{tag}_keep", [1, 2, 1024], F32))
        cfm = es.enter_context(nc.sbuf_tensor(f"{tag}_cfm", [8, 1024], F32))
        yacc = es.enter_context(nc.sbuf_tensor(f"{tag}_yacc", [128, 8, 1024], F32))
        wb = es.enter_context(nc.sbuf_tensor(f"{tag}_wb", [128, 8, 1024], BF16))
        wb2 = es.enter_context(nc.sbuf_tensor(f"{tag}_wb2", [128, 8, 1024], BF16))
        hblk = es.enter_context(nc.sbuf_tensor(f"{tag}_hblk", [128, 8, 1024], BF16))
        bcs = es.enter_context(nc.sbuf_tensor(f"{tag}_bcs", [128, 1024], BF16))
        rt = es.enter_context(nc.sbuf_tensor(f"{tag}_rt", [128, 8, 8], F32))
        rc = es.enter_context(nc.sbuf_tensor(f"{tag}_rc", [128, 4], F32))
        S.emit("sync", lambda h: h.dma_start(
            out=gw[:], in_=W["w_gate"].rearrange("(kt p) e -> p kt e", p=128),
            allow_slow_non_contiguous=True),
            writes=(tag + "gw",), dma=True)
        S.emit("sync", lambda h: h.dma_start(out=gbr[:], in_=W["b_gate"]),
               writes=(tag + "gbr",), dma=True)
        S.emit("sync", lambda h: h.dma_start(
            out=b1s[:], in_=W["b1p"]), writes=(tag + "b1",), dma=True)
        S.emit("sync", lambda h: h.dma_start(out=b2s[:], in_=W["b2"]),
               writes=(tag + "b2",), dma=True)
        def _colsum(h):
            for kk in range(8):
                last = h.matmul(ps[0:1, 1, 0:8], C.ones_col[:], gw[:, kk, :],
                                start=(kk == 0), stop=(kk == 7))
            return last
        S.emit("tensor", _colsum, reads=(tag + "gw", "ones_col"), writes=("psb1",))
        S.emit("scalar", lambda h: h.copy(out=Gs[:], in_=ps[0:1, 1, 0:8]),
               reads=("psb1",), writes=(tag + "Gs",))
        for mc in range(2):
            t0mc = mc * 1024
            for q in range(8):
                emit_ln(C, xn, tag + "xn", t0mc + q * 128, 128,
                        f"{tag}{mc}{q}", ps, x2, rows,
                        keep_ab=keep, keep_off=q * 128, keep_key=tag + "keep",
                        out_t0=q * 128)
            for tt in range(8):
                gsl = slice(t0mc + tt * 128, t0mc + (tt + 1) * 128)
                lsl = slice(tt * 128, (tt + 1) * 128)
                def _s1(h, gsl=gsl):
                    for kk in range(8):
                        last = h.matmul(ps[:, 4, 0:8], C.x_res[:, kk, gsl],
                                        gw[:, kk, :],
                                        start=(kk == 0), stop=(kk == 7))
                    return last
                S.emit("tensor", _s1, reads=("x_res", tag + "gw"), writes=("psb4",))
                S.emit("scalar", lambda h: h.copy(out=rt[:, 0, :], in_=ps[:, 4, 0:8]),
                       reads=("psb4",), writes=(tag + "rt",))
                S.emit("tensor", lambda h, lsl=lsl: h.matmul(
                    ps[:, 5, 0:1], keep[0:1, 0, lsl], C.ones_row[0:1, 0:1],
                    start=True, stop=True),
                    reads=(tag + "keep", "ones_row"), writes=("psb5",))
                S.emit("scalar", lambda h: h.copy(out=rc[:, 0:1], in_=ps[:, 5, 0:1]),
                       reads=("psb5",), writes=(tag + "rc",))
                S.emit("tensor", lambda h, lsl=lsl: h.matmul(
                    ps[:, 6, 0:8], keep[0:1, 1, lsl], Gs[:],
                    start=True, stop=False),
                    reads=(tag + "keep", tag + "Gs"), writes=("psb6",))
                S.emit("tensor", lambda h: h.matmul(
                    ps[:, 6, 0:8], C.ones_row[:], gbr[:],
                    start=False, stop=True),
                    reads=(tag + "gbr", "ones_row"), writes=("psb6",))
                S.emit("vector", lambda h: h.scalar_tensor_tensor(
                    out=rt[:, 1, :], in0=rt[:, 0, :], scalar=rc[:, 0:1],
                    in1=ps[:, 6, 0:8], op0=ALU.mult, op1=ALU.add),
                    reads=(tag + "rt", tag + "rc", "psb6"), writes=(tag + "rt",))
                S.emit("vector", lambda h: h.reduce_max(
                    out=rc[:, 1:2], in_=rt[:, 1, :], axis=AX.X, negate=True),
                    reads=(tag + "rt",), writes=(tag + "rc",))
                S.emit("scalar", lambda h: h.activation(
                    out=rt[:, 2, :], in_=rt[:, 1, :], func=AF.Exp,
                    bias=rc[:, 1:2]),
                    reads=(tag + "rt", tag + "rc"), writes=(tag + "rt",))
                S.emit("vector", lambda h: h.reduce_sum(
                    out=rc[:, 2:3], in_=rt[:, 2, :], axis=AX.X),
                    reads=(tag + "rt",), writes=(tag + "rc",))
                S.emit("vector", lambda h: h.reciprocal(
                    out=rc[:, 3:4], in_=rc[:, 2:3]),
                    reads=(tag + "rc",), writes=(tag + "rc",))
                S.emit("vector", lambda h: h.tensor_scalar_mul(
                    out=rt[:, 3, :], in0=rt[:, 2, :], scalar1=rc[:, 3:4]),
                    reads=(tag + "rt", tag + "rc"), writes=(tag + "rt",))
                S.emit("vector", lambda h: h.reduce_max(
                    out=rc[:, 1:2], in_=rt[:, 3, :], axis=AX.X),
                    reads=(tag + "rt", tag + "rc"), writes=(tag + "rc",))
                S.emit("vector", lambda h: h.tensor_scalar(
                    out=rt[:, 4, :], in0=rt[:, 3, :], scalar1=rc[:, 1:2],
                    scalar2=None, op0=ALU.is_equal),
                    reads=(tag + "rt", tag + "rc"), writes=(tag + "rt",))
                S.emit("vector", lambda h: h.tensor_scalar_mul(
                    out=rt[:, 5, :], in0=rt[:, 4, :], scalar1=rc[:, 1:2]),
                    reads=(tag + "rt", tag + "rc"), writes=(tag + "rt",))
                S.emit("tensor", lambda h: h.transpose(
                    ps[0:8, 7, 0:128], rt[:, 5, :], C.ident_f[:]),
                    reads=(tag + "rt", "ident_f"), writes=("psb7",))
                S.emit("scalar", lambda h, lsl=lsl: h.copy(
                    out=cfm[:, lsl], in_=ps[0:8, 7, 0:128]),
                    reads=("psb7",), writes=(tag + "cfm",))
            for i in range(8):
                for half in range(2):
                    hsl = slice(half * 512, (half + 1) * 512)
                    S.emit("tensor", lambda h, i=i, hsl=hsl: h.matmul(
                        ps[:, 2, :], b2s[:, ts(i)], cfm[:, hsl],
                        start=True, stop=True),
                        reads=(tag + "b2", tag + "cfm"), writes=("psb2",))
                    S.emit("vector", lambda h, i=i, hsl=hsl: h.tensor_copy(
                        out=yacc[:, i, hsl], in_=ps[:, 2, :]),
                        reads=("psb2",), writes=(tag + "yacc",))
            for e in range(8):
                for half in range(2):
                    hsl = slice(half * 512, (half + 1) * 512)
                    S.emit("tensor", lambda h, e=e, hsl=hsl, half=half: h.matmul(
                        ps[:, 2 + half, :], C.esel[0:8, e, :], cfm[:, hsl],
                        start=True, stop=True),
                        reads=(tag + "cfm", "esel"), writes=(f"psb{2 + half}",))
                    S.emit("scalar", lambda h, hsl=hsl, half=half: h.copy(
                        out=bcs[:, hsl], in_=ps[:, 2 + half, :]),
                        reads=(f"psb{2 + half}",), writes=(tag + "bcs",))
                for mb in range(4):
                    S.emit("gpsimd", lambda h, e=e, mb=mb: h.dma_start(
                        out=wb[:], in_=W["W1"][e].rearrange(
                            "(kt p) m -> p kt m", p=128)[:, :, ts(mb, 1024)]),
                        writes=(tag + "wb",), dma=True)
                    S.emit("gpsimd", lambda h, e=e, mb=mb: h.dma_start(
                        out=wb2[:], in_=W["W2"][e].rearrange(
                            "(kt p) d -> p kt d", p=128, kt=32)[:, mb * 8:(mb + 1) * 8, :]),
                        writes=(tag + "wb2",), dma=True)
                    for mt in range(8):
                        for half in range(2):
                            hsl = slice(half * 512, (half + 1) * 512)
                            pb = (2 * mt + half) % 3

                            def _mm1(h, mt=mt, hsl=hsl, pb=pb):
                                for kk in range(8):
                                    last = h.matmul(
                                        ps[:, pb, :], wb[:, kk, ts(mt)],
                                        xn[:, kk, hsl],
                                        start=(kk == 0), stop=(kk == 7))
                                return last
                            S.emit("tensor", _mm1,
                                   reads=(tag + "wb", tag + "xn"),
                                   writes=(f"psb{pb}",))
                            S.emit("scalar", lambda h, mt=mt, mb=mb, e=e, hsl=hsl, pb=pb: h.activation(
                                out=hblk[:, mt, hsl], in_=ps[:, pb, :], func=AF.Relu,
                                bias=b1s[:, mb * 8 + mt, e:e + 1]),
                                reads=(f"psb{pb}", tag + "b1"), writes=(tag + "hblk",))
                        S.emit("vector", lambda h, mt=mt: h.tensor_tensor(
                            out=hblk[:, mt, :], in0=hblk[:, mt, :], in1=bcs[:],
                            op=ALU.mult),
                            reads=(tag + "hblk", tag + "bcs"), writes=(tag + "hblk",))
                    for i in range(8):
                        for half in range(2):
                            hsl = slice(half * 512, (half + 1) * 512)
                            pb = 4 + (2 * i + half) % 3

                            def _mm2(h, i=i, hsl=hsl, pb=pb):
                                for kk in range(8):
                                    last = h.matmul(
                                        ps[:, pb, :], wb2[:, kk, ts(i)],
                                        hblk[:, kk, hsl],
                                        start=(kk == 0), stop=(kk == 7))
                                return last
                            S.emit("tensor", _mm2,
                                   reads=(tag + "wb2", tag + "hblk"),
                                   writes=(f"psb{pb}",))
                            S.emit("vector", lambda h, i=i, hsl=hsl, pb=pb: h.tensor_add(
                                out=yacc[:, i, hsl], in0=yacc[:, i, hsl],
                                in1=ps[:, pb, :]),
                                reads=(tag + "yacc", f"psb{pb}"), writes=(tag + "yacc",))
            for i in range(8):
                S.emit("vector", lambda h, i=i, t0mc=t0mc: h.tensor_add(
                    out=C.x_res[:, i, t0mc:t0mc + 1024],
                    in0=C.x_res[:, i, t0mc:t0mc + 1024], in1=yacc[:, i, :]),
                    reads=(tag + "yacc", "x_res"), writes=("x_res",))
        S.barrier()
        S.flush()



def emit_l1a(C, tag="l1a"):
    import contextlib
    nc, S = C.nc, C.S
    W = C.dram_w["l1a"]
    p_dram = C.dram["l1a_p"]
    r_dram = C.dram["l1a_r"]
    NIT = 41
    with contextlib.ExitStack() as es:
        ps = es.enter_context(nc.psum_tensor(f"{tag}_psA", [128, 8, 512], F32))
        x2 = es.enter_context(nc.sbuf_tensor(f"{tag}_x2", [128, 8, 512], F32))
        rows = es.enter_context(nc.sbuf_tensor(f"{tag}_rows", [1, 6, 512], F32))
        xnorm = es.enter_context(nc.sbuf_tensor(f"{tag}_xn", [128, 8, T_FULL], BF16))
        wblk = es.enter_context(nc.sbuf_tensor(f"{tag}_wblk", [128, 2, 8, 512], BF16))
        evb = es.enter_context(nc.sbuf_tensor(f"{tag}_evb", [128, 2, 4, 512], BF16))
        for ch in range(4):
            emit_ln(C, xnorm, "xnorm", ch * 512, 512, tag + str(ch), ps, x2, rows)
        for proj, src_w, n_i, dst in (("p", W["w_inT"], NIT, p_dram),
                                      ("r", W["w_resT"], 16, r_dram)):
            nblk = (n_i + 3) // 4
            for ib in range(nblk):
                i0 = ib * 4
                ni = min(4, n_i - i0)
                wsl = ib % 2
                S.emit("gpsimd", lambda h, src_w=src_w, i0=i0, ni=ni, wsl=wsl: h.dma_start(
                    out=wblk[:, wsl, :, 0:ni * 128],
                    in_=src_w.rearrange("(kt p) i -> p kt i", p=128)[:, :, i0 * 128:(i0 + ni) * 128]),
                    writes=(f"{tag}_wblk{wsl}",), dma=True)
                for ch in range(4):
                    esl = (ib * 4 + ch) % 2
                    for i in range(ni):
                        pb = (ib * 4 + i) % 8

                        def _inp(h, ch=ch, i=i, wsl=wsl, pb=pb):
                            for kk in range(8):
                                last = h.matmul(
                                    ps[:, pb, :], wblk[:, wsl, kk, ts(i)],
                                    xnorm[:, kk, ch * 512:(ch + 1) * 512],
                                    start=(kk == 0), stop=(kk == 7))
                            return last
                        S.emit("tensor", _inp,
                               reads=(f"{tag}_wblk{wsl}", "xnorm"),
                               writes=(f"psb{pb}",))
                        S.emit("scalar", lambda h, i=i, esl=esl, pb=pb: h.copy(
                            out=evb[:, esl, i, :], in_=ps[:, pb, :]),
                            reads=(f"psb{pb}",), writes=(f"{tag}_evb{esl}",))
                    S.emit("sync", lambda h, dst=dst, i0=i0, ni=ni, ch=ch, esl=esl: h.dma_start(
                        out=dst[i0 * 128:(i0 + ni) * 128, ch * 512:(ch + 1) * 512].rearrange(
                            "(i p) t -> p i t", p=128),
                        in_=evb[:, esl, 0:ni, :]),
                        reads=(f"{tag}_evb{esl}",), writes=(f"{tag}_{proj}d",), dma=True)
        S.barrier()
        S.flush()
    S.barrier()
    tc = 256
    with contextlib.ExitStack() as es:
        alloc_esel(C, es, tag)
        ps = es.enter_context(nc.psum_tensor(f"{tag}_psB", [128, 8, 512], F32))
        wout = es.enter_context(nc.sbuf_tensor(f"{tag}_wout", [128, 16, 512], BF16))
        wdt = es.enter_context(nc.sbuf_tensor(f"{tag}_wdt", [64, 8], BF16))
        convw = es.enter_context(nc.sbuf_tensor(f"{tag}_cw", [128, 16, 4], F32))
        convb = es.enter_context(nc.sbuf_tensor(f"{tag}_cb", [128, 16], F32))
        dtb = es.enter_context(nc.sbuf_tensor(f"{tag}_dtb", [8, 1], F32))
        Afl = es.enter_context(nc.sbuf_tensor(f"{tag}_Afl", [1, 128], F32))
        Abc = es.enter_context(nc.sbuf_tensor(f"{tag}_Abc", [128, 128], F32))
        Dsk = es.enter_context(nc.sbuf_tensor(f"{tag}_Dsk", [128, 16], F32))
        nrmw = es.enter_context(nc.sbuf_tensor(f"{tag}_nw", [128, 16], F32))
        carry = es.enter_context(nc.sbuf_tensor(f"{tag}_carry", [128, 256], F32))
        xh = es.enter_context(nc.sbuf_tensor(f"{tag}_xh", [128, 16, tc + 4], BF16))
        ub = es.enter_context(nc.sbuf_tensor(f"{tag}_u", [128, 16, tc], F32))
        dts = es.enter_context(nc.sbuf_tensor(f"{tag}_dts", [8, tc], F32))
        dtr = es.enter_context(nc.sbuf_tensor(f"{tag}_dtr", [64, tc], BF16))
        Bf = es.enter_context(nc.sbuf_tensor(f"{tag}_Bf", [16, 8, tc], BF16))
        eselb = es.enter_context(nc.sbuf_tensor(f"{tag}_eselb", [16, 16, 128], BF16))
        Cf = es.enter_context(nc.sbuf_tensor(f"{tag}_Cf", [16, 8, tc], BF16))
        du = es.enter_context(nc.sbuf_tensor(f"{tag}_du", [128, 16, tc], F32))
        ybuf = es.enter_context(nc.sbuf_tensor(f"{tag}_y", [128, 16, tc], F32))
        zb = es.enter_context(nc.sbuf_tensor(f"{tag}_z", [128, 16, tc], BF16))
        rbuf = es.enter_context(nc.sbuf_tensor(f"{tag}_rin", [128, 16, tc], BF16))
        gnb = es.enter_context(nc.sbuf_tensor(f"{tag}_gn", [128, 16, tc], BF16))
        dAs = es.enter_context(nc.sbuf_tensor(f"{tag}_dAs", [128, 2, tc], F32))
        Xs = es.enter_context(nc.sbuf_tensor(f"{tag}_Xs", [128, 2, tc], BF16))
        hs = es.enter_context(nc.sbuf_tensor(f"{tag}_hs", [128, 2, 2, tc], F32))
        dubuf = es.enter_context(nc.sbuf_tensor(f"{tag}_dub", [128, 16, tc], BF16))
        bcBs = es.enter_context(nc.sbuf_tensor(f"{tag}_bcBs", [128, 2, tc], BF16))
        bcCs = es.enter_context(nc.sbuf_tensor(f"{tag}_bcCs", [128, 2, tc], BF16))
        rrow = es.enter_context(nc.sbuf_tensor(f"{tag}_rrow", [1, 2, tc], F32))
        mk = tag + "m"
        for name, dst, key in (("conv_w", convw, "cw"), ("conv_b", convb, "cb"),
                               ("dt_bias", dtb, "dtb"), ("A_flat", Afl, "Afl"),
                               ("Dskip", Dsk, "D"), ("norm_w", nrmw, "nw")):
            S.emit("sync", lambda h, name=name, dst=dst: h.dma_start(
                out=dst[:], in_=W[name]), writes=(mk + key,), dma=True)
        S.emit("gpsimd", lambda h: h.dma_start(out=wdt[:], in_=W["w_dtT"]),
               writes=(mk + "wdt",), dma=True)
        S.emit("scalar", lambda h: h.activation(out=Afl[:], in_=Afl[:], func=AF.Exp),
               reads=(mk + "Afl",), writes=(mk + "Afl",))
        S.emit("vector", lambda h: h.tensor_scalar_mul(
            out=Afl[:], in0=Afl[:], scalar1=-1.0),
            reads=(mk + "Afl",), writes=(mk + "Afl",))
        S.emit("tensor", lambda h: h.matmul(
            ps[:, 0, 0:128], C.ones_row[:], Afl[:], start=True, stop=True),
            reads=(mk + "Afl", "ones_row"), writes=("psb0",))
        S.emit("scalar", lambda h: h.copy(out=Abc[:], in_=ps[:, 0, 0:128]),
               reads=("psb0",), writes=(mk + "Abc",))
        S.emit("vector", lambda h: h.memset(carry[:], 0.0), writes=(mk + "carry",))
        S.emit("scalar", lambda h: h.copy(out=eselb[:], in_=C.esel[:]),
               reads=("esel",), writes=(mk + "eselb",))
        pr = p_dram.rearrange("(j p) t -> p j t", p=128)
        rr = r_dram.rearrange("(j p) t -> p j t", p=128)
        for ch in range(8):
            t0 = ch * tc
            if ch == 0:
                S.emit("vector", lambda h: h.memset(xh[:, :, 0:3], 0.0),
                       writes=(mk + "xh",))
                S.emit("sync", lambda h: h.dma_start(
                    out=xh[:, :, 3:3 + tc], in_=pr[:, 16:32, 0:tc]),
                    reads=(f"{tag}_pd",), writes=(mk + "xh",), dma=True)
            else:
                S.emit("sync", lambda h, t0=t0: h.dma_start(
                    out=xh[:, :, 0:3 + tc], in_=pr[:, 16:32, t0 - 3:t0 + tc]),
                    reads=(f"{tag}_pd",), writes=(mk + "xh",), dma=True)
            S.emit("sync", lambda h, t0=t0: h.dma_start(
                out=dtr[:], in_=pr[0:64, 32, t0:t0 + tc]),
                reads=(f"{tag}_pd",), writes=(mk + "dtr",), dma=True)
            S.emit("sync", lambda h, t0=t0: h.dma_start(
                out=gnb[:, 0:4, :], in_=pr[:, 33:37, t0:t0 + tc]),
                reads=(f"{tag}_pd",), writes=(mk + "gn",), dma=True)
            S.emit("sync", lambda h, t0=t0: h.dma_start(
                out=gnb[:, 4:8, :], in_=pr[:, 37:41, t0:t0 + tc]),
                reads=(f"{tag}_pd",), writes=(mk + "gn",), dma=True)
            for j in range(16):
                S.emit("vector", lambda h, j=j: h.tensor_scalar(
                    out=ub[:, j, :], in0=xh[:, j, 0:tc], scalar1=convw[:, j, 0:1],
                    scalar2=convb[:, j:j + 1], op0=ALU.mult, op1=ALU.add),
                    reads=(mk + "xh", mk + "cw", mk + "cb"), writes=(mk + "u",))
                for k4 in (1, 2, 3):
                    S.emit("vector", lambda h, j=j, k4=k4: h.scalar_tensor_tensor(
                        out=ub[:, j, :], in0=xh[:, j, k4:k4 + tc],
                        scalar=convw[:, j, k4:k4 + 1], in1=ub[:, j, :],
                        op0=ALU.mult, op1=ALU.add),
                        reads=(mk + "xh", mk + "cw", mk + "u"), writes=(mk + "u",))
            for g in range(8):
                S.emit("scalar", lambda h, g=g: h.copy(
                    out=Bf[:, g, :], in_=gnb[64 * (g % 2):64 * (g % 2) + 16, g // 2, :]),
                    reads=(mk + "gn",), writes=(mk + "Bf",))
                S.emit("scalar", lambda h, g=g: h.copy(
                    out=Cf[:, g, :], in_=gnb[64 * (g % 2):64 * (g % 2) + 16, 4 + g // 2, :]),
                    reads=(mk + "gn",), writes=(mk + "Cf",))
            S.emit("tensor", lambda h: h.matmul(
                ps[0:8, 6, :tc], wdt[:, :], dtr[:],
                start=True, stop=True),
                reads=(mk + "dtr", mk + "wdt"), writes=("psb6",))
            S.emit("scalar", lambda h: h.activation(
                out=dts[:, :],
                in_=ps[0:8, 6, :tc], func=AF.Identity, bias=dtb[:, 0:1]),
                reads=("psb6", mk + "dtb"), writes=(mk + "dts",))
            S.emit("vector", lambda h: h.tensor_scalar_max(
                out=du[0:8, 0, :], in0=dts[:, :], scalar1=0.0),
                reads=(mk + "dts",), writes=(mk + "duscr",))
            S.emit("scalar", lambda h: h.activation(
                out=dts[:, :], in_=dts[:, :], func=AF.Abs),
                reads=(mk + "dts",), writes=(mk + "dts",))
            S.emit("scalar", lambda h: h.activation(
                out=dts[:, :], in_=dts[:, :], func=AF.Exp, scale=-1.0),
                reads=(mk + "dts",), writes=(mk + "dts",))
            S.emit("scalar", lambda h: h.activation(
                out=dts[:, :], in_=dts[:, :], func=AF.Ln, bias=1.0),
                reads=(mk + "dts",), writes=(mk + "dts",))
            S.emit("vector", lambda h: h.tensor_add(
                out=dts[:, :], in0=dts[:, :], in1=du[0:8, 0, :]),
                reads=(mk + "dts", mk + "duscr"), writes=(mk + "dts",))
            S.emit("vector", lambda h: h.tensor_scalar(
                out=dts[:, :], in0=dts[:, :], scalar1=1e-4, scalar2=1.0,
                op0=ALU.max, op1=ALU.min),
                reads=(mk + "dts",), writes=(mk + "dts",))
            for j in range(16):
                S.emit("vector", lambda h, j=j: h.tensor_scalar_mul(
                    out=ybuf[:, j, :], in0=ub[:, j, :], scalar1=Dsk[:, j:j + 1]),
                    reads=(mk + "u", mk + "D"), writes=(mk + "y",))
            for hh in range(8):
                hb = hh % 2
                S.emit("tensor", lambda h, hh=hh, hb=hb: h.matmul(
                    ps[:, hb, :tc], C.esel[0:8, hh, :], dts[:, :],
                    start=True, stop=True),
                    reads=(mk + "dts", "esel"), writes=(f"psb{hb}",))
                for jo in range(2):
                    j = 2 * hh + jo
                    S.emit("vector", lambda h, j=j, hb=hb: h.tensor_tensor(
                        out=dubuf[:, j, :], in0=ub[:, j, :], in1=ps[:, hb, :tc],
                        op=ALU.mult),
                        reads=(mk + "u", f"psb{hb}"), writes=(mk + "dub",))
                for n in range(16):
                    bb = 2 + n % 2
                    cb2 = 4 + n % 2
                    par = n % 2
                    S.emit("tensor", lambda h, hh=hh, n=n, bb=bb: h.matmul(
                        ps[:, bb, :tc], eselb[:, n, :],
                        Bf[:, hh, :],
                        start=True, stop=True),
                        reads=(mk + "Bf", mk + "eselb"), writes=(f"psb{bb}",))
                    S.emit("tensor", lambda h, hh=hh, n=n, cb2=cb2: h.matmul(
                        ps[:, cb2, :tc], eselb[:, n, :],
                        Cf[:, hh, :],
                        start=True, stop=True),
                        reads=(mk + "Cf", mk + "eselb"), writes=(f"psb{cb2}",))
                    S.emit("scalar", lambda h, bb=bb, par=par: h.copy(
                        out=bcBs[:, par, :], in_=ps[:, bb, :tc]),
                        reads=(f"psb{bb}",), writes=(f"{mk}bcBs{par}",))
                    dsl = n % 2
                    S.emit("scalar", lambda h, hh=hh, n=n, dsl=dsl, hb=hb: h.activation(
                        out=dAs[:, dsl, :], in_=ps[:, hb, :tc], func=AF.Exp,
                        scale=Abc[:, hh * 16 + n:hh * 16 + n + 1]),
                        reads=(f"psb{hb}", mk + "Abc"), writes=(f"{mk}dA{dsl}",))
                    for jo in range(2):
                        j = 2 * hh + jo
                        S.emit("vector", lambda h, j=j, jo=jo, par=par, dsl=dsl: h.tensor_tensor(
                            out=Xs[:, jo, :], in0=dubuf[:, j, :], in1=bcBs[:, par, :],
                            op=ALU.mult),
                            reads=(mk + "dub", f"{mk}bcBs{par}"), writes=(f"{mk}X{jo}",))
                        S.emit("vector", lambda h, j=j, n=n, jo=jo, par=par, dsl=dsl: h.tensor_tensor_scan(
                            out=hs[:, par, jo, :], data0=dAs[:, dsl, :], data1=Xs[:, jo, :],
                            initial=carry[:, j * 16 + n:j * 16 + n + 1],
                            op0=ALU.mult, op1=ALU.add),
                            reads=(f"{mk}dA{dsl}", f"{mk}X{jo}", mk + "carry",
                                   f"{mk}hsG{par}"),
                            writes=(f"{mk}h{par}{jo}",))
                    S.emit("vector", lambda h, hh=hh, n=n, par=par: h.tensor_copy(
                        out=carry.ap()[:, 2 * hh * 16 + n:(2 * hh + 2) * 16:16],
                        in_=hs.ap()[:, par, :, tc - 1]),
                        reads=(f"{mk}h{par}0", f"{mk}h{par}1"),
                        writes=(mk + "carry",))
                    S.emit("vector", lambda h, par=par, cb2=cb2: h.tensor_tensor(
                        out=hs[:, par, :, :], in0=hs[:, par, :, :],
                        in1=ps.ap()[:, cb2:cb2 + 1, :tc].broadcast_to([128, 2, tc]),
                        op=ALU.mult),
                        reads=(f"{mk}h{par}0", f"{mk}h{par}1", f"psb{cb2}"),
                        writes=(f"{mk}h{par}0", f"{mk}h{par}1")),
                    S.emit("gpsimd", lambda h, hh=hh, par=par: h.tensor_add(
                        out=ybuf[:, 2 * hh:2 * hh + 2, :], in0=ybuf[:, 2 * hh:2 * hh + 2, :],
                        in1=hs[:, par, :, :]),
                        reads=(f"{mk}h{par}0", f"{mk}h{par}1", mk + "y"),
                        writes=(mk + "y", f"{mk}hsG{par}"))
            S.emit("sync", lambda h, t0=t0: h.dma_start(
                out=zb[:], in_=pr[:, 0:16, t0:t0 + tc]),
                reads=(f"{tag}_pd",), writes=(mk + "z",), dma=True)
            S.emit("sync", lambda h, t0=t0: h.dma_start(
                out=rbuf[:], in_=rr[:, 0:16, t0:t0 + tc]),
                reads=(f"{tag}_rd",), writes=(mk + "rb",), dma=True)
            S.emit("scalar", lambda h: h.activation(
                out=zb[:], in_=zb[:], func=AF.Silu),
                reads=(mk + "z",), writes=(mk + "z",))
            S.emit("vector", lambda h: h.tensor_mul(
                out=ybuf[:], in0=ybuf[:], in1=zb[:]),
                reads=(mk + "y", mk + "z"), writes=(mk + "y",))
            S.emit("vector", lambda h: h.tensor_add(
                out=ybuf[:], in0=ybuf[:], in1=rbuf[:]),
                reads=(mk + "y", mk + "rb"), writes=(mk + "y",))
            S.emit("scalar", lambda h: h.activation(
                out=du[:], in_=ybuf[:], func=AF.Square),
                reads=(mk + "y",), writes=(mk + "du",))
            def _rms(h):
                for j in range(16):
                    last = h.matmul(ps[0:1, 6, :tc], C.ones_col[:], du[:, j, :],
                                    start=(j == 0), stop=(j == 15))
                return last
            S.emit("tensor", _rms, reads=(mk + "du", "ones_col"), writes=("psb6",))
            S.emit("vector", lambda h: h.tensor_scalar_mul(
                out=rrow[:, 0, :], in0=ps[0:1, 6, :tc], scalar1=1.0 / 2048.0),
                reads=("psb6",), writes=(mk + "rrow",))
            S.emit("scalar", lambda h: h.activation(
                out=rrow[:, 1, :], in_=rrow[:, 0, :], func=AF.Sqrt,
                bias=C.cpool[0:1, 1:2]),
                reads=(mk + "rrow", "cpool"), writes=(mk + "rrow",))
            S.emit("vector", lambda h: h.reciprocal(
                out=rrow[:, 0, :], in_=rrow[:, 1, :]),
                reads=(mk + "rrow",), writes=(mk + "rrow",))
            S.emit("tensor", lambda h: h.matmul(
                ps[:, 7, :tc], C.ones_row[:], rrow[:, 0, :], start=True, stop=True),
                reads=(mk + "rrow", "ones_row"), writes=("psb7",))
            for j in range(16):
                S.emit("vector", lambda h, j=j: h.tensor_scalar_mul(
                    out=du[:, j, :], in0=ybuf[:, j, :], scalar1=nrmw[:, j:j + 1]),
                    reads=(mk + "y", mk + "nw"), writes=(mk + "du",))
            S.emit("vector", lambda h: h.tensor_tensor(
                out=gnb[:], in0=du[:],
                in1=ps.ap()[:, 7:8, :tc].broadcast_to([128, 16, tc]), op=ALU.mult),
                reads=(mk + "du", "psb7"), writes=(mk + "gn",))
            for ihalf in range(2):
                S.emit("gpsimd", lambda h, ihalf=ihalf: h.dma_start(
                    out=wout[:], in_=W["w_outT"].rearrange(
                        "(kt p) i -> p kt i", p=128)[:, :, ihalf * 512:(ihalf + 1) * 512]),
                    writes=(mk + "wout",), dma=True)
                for i in range(4):
                    ig = ihalf * 4 + i
                    pb = 2 + i % 4

                    def _outp(h, i=i, pb=pb):
                        for kk in range(16):
                            last = h.matmul(
                                ps[:, pb, :tc], wout[:, kk, ts(i)], gnb[:, kk, :],
                                start=(kk == 0), stop=(kk == 15))
                        return last
                    S.emit("tensor", _outp, reads=(mk + "wout", mk + "gn"),
                           writes=(f"psb{pb}",))
                    S.emit("vector", lambda h, ig=ig, t0=t0, pb=pb: h.tensor_add(
                        out=C.x_res[:, ig, t0:t0 + tc], in0=C.x_res[:, ig, t0:t0 + tc],
                        in1=ps[:, pb, :tc]),
                        reads=(f"psb{pb}", "x_res"), writes=("x_res",))
        S.barrier()
        S.flush()



def emit_l1b(C, tag="l1b"):
    import contextlib
    nc, S = C.nc, C.S
    W = C.dram_w["l1b"]
    tc = 512
    with contextlib.ExitStack() as es:
        ps_ = es.enter_context(nc.psum_tensor(f"{tag}_ps", [128, 8, 512], F32))
        ps = ps_
        psb16 = ps_.bitcast(BF16)
        x2 = es.enter_context(nc.sbuf_tensor(f"{tag}_x2", [128, 8, 128], F32))
        rows = es.enter_context(nc.sbuf_tensor(f"{tag}_rows", [1, 6, 128], F32))
        xn = es.enter_context(nc.sbuf_tensor(f"{tag}_xn", [128, 8, T_FULL], BF16))
        kb = es.enter_context(nc.sbuf_tensor(f"{tag}_kb", [128, 8, T_FULL], BF16))
        vt = es.enter_context(nc.sbuf_tensor(f"{tag}_vt", [128, 16, 1040], BF16))
        obuf = es.enter_context(nc.sbuf_tensor(f"{tag}_ob", [128, 8, 512], BF16))
        pt = es.enter_context(nc.sbuf_tensor(f"{tag}_pt", [128, 2, 512], BF16))
        wdk = es.enter_context(nc.sbuf_tensor(f"{tag}_wdksb", [128, 8, 128], BF16))
        wdv = es.enter_context(nc.sbuf_tensor(f"{tag}_wdvsb", [128, 8, 128], BF16))
        wuk = es.enter_context(nc.sbuf_tensor(f"{tag}_wuksb", [128, 1024], BF16))
        wuv = es.enter_context(nc.sbuf_tensor(f"{tag}_wuvsb", [128, 1024], BF16))
        wo = es.enter_context(nc.sbuf_tensor(f"{tag}_wosb", [128, 8, 512], BF16))
        bdk = es.enter_context(nc.sbuf_tensor(f"{tag}_bdksb", [128, 2], F32))
        buk = es.enter_context(nc.sbuf_tensor(f"{tag}_buksb", [128, 8, 2], F32))
        bo = es.enter_context(nc.sbuf_tensor(f"{tag}_bosb", [128, 8], F32))
        cosb = es.enter_context(nc.sbuf_tensor(f"{tag}_cossb", [128, 512], BF16))
        sinb = es.enter_context(nc.sbuf_tensor(f"{tag}_sinsb", [128, 512], BF16))
        perm = es.enter_context(nc.sbuf_tensor(f"{tag}_permsb", [128, 128], BF16))
        ckb = es.enter_context(nc.sbuf_tensor(f"{tag}_ckb", [128, 512], BF16))
        cvb = es.enter_context(nc.sbuf_tensor(f"{tag}_cvb", [128, 512], BF16))
        vtmp = es.enter_context(nc.sbuf_tensor(f"{tag}_vtmp", [128, 512], BF16))
        rtmp = es.enter_context(nc.sbuf_tensor(f"{tag}_rtmp", [128, 2, 512], BF16))
        osc = es.enter_context(nc.sbuf_tensor(f"{tag}_osc", [64, 512], BF16))
        den = es.enter_context(nc.sbuf_tensor(f"{tag}_den", [1, 512], F32))
        mk = tag
        for name, dst, key, cast in (
                ("w_dk", wdk, "wdk", True), ("w_dv", wdv, "wdv", True),
                ("w_uk", wuk, "wuk", True), ("w_uv", wuv, "wuv", True),
                ("perm", perm, "perm", True)):
            if name in ("w_dk", "w_dv"):
                inap = W[name].rearrange("(kt p) c -> p kt c", p=128)
            else:
                inap = W[name]
            S.emit("gpsimd", lambda h, inap=inap, dst=dst: h.dma_start(
                out=dst[:], in_=inap), writes=(mk + key,), dma=True)
        for name, dst, key in (("b_dkv", bdk, "bdk"), ("b_ukv", buk, "buk"),
                               ("b_o", bo, "bo")):
            S.emit("sync", lambda h, name=name, dst=dst: h.dma_start(
                out=dst[:], in_=W[name]), writes=(mk + key,), dma=True)
        for q in range(16):
            emit_ln(C, xn, "xnorm", q * 128, 128, tag + str(q), ps, x2, rows)
        S.emit("vector", lambda h: h.memset(vt[:, :, 64::65], 1.0),
               writes=(mk + "vt",))
        for ch in range(4):
            t0 = ch * tc
            csl = slice(t0, t0 + tc)
            def _ck(h, csl=csl):
                for kk in range(8):
                    last = h.matmul(ps[:, 0, :], wdk[:, kk, :], xn[:, kk, csl],
                                    start=(kk == 0), stop=(kk == 7))
                return last
            S.emit("tensor", _ck, reads=(mk + "wdk", "xnorm"), writes=("psb0",))
            S.emit("scalar", lambda h: h.activation(
                out=ckb[:], in_=ps[:, 0, :], func=AF.Identity, bias=bdk[:, 0:1]),
                reads=("psb0", mk + "bdk"), writes=(mk + "ckb",))
            def _cv(h, csl=csl):
                for kk in range(8):
                    last = h.matmul(ps[:, 1, :], wdv[:, kk, :], xn[:, kk, csl],
                                    start=(kk == 0), stop=(kk == 7))
                return last
            S.emit("tensor", _cv, reads=(mk + "wdv", "xnorm"), writes=("psb1",))
            S.emit("scalar", lambda h: h.activation(
                out=cvb[:], in_=ps[:, 1, :], func=AF.Identity, bias=bdk[:, 1:2]),
                reads=("psb1", mk + "bdk"), writes=(mk + "cvb",))
            for i in range(8):
                S.emit("tensor", lambda h, i=i: h.matmul(
                    ps[:, 2, :], wuk[:, ts(i)], ckb[:],
                    start=True, stop=True),
                    reads=(mk + "wuk", mk + "ckb"), writes=("psb2",))
                S.emit("scalar", lambda h, i=i, csl=csl: h.activation(
                    out=kb[:, i, csl], in_=ps[:, 2, :], func=AF.Identity,
                    bias=buk[:, i, 0:1]),
                    reads=("psb2", mk + "buk"), writes=(mk + "kb",))
                S.emit("tensor", lambda h, i=i: h.matmul(
                    ps[:, 3, :], wuv[:, ts(i)], cvb[:],
                    start=True, stop=True),
                    reads=(mk + "wuv", mk + "cvb"), writes=("psb3",))
                S.emit("scalar", lambda h, i=i: h.activation(
                    out=vtmp[:], in_=ps[:, 3, :], func=AF.Identity,
                    bias=buk[:, i, 1:2]),
                    reads=("psb3", mk + "buk"), writes=(mk + "vtmp",))
                for tt4 in range(4):
                    pb = 6 + tt4 % 2
                    S.emit("tensor", lambda h, tt4=tt4, pb=pb: h.transpose(
                        psb16[:, pb, 0:128], vtmp[:, tt4 * 128:(tt4 + 1) * 128],
                        C.ident_b[:]),
                        reads=(mk + "vtmp", "ident_b"), writes=(f"psb{pb}",))
                    ttg = ch * 4 + tt4
                    S.emit("scalar", lambda h, i=i, ttg=ttg, pb=pb: h.copy(
                        out=vt[:, ttg, (2 * i) * 65:(2 * i) * 65 + 64],
                        in_=psb16[:, pb, 0:64]),
                        reads=(f"psb{pb}",), writes=(mk + "vt",))
                    S.emit("scalar", lambda h, i=i, ttg=ttg, pb=pb: h.copy(
                        out=vt[:, ttg, (2 * i + 1) * 65:(2 * i + 1) * 65 + 64],
                        in_=psb16[:, pb, 64:128]),
                        reads=(f"psb{pb}",), writes=(mk + "vt",))
            S.emit("gpsimd", lambda h, csl=csl: h.dma_start(
                out=cosb[:], in_=W["cos"][:, csl]),
                writes=(mk + "cos",), dma=True)
            S.emit("gpsimd", lambda h, csl=csl: h.dma_start(
                out=sinb[:], in_=W["sin"][:, csl]),
                writes=(mk + "sin",), dma=True)
            for buf, bkey in ((xn, "xnorm"), (kb, mk + "kb")):
                for dt in range(8):
                    pb = 4 + dt % 2
                    S.emit("tensor", lambda h, buf=buf, dt=dt, csl=csl, pb=pb: h.matmul(
                        ps[:, pb, :], perm[:], buf[:, dt, csl],
                        start=True, stop=True),
                        reads=(bkey, mk + "perm"), writes=(f"psb{pb}",))
                    S.emit("vector", lambda h, buf=buf, dt=dt, csl=csl: h.tensor_tensor(
                        out=rtmp[:, 0, :], in0=buf[:, dt, csl], in1=cosb[:],
                        op=ALU.mult),
                        reads=(bkey, mk + "cos"), writes=(mk + "rtmp",))
                    S.emit("vector", lambda h, dt=dt, csl=csl, pb=pb: h.tensor_tensor(
                        out=rtmp[:, 1, :], in0=ps[:, pb, :], in1=sinb[:],
                        op=ALU.mult),
                        reads=(f"psb{pb}", mk + "sin"), writes=(mk + "rtmp",))
                    S.emit("vector", lambda h, buf=buf, dt=dt, csl=csl: h.tensor_add(
                        out=buf[:, dt, csl], in0=rtmp[:, 0, :], in1=rtmp[:, 1, :]),
                        reads=(mk + "rtmp",), writes=(bkey,)),
        for ch in range(4):
            for hh in range(16):
                hb = 64 * (hh % 2)
                hdt = hh // 2
                qsl = slice(ch * 512, (ch + 1) * 512)
                pvb = 4 + hh % 2
                for tkt in range(16):
                    sb = tkt % 3
                    S.emit("tensor", lambda h, hb=hb, hdt=hdt, tkt=tkt, qsl=qsl, sb=sb: h.matmul(
                        ps[:, sb, :], kb[hb:hb + 64, hdt, ts(tkt)],
                        xn[hb:hb + 64, hdt, qsl], start=True, stop=True),
                        reads=(mk + "kb", "xnorm"), writes=(f"psb{sb}",))
                    psl = tkt % 2
                    S.emit("scalar", lambda h, tkt=tkt, sb=sb, psl=psl: h.activation(
                        out=pt[:, psl, :], in_=ps[:, sb, :],
                        func=AF.Exp, scale=0.125),
                        reads=(f"psb{sb}",), writes=(f"{mk}pt{psl}",))
                    S.emit("tensor", lambda h, hh=hh, tkt=tkt, psl=psl, pvb=pvb: h.matmul(
                        ps[0:65, pvb, :], vt[:, tkt, hh * 65:(hh + 1) * 65],
                        pt[:, psl, :],
                        start=(tkt == 0), stop=(tkt == 15)),
                        reads=(mk + "vt", f"{mk}pt{psl}"), writes=(f"psb{pvb}",))
                S.emit("scalar", lambda h, pvb=pvb: h.copy(
                    out=osc[:], in_=ps[0:64, pvb, :]),
                    reads=(f"psb{pvb}",), writes=(mk + "osc",))
                S.emit("scalar", lambda h, pvb=pvb: h.copy(
                    out=den[:], in_=ps[64:65, pvb, :]),
                    reads=(f"psb{pvb}",), writes=(mk + "den",))
                S.emit("vector", lambda h: h.reciprocal(
                    out=den[:], in_=den[:]),
                    reads=(mk + "den",), writes=(mk + "den",))
                S.emit("tensor", lambda h: h.matmul(
                    ps[0:64, 6, :], C.ones_row[0:1, 0:64], den[:],
                    start=True, stop=True),
                    reads=(mk + "den", "ones_row"), writes=("psb6",))
                S.emit("vector", lambda h, hb=hb, hdt=hdt: h.tensor_tensor(
                    out=obuf[hb:hb + 64, hdt, :], in0=osc[:],
                    in1=ps[0:64, 6, :], op=ALU.mult),
                    reads=(mk + "osc", "psb6"), writes=(mk + "ob",))
            for wh in range(2):
                S.emit("gpsimd", lambda h, wh=wh: h.dma_start(
                    out=wo[:], in_=W["w_o"].rearrange(
                        "(kt p) m -> p kt m", p=128)[:, :, wh * 512:(wh + 1) * 512]),
                    writes=(mk + "wo",), dma=True)
                for i in range(4):
                    ig = wh * 4 + i
                    pb = 6 + i % 2

                    def _wo(h, i=i, pb=pb):
                        for kk in range(8):
                            last = h.matmul(
                                ps[:, pb, :], wo[:, kk, ts(i)], obuf[:, kk, :],
                                start=(kk == 0), stop=(kk == 7))
                        return last
                    S.emit("tensor", _wo, reads=(mk + "wo", mk + "ob"),
                           writes=(f"psb{pb}",))
                    S.emit("scalar", lambda h, ig=ig, pb=pb: h.activation(
                        out=rtmp[:, 0, :], in_=ps[:, pb, :], func=AF.Identity,
                        bias=bo[:, ig:ig + 1]),
                        reads=(f"psb{pb}", mk + "bo"), writes=(mk + "rtmp",))
                    S.emit("vector", lambda h, ig=ig, ch=ch: h.tensor_add(
                        out=C.x_res[:, ig, ch * 512:(ch + 1) * 512],
                        in0=C.x_res[:, ig, ch * 512:(ch + 1) * 512],
                        in1=rtmp[:, 0, :]),
                        reads=(mk + "rtmp", "x_res"), writes=("x_res",))
        S.barrier()
        S.flush()


def build_program(stages, io_mode):
    nc = bass.Bass("TRN2", target_bir_lowering=False)
    C = Ctx()
    C.nc = nc
    C.dram = {}
    C.dram_w = {}
    declare_dram(C, stages, io_mode)
    with contextlib.ExitStack() as es0:
        sems = {}
        for e in ENGINES:
            for p in range(Sched.NPOOL[e]):
                sems[f"{e}:c{p}"] = es0.enter_context(
                    nc.semaphore(f"sem_{e}_c{p}"))
            for p in range(Sched.NDMA.get(e, 0)):
                sems[f"{e}:d{p}"] = es0.enter_context(
                    nc.semaphore(f"sem_{e}_d{p}"))
        block = es0.enter_context(nc.Block())
        C.S = Sched(nc, block, sems)
        C.x_res = es0.enter_context(nc.sbuf_tensor("x_res", [128, NDT, T_FULL], F32))
        build_consts(C, es0)
        C.S.barrier()
        if io_mode == "full":
            with contextlib.ExitStack() as es:
                stage_pro(C, es)
                C.S.barrier()
                C.S.flush()
        else:
            with contextlib.ExitStack() as es:
                stage_load_res(C, es)
                C.S.barrier()
                C.S.flush()
        for st in stages:
            with contextlib.ExitStack() as es:
                STAGE_FNS[st](C, es)
                C.S.flush()
        if io_mode == "full":
            with contextlib.ExitStack() as es:
                C.S.barrier()
                stage_epi(C, es)
                C.S.flush()
        else:
            with contextlib.ExitStack() as es:
                C.S.barrier()
                stage_store_res(C, es)
                C.S.flush()
        C.S.emit("sync", lambda h: h.nop(), reads=("out_done",))
        C.S.flush()
    return C


def declare_dram(C, stages, io_mode):
    nc = C.nc
    if io_mode == "full":
        C.dram["x_tm"] = nc.dram_tensor("x_tm", [T_FULL, D_MODEL], F32,
                                        kind="ExternalInput").ap()
        C.dram["out_tm"] = nc.dram_tensor("out_tm", [T_FULL, D_MODEL], F32,
                                          kind="ExternalOutput").ap()
    else:
        C.dram["xin_fm"] = nc.dram_tensor("xin_fm", [D_MODEL, T_FULL], F32,
                                          kind="ExternalInput").ap()
        C.dram["out_tm"] = nc.dram_tensor("xout_fm", [D_MODEL, T_FULL], F32,
                                          kind="ExternalOutput").ap()
        C.dram["xout_fm"] = C.dram["out_tm"]
    for st in stages:
        if st in ("l0a", "l0b"):
            nm = 1 if st == "l0a" else 4
            tsub = T_FULL if st == "l0a" else 512
            W = {}
            W["w_inT"] = nc.dram_tensor(f"{st}_w_inT", [nm, 1024, 4096], F32,
                                        kind="ExternalInput").ap()
            W["conv_w"] = nc.dram_tensor(f"{st}_conv_w", [nm, 128, 16, 4], F32,
                                         kind="ExternalInput").ap()
            W["conv_b"] = nc.dram_tensor(f"{st}_conv_b", [nm, 128, 16], F32,
                                         kind="ExternalInput").ap()
            W["w_xprojT"] = nc.dram_tensor(f"{st}_w_xprojT", [nm, 2048, 128], F32,
                                           kind="ExternalInput").ap()
            W["w_dtT"] = nc.dram_tensor(f"{st}_w_dtT", [nm, 64, 2048], F32,
                                        kind="ExternalInput").ap()
            W["b_dt"] = nc.dram_tensor(f"{st}_b_dt", [nm, 128, 16], F32,
                                       kind="ExternalInput").ap()
            W["A_log"] = nc.dram_tensor(f"{st}_A_log", [nm, 128, 256], F32,
                                        kind="ExternalInput").ap()
            W["D"] = nc.dram_tensor(f"{st}_D", [nm, 128, 16], F32,
                                    kind="ExternalInput").ap()
            W["w_outT"] = nc.dram_tensor(f"{st}_w_outT", [nm, 2048, 1024], F32,
                                         kind="ExternalInput").ap()
            C.dram_w[st] = W
            C.dram[f"{st}_xz"] = nc.dram_tensor(f"{st}_xz", [nm, 4096, tsub], BF16,
                                                kind="Internal").ap()
        elif st in ("l0f1", "l0f2") and "l0f" not in C.dram_w:
            W = {}
            W["gate_wT"] = nc.dram_tensor("l0f_gate_wT", [1024, 16], F32,
                                          kind="ExternalInput").ap()
            W["gate_b"] = nc.dram_tensor("l0f_gate_b", [1, 16], F32,
                                         kind="ExternalInput").ap()
            W["w13T"] = nc.dram_tensor("l0f_w13T", [16, 1024, 1024], F32,
                                       kind="ExternalInput").ap()
            W["w2T"] = nc.dram_tensor("l0f_w2T", [16, 512, 1024], F32,
                                      kind="ExternalInput").ap()
            W["w13sT"] = nc.dram_tensor("l0f_w13sT", [1024, 1024], F32,
                                        kind="ExternalInput").ap()
            W["w2sT"] = nc.dram_tensor("l0f_w2sT", [512, 1024], F32,
                                       kind="ExternalInput").ap()
            C.dram_w["l0f"] = W
        elif st in ("l1f1", "l1f2") and "l1f" not in C.dram_w:
            W = {}
            W["w_gate"] = nc.dram_tensor("l1f_w_gate", [1024, 8], F32,
                                         kind="ExternalInput").ap()
            W["b_gate"] = nc.dram_tensor("l1f_b_gate", [1, 8], F32,
                                         kind="ExternalInput").ap()
            W["W1"] = nc.dram_tensor("l1f_W1", [8, 1024, 4096], F32,
                                     kind="ExternalInput").ap()
            W["W2"] = nc.dram_tensor("l1f_W2", [8, 4096, 1024], F32,
                                     kind="ExternalInput").ap()
            W["b1p"] = nc.dram_tensor("l1f_b1p", [128, 32, 8], F32,
                                      kind="ExternalInput").ap()
            W["b2"] = nc.dram_tensor("l1f_b2", [8, 1024], F32,
                                     kind="ExternalInput").ap()
            C.dram_w["l1f"] = W
        elif st == "l1a":
            W = {}
            W["w_inT"] = nc.dram_tensor("l1a_w_inT", [1024, 5248], F32,
                                        kind="ExternalInput").ap()
            W["w_resT"] = nc.dram_tensor("l1a_w_resT", [1024, 2048], F32,
                                         kind="ExternalInput").ap()
            W["w_outT"] = nc.dram_tensor("l1a_w_outT", [2048, 1024], F32,
                                         kind="ExternalInput").ap()
            W["w_dtT"] = nc.dram_tensor("l1a_w_dtT", [64, 8], F32,
                                        kind="ExternalInput").ap()
            W["conv_w"] = nc.dram_tensor("l1a_conv_w", [128, 16, 4], F32,
                                         kind="ExternalInput").ap()
            W["conv_b"] = nc.dram_tensor("l1a_conv_b", [128, 16], F32,
                                         kind="ExternalInput").ap()
            W["dt_bias"] = nc.dram_tensor("l1a_dt_bias", [8, 1], F32,
                                          kind="ExternalInput").ap()
            W["A_flat"] = nc.dram_tensor("l1a_A_flat", [1, 128], F32,
                                         kind="ExternalInput").ap()
            W["Dskip"] = nc.dram_tensor("l1a_Dskip", [128, 16], F32,
                                        kind="ExternalInput").ap()
            W["norm_w"] = nc.dram_tensor("l1a_norm_w", [128, 16], F32,
                                         kind="ExternalInput").ap()
            C.dram_w["l1a"] = W
            C.dram["l1a_p"] = nc.dram_tensor("l1a_p", [5248, 2048], BF16,
                                             kind="Internal").ap()
            C.dram["l1a_r"] = nc.dram_tensor("l1a_r", [2048, 2048], BF16,
                                             kind="Internal").ap()
        elif st == "l1b":
            W = {}
            for nm, shp in (("w_dk", [1024, 128]), ("w_dv", [1024, 128]),
                            ("w_uk", [128, 1024]), ("w_uv", [128, 1024]),
                            ("w_o", [1024, 1024]), ("cos", [128, 2048]),
                            ("sin", [128, 2048]), ("perm", [128, 128]),
                            ("b_dkv", [128, 2]), ("b_ukv", [128, 8, 2]),
                            ("b_o", [128, 8])):
                W[nm] = nc.dram_tensor(f"l1b_{nm}", shp, F32,
                                       kind="ExternalInput").ap()
            C.dram_w["l1b"] = W


def stage_l0f1(C, es):
    emit_l0ffn(C, "l0f1")


def stage_l0f2(C, es):
    emit_l0ffn(C, "l0f2")


def stage_l1f1(C, es):
    emit_l1ffn(C, "l1f1")


def stage_l1f2(C, es):
    emit_l1ffn(C, "l1f2")


def stage_l1a(C, es):
    emit_l1a(C)


def stage_l1b(C, es):
    emit_l1b(C)


STAGE_FNS = {
    "l1b": stage_l1b,
    "l1a": stage_l1a,
    "l0a": stage_l0a,
    "l0b": stage_l0b,
    "l0f1": stage_l0f1,
    "l0f2": stage_l0f2,
    "l1f1": stage_l1f1,
    "l1f2": stage_l1f2,
}


def prep_weights(I):
    out = {}

    def mamba1_pack(pfx, w_in, conv_w, conv_b, w_xproj, w_dt, b_dt, a_log, d_skip,
                    w_out):
        nm = w_in.shape[0]
        o = {}
        o["w_inT"] = np.ascontiguousarray(np.swapaxes(w_in, 1, 2))
        o["conv_w"] = np.ascontiguousarray(
            conv_w.reshape(nm, 16, 128, 4).transpose(0, 2, 1, 3))
        o["conv_b"] = np.ascontiguousarray(
            conv_b.reshape(nm, 16, 128).transpose(0, 2, 1))
        xp = np.zeros((nm, 2048, 128), np.float32)
        xp[:, :, 0:16] = np.swapaxes(w_xproj[:, 64:80, :], 1, 2)
        xp[:, :, 32:48] = np.swapaxes(w_xproj[:, 80:96, :], 1, 2)
        xp[:, :, 64:128] = np.swapaxes(w_xproj[:, 0:64, :], 1, 2)
        o["w_xprojT"] = xp
        o["w_dtT"] = np.ascontiguousarray(np.swapaxes(w_dt, 1, 2))
        o["b_dt"] = np.ascontiguousarray(b_dt.reshape(nm, 16, 128).transpose(0, 2, 1))
        o["A_log"] = np.ascontiguousarray(
            a_log.reshape(nm, 16, 128, 16).transpose(0, 2, 1, 3).reshape(nm, 128, 256))
        o["D"] = np.ascontiguousarray(d_skip.reshape(nm, 16, 128).transpose(0, 2, 1))
        o["w_outT"] = np.ascontiguousarray(np.swapaxes(w_out, 1, 2))
        return {f"{pfx}_{k}": v for k, v in o.items()}

    out.update(mamba1_pack(
        "l0a", I["L0_attn_a_W_in"][None], I["L0_attn_a_conv_w"][None],
        I["L0_attn_a_conv_b"][None], I["L0_attn_a_W_xproj"][None],
        I["L0_attn_a_W_dt"][None], I["L0_attn_a_b_dt"][None],
        I["L0_attn_a_A_log"][None], I["L0_attn_a_D_skip"][None],
        I["L0_attn_a_W_out"][None]))
    out.update(mamba1_pack(
        "l0b", I["L0_attn_b_in_w"], I["L0_attn_b_conv_w"], I["L0_attn_b_conv_b"],
        I["L0_attn_b_xproj_w"], I["L0_attn_b_dt_w"], I["L0_attn_b_dt_b"],
        I["L0_attn_b_A_log"], I["L0_attn_b_D_skip"], I["L0_attn_b_out_w"]))
    out["l0f_gate_wT"] = np.swapaxes(I["L0_ffn_gate_w"], 0, 1)
    out["l0f_gate_b"] = np.asarray(I["L0_ffn_gate_b"])[None, :]
    out["l0f_w13T"] = np.swapaxes(I["L0_ffn_w13"], 1, 2)
    out["l0f_w2T"] = np.swapaxes(I["L0_ffn_w2"], 1, 2)
    out["l0f_w13sT"] = np.swapaxes(I["L0_ffn_w13_s"], 0, 1)
    out["l0f_w2sT"] = np.swapaxes(I["L0_ffn_w2_s"], 0, 1)
    out["l1f_w_gate"] = np.asarray(I["L1_ffn_w_gate"])
    out["l1f_b_gate"] = np.asarray(I["L1_ffn_b_gate"])[None, :]
    out["l1f_W1"] = np.asarray(I["L1_ffn_W1"])
    out["l1f_W2"] = np.asarray(I["L1_ffn_W2"])
    out["l1f_b1p"] = np.asarray(I["L1_ffn_b1"]).reshape(8, 32, 128).transpose(2, 1, 0)
    out["l1f_b2"] = np.asarray(I["L1_ffn_b2"])
    win = np.asarray(I["L1_attn_a_W_in"])
    wp = np.zeros((5248, 1024), np.float32)
    wp[0:4160] = win[0:4160]
    for hh in range(8):
        wp[4224 + 64 * hh: 4224 + 64 * hh + 16] = win[4160 + 16 * hh: 4160 + 16 * (hh + 1)]
        wp[4736 + 64 * hh: 4736 + 64 * hh + 16] = win[4288 + 16 * hh: 4288 + 16 * (hh + 1)]
    out["l1a_w_inT"] = np.ascontiguousarray(wp.T)
    out["l1a_w_resT"] = np.ascontiguousarray(np.asarray(I["L1_attn_a_W_res"]).T)
    out["l1a_w_outT"] = np.ascontiguousarray(np.asarray(I["L1_attn_a_W_out"]).T)
    out["l1a_w_dtT"] = np.ascontiguousarray(np.asarray(I["L1_attn_a_W_dt"]).T)
    out["l1a_conv_w"] = np.ascontiguousarray(
        np.asarray(I["L1_attn_a_conv_w"]).reshape(16, 128, 4).transpose(1, 0, 2))
    out["l1a_conv_b"] = np.ascontiguousarray(
        np.asarray(I["L1_attn_a_conv_b"]).reshape(16, 128).T)
    out["l1a_dt_bias"] = np.asarray(I["L1_attn_a_dt_bias"])[:, None]
    out["l1a_A_flat"] = np.asarray(I["L1_attn_a_A_log"]).reshape(1, 128)
    out["l1a_Dskip"] = np.ascontiguousarray(
        np.asarray(I["L1_attn_a_Dskip"]).reshape(2048).reshape(16, 128).T)
    out["l1a_norm_w"] = np.ascontiguousarray(
        np.asarray(I["L1_attn_a_norm_w"]).reshape(16, 128).T)
    out["l1b_w_dk"] = np.asarray(I["L1_attn_b_w_dk"])
    out["l1b_w_dv"] = np.asarray(I["L1_attn_b_w_dv"])
    out["l1b_w_uk"] = np.asarray(I["L1_attn_b_w_uk"])
    out["l1b_w_uv"] = np.asarray(I["L1_attn_b_w_uv"])
    out["l1b_w_o"] = np.asarray(I["L1_attn_b_w_o"])
    out["l1b_b_dkv"] = np.stack([np.asarray(I["L1_attn_b_b_dk"]),
                                 np.asarray(I["L1_attn_b_b_dv"])], axis=1)
    out["l1b_b_ukv"] = np.stack(
        [np.asarray(I["L1_attn_b_b_uk"]).reshape(8, 128).T,
         np.asarray(I["L1_attn_b_b_uv"]).reshape(8, 128).T], axis=2)
    out["l1b_b_o"] = np.ascontiguousarray(
        np.asarray(I["L1_attn_b_b_o"]).reshape(8, 128).T)
    t = np.arange(2048, dtype=np.float64)
    invf = 10000.0 ** (-(np.arange(128) % 32) / 32.0)
    ang = t[None, :] * invf[:, None]
    out["l1b_cos"] = np.cos(ang).astype(np.float32)
    out["l1b_sin"] = np.sin(ang).astype(np.float32)
    pm = np.zeros((128, 128), np.float32)
    for i in range(128):
        j = i % 64
        if j < 32:
            pm[i + 32, i] = -1.0
        else:
            pm[i - 32, i] = 1.0
    out["l1b_perm"] = pm
    return {k: np.ascontiguousarray(np.asarray(v, np.float32)) for k, v in out.items()}


def input_names_for(stages):
    pref = {"l0a": "l0a_", "l0b": "l0b_", "l0f1": "l0f_", "l0f2": "l0f_",
            "l1a": "l1a_", "l1b": "l1b_", "l1f1": "l1f_", "l1f2": "l1f_"}
    return sorted({p for st in stages for p in [pref[st]]})


def run_chain(inputs, launch_groups=None):
    launch_groups = launch_groups or LAUNCH_GROUPS
    x = np.asarray(inputs["L0_attn_a_hidden_states"], np.float32)
    wmap = prep_weights(inputs)
    n_groups = len(launch_groups)
    cur = None
    for gi, stages in enumerate(launch_groups):
        first = gi == 0
        last = gi == n_groups - 1
        io_mode = "full" if n_groups == 1 else ("chain" if not (first or last) else
                                                ("full" if n_groups == 1 else "chain"))
        C = build_program(stages, "chain")
        names = needed_weight_names(stages)
        in_maps = []
        for c in range(N_CORES):
            m = {"xin_fm": cur[c]}
            for nmname in names:
                m[nmname] = wmap[nmname]
            in_maps.append(m)
        res = run_bass_kernel_spmd(C.nc, in_maps, core_ids=list(range(N_CORES)))
        cur = np.stack([r["xout_fm"] for r in res.results])
    return cur


def needed_weight_names(stages):
    names = []
    for st in stages:
        if st in ("l0a", "l0b"):
            names += [f"{st}_{k}" for k in ("w_inT", "conv_w", "conv_b", "w_xprojT",
                                            "w_dtT", "b_dt", "A_log", "D", "w_outT")]
        elif st in ("l0f1", "l0f2"):
            names += [f"l0f_{k}" for k in ("gate_wT", "gate_b", "w13T", "w2T",
                                           "w13sT", "w2sT")]
        elif st in ("l1f1", "l1f2"):
            names += [f"l1f_{k}" for k in ("w_gate", "b_gate", "W1", "W2",
                                           "b1p", "b2")]
        elif st == "l1a":
            names += [f"l1a_{k}" for k in ("w_inT", "w_resT", "w_outT", "w_dtT",
                                           "conv_w", "conv_b", "dt_bias",
                                           "A_flat", "Dskip", "norm_w")]
        elif st == "l1b":
            names += [f"l1b_{k}" for k in ("w_dk", "w_dv", "w_uk", "w_uv", "w_o",
                                           "cos", "sin", "perm", "b_dkv",
                                           "b_ukv", "b_o")]
    return sorted(set(names))


def kernel(**inputs):
    x = np.ascontiguousarray(np.asarray(inputs["L0_attn_a_hidden_states"],
                                        np.float32))
    wmap = prep_weights(inputs)
    C = build_program(ALL_STAGES, "full")
    names = needed_weight_names(ALL_STAGES)
    in_maps = []
    for c in range(N_CORES):
        m = {"x_tm": np.ascontiguousarray(x[c])}
        for nm in names:
            m[nm] = wmap[nm]
        in_maps.append(m)
    res = run_bass_kernel_spmd(C.nc, in_maps, core_ids=list(range(N_CORES)))
    return np.stack([np.asarray(r["out_tm"], np.float32) for r in res.results])


# version 13
# speedup vs baseline: 1.2345x; 1.0388x over previous
import contextlib
import os
import numpy as np

MAMBA_PHASES = int(os.environ.get("MAMBA_PHASES", "2"))

import concourse.bass as bass
import concourse.mybir as mybir
from concourse.bass_utils import run_bass_kernel_spmd

F32 = mybir.dt.float32
BF16 = mybir.dt.bfloat16
AF = mybir.ActivationFunctionType
ALU = mybir.AluOpType
AX = mybir.AxisListType

D_MODEL = 1024
T_FULL = 2048
NDT = 8
N_CORES = 8

ENGINES = ("sync", "tensor", "vector", "scalar", "gpsimd")

ALL_STAGES = ["l0a", "l0f1", "l0b", "l0f2", "l1a", "l1f1", "l1b", "l1f2"]
LAUNCH_GROUPS = [ALL_STAGES]


def ts(i, n=128):
    return slice(i * n, (i + 1) * n)


class Sched:

    NPOOL = {"sync": 6, "tensor": 12, "vector": 12, "scalar": 12, "gpsimd": 6}
    NDMA = {"sync": 8, "gpsimd": 8, "scalar": 4}

    def __init__(self, nc, block, sems):
        self.nc = nc
        self.block = block
        self.sems = sems
        self.cnt = {}
        for e in ENGINES:
            for p in range(self.NPOOL[e]):
                self.cnt[f"{e}:c{p}"] = 0
            for p in range(self.NDMA.get(e, 0)):
                self.cnt[f"{e}:d{p}"] = 0
        self.gi = {e: 0 for e in ENGINES}
        self.dma_i = {e: 0 for e in ENGINES}
        self.eclock = {e: {} for e in ENGINES}
        self.gclock = {}
        self.bufs = {}
        self.pending = {e: [] for e in ENGINES}
        self.nbar = 0

    def emit(self, eng, fn, reads=(), writes=(), dma=False, extra_waits=()):
        need = dict(extra_waits)
        ec = self.eclock[eng]

        def add_clock(c):
            for f, v in c.items():
                if ec.get(f, 0) < v and need.get(f, 0) < v:
                    need[f] = v

        for k in reads:
            b = self.bufs.get(k)
            if b is not None:
                add_clock(b["w"])
        for k in writes:
            b = self.bufs.get(k)
            if b is not None:
                add_clock(b["w"])
                add_clock(b["r"])
        if dma:
            p = self.dma_i[eng] % self.NDMA[eng]
            self.dma_i[eng] += 1
            tag = f"{eng}:d{p}"
            inc = 16
        else:
            p = self.gi[eng] % self.NPOOL[eng]
            self.gi[eng] += 1
            tag = f"{eng}:c{p}"
            inc = 1
        prev = self.cnt[tag]
        if prev and ec.get(tag, 0) < prev and need.get(tag, 0) < prev:
            need[tag] = prev
        self.cnt[tag] += inc
        mycnt = self.cnt[tag]
        waits = sorted(need.items())
        for f, v in waits:
            if ec.get(f, 0) < v:
                ec[f] = v
            pc = self.gclock.get((f, v))
            if pc:
                for f2, v2 in pc.items():
                    if ec.get(f2, 0) < v2:
                        ec[f2] = v2
        myclock = dict(ec)
        myclock[tag] = mycnt
        self.gclock[(tag, mycnt)] = myclock
        for k in writes:
            self.bufs[k] = {"w": {tag: mycnt}, "r": {}}
        for k in reads:
            b = self.bufs.setdefault(k, {"w": {}, "r": {}})
            b["r"][tag] = mycnt
        sems = self.sems

        def run(h, waits=waits, fn=fn, tag=tag, inc=inc):
            for f, v in waits:
                h.wait_ge(sems[f], v)
            fn(h).then_inc(sems[tag], inc)

        self.pending[eng].append(run)
        return (tag, mycnt)

    def flush(self):
        for eng in ENGINES:
            lst = self.pending[eng]
            if not lst:
                continue

            def body(h, lst=tuple(lst)):
                for r in lst:
                    r(h)

            getattr(self.block, eng)(body)
            self.pending[eng] = []

    def drain_waits(self, eng):
        ec = self.eclock[eng]
        out = []
        for p in range(self.NPOOL[eng]):
            tag = f"{eng}:c{p}"
            if self.cnt[tag] and ec.get(tag, 0) < self.cnt[tag]:
                out.append((tag, self.cnt[tag]))
        for p in range(self.NDMA.get(eng, 0)):
            tag = f"{eng}:d{p}"
            if self.cnt[tag] and ec.get(tag, 0) < self.cnt[tag]:
                out.append((tag, self.cnt[tag]))
        return tuple(out)

    def barrier(self):
        n = self.nbar
        self.nbar += 1
        keys = tuple(f"__bar{n}_{e}" for e in ENGINES)
        for e, k in zip(ENGINES, keys):
            self.emit(e, lambda h: h.nop(), writes=(k,),
                      extra_waits=self.drain_waits(e))
        for e in ENGINES:
            self.emit(e, lambda h: h.nop(), reads=keys)


class Ctx:
    pass


def build_consts(C, es):
    nc, S = C.nc, C.S
    C.ident_f = es.enter_context(nc.sbuf_tensor("ident_f", [128, 128], F32))
    C.ident_b = es.enter_context(nc.sbuf_tensor("ident_b", [128, 128], BF16))
    C.ones_col = es.enter_context(nc.sbuf_tensor("ones_col", [128, 1], F32))
    C.ones_row = es.enter_context(nc.sbuf_tensor("ones_row", [1, 128], F32))
    C.cpool = es.enter_context(nc.sbuf_tensor("cpool", [128, 4], F32))

    for t, key in ((C.ident_f, "ident_f"), (C.ident_b, "ident_b")):
        S.emit("gpsimd", lambda h, t=t: h.memset(t[:], 0.0), writes=(key,))
        S.emit("gpsimd", lambda h, t=t: h.affine_select(
            out=t[:], in_=t[:], compare_op=ALU.not_equal, fill=1.0, base=0,
            pattern=[[-1, 128]], channel_multiplier=1),
            reads=(key,), writes=(key,))
    S.emit("gpsimd", lambda h: h.memset(C.ones_col[:], 1.0), writes=("ones_col",))
    S.emit("gpsimd", lambda h: h.memset(C.ones_row[:], 1.0), writes=("ones_row",))
    for col, val in ((0, 1e-5), (1, 1e-6), (2, 1.0), (3, 0.0)):
        S.emit("gpsimd", lambda h, col=col, val=val: h.memset(
            C.cpool[:, col:col + 1], val), writes=("cpool",))
    nc.const_aps.register(F32, 1e-5, C.cpool[:, 0:1])
    nc.const_aps.register(F32, 1e-6, C.cpool[:, 1:2])


def alloc_esel(C, es, tag):
    nc, S = C.nc, C.S
    C.esel = es.enter_context(nc.sbuf_tensor(f"esel_{tag}", [16, 16, 128], F32))
    S.emit("gpsimd", lambda h: h.memset(C.esel[:], 0.0), writes=("esel",))
    S.emit("gpsimd", lambda h: h.affine_select(
        out=C.esel[:], in_=C.esel[:], compare_op=ALU.not_equal, fill=1.0, base=0,
        pattern=[[-1, 16], [0, 128]], channel_multiplier=1),
        reads=("esel",), writes=("esel",))


def stage_pro(C, es):
    nc, S = C.nc, C.S
    x_tm = C.dram["x_tm"]
    xt = es.enter_context(nc.sbuf_tensor("pro_xt", [128, 2, 1024], F32))
    ps = es.enter_context(nc.psum_tensor("pro_ps", [128, 8, 512], F32))
    for tt in range(16):
        sl = tt % 2
        S.emit("sync", lambda h, tt=tt, sl=sl: h.dma_start(
            out=xt[:, sl, :], in_=x_tm[ts(tt), :]),
            writes=(f"pro_xt{sl}",), dma=True)
        for dt in range(NDT):
            pb = dt % 8
            S.emit("tensor", lambda h, sl=sl, dt=dt, pb=pb: h.transpose(
                ps[:, pb, 0:128], xt[:, sl, ts(dt)], C.ident_f[:]),
                reads=(f"pro_xt{sl}", "ident_f"), writes=(f"pro_ps{pb}",))
            S.emit("scalar", lambda h, tt=tt, dt=dt, pb=pb: h.copy(
                out=C.x_res[:, dt, ts(tt)], in_=ps[:, pb, 0:128]),
                reads=(f"pro_ps{pb}",), writes=("x_res",))


def stage_epi(C, es):
    nc, S = C.nc, C.S
    out_tm = C.dram["out_tm"]
    ot = es.enter_context(nc.sbuf_tensor("epi_ot", [128, 2, 1024], F32))
    ps = es.enter_context(nc.psum_tensor("epi_ps", [128, 8, 512], F32))
    for tt in range(16):
        sl = tt % 2
        for dt in range(NDT):
            pb = dt % 8
            S.emit("tensor", lambda h, tt=tt, dt=dt, pb=pb: h.transpose(
                ps[:, pb, 0:128], C.x_res[:, dt, ts(tt)], C.ident_f[:]),
                reads=("x_res", "ident_f"), writes=(f"epi_ps{pb}",))
            S.emit("scalar", lambda h, dt=dt, sl=sl, pb=pb: h.copy(
                out=ot[:, sl, ts(dt)], in_=ps[:, pb, 0:128]),
                reads=(f"epi_ps{pb}",), writes=(f"epi_ot{sl}",))
        S.emit("sync", lambda h, tt=tt, sl=sl: h.dma_start(
            out=out_tm[ts(tt), :], in_=ot[:, sl, :]),
            reads=(f"epi_ot{sl}",), writes=("out_done",), dma=True)


def stage_load_res(C, es):
    S = C.S
    xin = C.dram["xin_fm"]
    S.emit("sync", lambda h: h.dma_start(
        out=C.x_res[:], in_=xin.rearrange("(a p) t -> p a t", p=128)),
        writes=("x_res",), dma=True)


def stage_store_res(C, es):
    S = C.S
    xout = C.dram["xout_fm"]
    S.emit("sync", lambda h: h.dma_start(
        out=xout.rearrange("(a p) t -> p a t", p=128), in_=C.x_res[:]),
        reads=("x_res",), writes=("out_done",), dma=True)


def emit_ln(C, xnorm, xnk, t0, tc, tag, ps, x2, rows, banks=(4, 5, 6, 7),
            keep_ab=None, keep_off=0, keep_key=None, out_t0=None):
    S = C.S
    ba, bb, b1, b2 = banks
    k = "ln" + tag
    kx2 = "buf_" + x2.name
    krows = "buf_" + rows.name
    sl = (slice(None), slice(None), slice(t0, t0 + tc))
    S.emit("scalar", lambda h: h.activation(out=x2[:, :, :tc], in_=C.x_res[sl],
                                            func=AF.Square),
           reads=("x_res",), writes=(kx2,))
    for j in range(NDT):
        S.emit("tensor", lambda h, j=j: h.matmul(
            ps[0:1, b1, :tc], C.ones_col[:], C.x_res[:, j, t0:t0 + tc],
            start=(j == 0), stop=(j == NDT - 1)),
            reads=("x_res", "ones_col"), writes=(f"psb{b1}",))
    for j in range(NDT):
        S.emit("tensor", lambda h, j=j: h.matmul(
            ps[0:1, b2, :tc], C.ones_col[:], x2[:, j, :tc],
            start=(j == 0), stop=(j == NDT - 1)),
            reads=(kx2, "ones_col"), writes=(f"psb{b2}",))
    S.emit("vector", lambda h: h.tensor_scalar_mul(
        out=rows[:, 2, :tc], in0=ps[0:1, b1, :tc], scalar1=1.0 / D_MODEL),
        reads=(f"psb{b1}",), writes=(krows,))
    S.emit("vector", lambda h: h.tensor_scalar_mul(
        out=rows[:, 3, :tc], in0=ps[0:1, b2, :tc], scalar1=1.0 / D_MODEL),
        reads=(f"psb{b2}", krows), writes=(krows,))
    S.emit("vector", lambda h: h.tensor_mul(
        out=rows[:, 0, :tc], in0=rows[:, 2, :tc], in1=rows[:, 2, :tc]),
        reads=(krows,), writes=(krows,))
    S.emit("vector", lambda h: h.tensor_sub(
        out=rows[:, 3, :tc], in0=rows[:, 3, :tc], in1=rows[:, 0, :tc]),
        reads=(krows,), writes=(krows,))
    S.emit("scalar", lambda h: h.activation(
        out=rows[:, 1, :tc], in_=rows[:, 3, :tc], func=AF.Sqrt,
        bias=C.cpool[0:1, 0:1]),
        reads=(krows, "cpool"), writes=(krows,))
    S.emit("vector", lambda h: h.reciprocal(
        out=rows[:, 4, :tc], in_=rows[:, 1, :tc]),
        reads=(krows,), writes=(krows,))
    S.emit("vector", lambda h: h.tensor_mul(
        out=rows[:, 5, :tc], in0=rows[:, 2, :tc], in1=rows[:, 4, :tc]),
        reads=(krows,), writes=(krows,))
    S.emit("vector", lambda h: h.tensor_scalar_mul(
        out=rows[:, 5, :tc], in0=rows[:, 5, :tc], scalar1=-1.0),
        reads=(krows,), writes=(krows,))
    S.emit("tensor", lambda h: h.matmul(
        ps[:, ba, :tc], C.ones_row[:], rows[:, 4, :tc], start=True, stop=True),
        reads=(krows, "ones_row"), writes=(f"psb{ba}",))
    S.emit("tensor", lambda h: h.matmul(
        ps[:, bb, :tc], C.ones_row[:], rows[:, 5, :tc], start=True, stop=True),
        reads=(krows, "ones_row"), writes=(f"psb{bb}",))
    S.emit("vector", lambda h: h.tensor_tensor(
        out=x2[:, :, :tc], in0=C.x_res[sl],
        in1=ps.ap()[:, ba:ba + 1, :tc].broadcast_to([128, NDT, tc]), op=ALU.mult),
        reads=("x_res", f"psb{ba}"), writes=(kx2,))
    o0 = t0 if out_t0 is None else out_t0
    S.emit("vector", lambda h: h.tensor_tensor(
        out=xnorm[:, :, o0:o0 + tc], in0=x2[:, :, :tc],
        in1=ps.ap()[:, bb:bb + 1, :tc].broadcast_to([128, NDT, tc]), op=ALU.add),
        reads=(kx2, f"psb{bb}"), writes=(xnk,))
    if keep_ab is not None:
        S.emit("vector", lambda h: h.tensor_copy(
            out=keep_ab[:, 0:2, keep_off:keep_off + tc],
            in_=rows[:, 4:6, :tc]),
            reads=(krows,), writes=(keep_key,))


def emit_mamba1(C, tag, n_mamba, t_sub):
    nc, S = C.nc, C.S
    W = C.dram_w[tag]
    xz_dram = C.dram[f"{tag}_xz"]
    with contextlib.ExitStack() as es:
        ps = es.enter_context(nc.psum_tensor(f"{tag}_psA", [128, 8, 512], F32))
        x2 = es.enter_context(nc.sbuf_tensor(f"{tag}_x2", [128, 8, 512], F32))
        rows = es.enter_context(nc.sbuf_tensor(f"{tag}_rows", [1, 6, 512], F32))
        xnorm = es.enter_context(nc.sbuf_tensor(f"{tag}_xn", [128, 8, T_FULL], BF16))
        wblk = es.enter_context(nc.sbuf_tensor(f"{tag}_wblk", [128, 2, 8, 512], BF16))
        evb = es.enter_context(nc.sbuf_tensor(f"{tag}_evb", [128, 2, 4, 512], BF16))
        if n_mamba > 1:
            xnsub = es.enter_context(nc.sbuf_tensor(f"{tag}_xns", [128, 8, t_sub], BF16))
        for ch in range(4):
            emit_ln(C, xnorm, "xnorm", ch * 512, 512, tag + str(ch), ps, x2, rows)
        if MAMBA_PHASES < 1:
            S.barrier()
            S.flush()
            return
        tcA = min(512, t_sub)
        ntc = max(1, t_sub // 512)
        for m in range(n_mamba):
            if n_mamba > 1:
                S.emit("vector", lambda h, m=m: h.tensor_copy(
                    out=xnsub[:], in_=xnorm[:, :, m::n_mamba]),
                    reads=("xnorm",), writes=("xnsub",))
                xn, xnk = xnsub, "xnsub"
            else:
                xn, xnk = xnorm, "xnorm"
            for ib in range(int(os.environ.get("MAMBA_IB", "8"))):
                wsl = ib % 2
                if True:
                    S.emit("gpsimd", lambda h, m=m, ib=ib, wsl=wsl: h.dma_start(
                        out=wblk[:, wsl, :, :],
                        in_=W["w_inT"][m].rearrange("(kt p) i -> p kt i", p=128)[:, :, ts(ib, 512)]),
                        writes=(f"{tag}_wblk{wsl}",), dma=True)
                if os.environ.get("MAMBA_NOMM"):
                    continue
                for ch in range(ntc):
                    esl = (ib * ntc + ch) % 2
                    for i in range(4):
                        pb = (ib * 4 + i) % 8
                        for kk in range(8):
                            S.emit("tensor", lambda h, ch=ch, i=i, kk=kk, wsl=wsl, pb=pb: h.matmul(
                                ps[:, pb, :tcA], wblk[:, wsl, kk, ts(i)],
                                xn[:, kk, ch * 512:ch * 512 + tcA],
                                start=(kk == 0), stop=(kk == 7)),
                                reads=(f"{tag}_wblk{wsl}", xnk), writes=(f"psb{pb}",))
                        S.emit("scalar", lambda h, i=i, esl=esl, pb=pb: h.copy(
                            out=evb[:, esl, i, :tcA], in_=ps[:, pb, :tcA]),
                            reads=(f"psb{pb}",), writes=(f"{tag}_evb{esl}",))
                    if not os.environ.get("MAMBA_NOSPILL"):
                        S.emit("sync", lambda h, m=m, ib=ib, ch=ch, esl=esl: h.dma_start(
                            out=xz_dram[m, ts(ib, 512), ch * 512:ch * 512 + tcA].rearrange(
                                "(i p) t -> p i t", p=128),
                            in_=evb[:, esl, :, :tcA]),
                            reads=(f"{tag}_evb{esl}",), writes=(f"{tag}_xzd",), dma=True)
        S.barrier()
        S.flush()
    if MAMBA_PHASES < 2:
        return
    tc = 256
    nch = t_sub // tc
    with contextlib.ExitStack() as es:
        alloc_esel(C, es, tag)
        ps = es.enter_context(nc.psum_tensor(f"{tag}_psB", [128, 8, 512], F32))
        wxp = es.enter_context(nc.sbuf_tensor(f"{tag}_wxp", [128, 16, 128], BF16))
        wdt = es.enter_context(nc.sbuf_tensor(f"{tag}_wdt", [64, 2048], BF16))
        wout = es.enter_context(nc.sbuf_tensor(f"{tag}_wout", [128, 16, 512], BF16))
        convw = es.enter_context(nc.sbuf_tensor(f"{tag}_cw", [128, 16, 4], F32))
        convb = es.enter_context(nc.sbuf_tensor(f"{tag}_cb", [128, 16], F32))
        bdt = es.enter_context(nc.sbuf_tensor(f"{tag}_bdt", [128, 16], F32))
        Asb = es.enter_context(nc.sbuf_tensor(f"{tag}_A", [128, 256], F32))
        Dsb = es.enter_context(nc.sbuf_tensor(f"{tag}_Dsk", [128, 16], F32))
        carry = es.enter_context(nc.sbuf_tensor(f"{tag}_carry", [128, 256], F32))
        xh = es.enter_context(nc.sbuf_tensor(f"{tag}_xh", [128, 1, 16, tc + 4], BF16))
        zb = es.enter_context(nc.sbuf_tensor(f"{tag}_z", [128, 2, 16, tc], BF16))
        ub = es.enter_context(nc.sbuf_tensor(f"{tag}_u", [128, 16, tc], BF16))
        delta = es.enter_context(nc.sbuf_tensor(f"{tag}_dl", [128, 16, tc], F32))
        ybuf = es.enter_context(nc.sbuf_tensor(f"{tag}_y", [128, 16, tc], F32))
        bcrB = es.enter_context(nc.sbuf_tensor(f"{tag}_bcrB", [16, tc], F32))
        bcrC = es.enter_context(nc.sbuf_tensor(f"{tag}_bcrC", [16, tc], F32))
        dtr = es.enter_context(nc.sbuf_tensor(f"{tag}_dtr", [64, tc], BF16))
        dub = es.enter_context(nc.sbuf_tensor(f"{tag}_dub", [128, 16, tc], BF16))
        bcBs = es.enter_context(nc.sbuf_tensor(f"{tag}_bcBs", [128, tc], BF16))
        dAs = es.enter_context(nc.sbuf_tensor(f"{tag}_dAs", [128, 8, tc], F32))
        Xs = es.enter_context(nc.sbuf_tensor(f"{tag}_Xs", [128, 8, tc], BF16))
        hs = es.enter_context(nc.sbuf_tensor(f"{tag}_hs", [128, 2, 16, tc], BF16))
        bcCs = es.enter_context(nc.sbuf_tensor(f"{tag}_bcCs", [128, 2, tc], BF16))
        for m in range(n_mamba):
            mk = f"{tag}m"
            for name, key in (("conv_w", "cw"), ("conv_b", "cb"), ("b_dt", "bdt"),
                              ("D", "D"), ("A_log", "A")):
                dst = {"cw": convw, "cb": convb, "bdt": bdt, "D": Dsb, "A": Asb}[key]
                S.emit("sync", lambda h, m=m, name=name, dst=dst: h.dma_start(
                    out=dst[:], in_=W[name][m]), writes=(mk + key,), dma=True)
            S.emit("scalar", lambda h: h.activation(out=Asb[:], in_=Asb[:], func=AF.Exp),
                   reads=(mk + "A",), writes=(mk + "A",))
            S.emit("vector", lambda h: h.tensor_scalar_mul(
                out=Asb[:], in0=Asb[:], scalar1=-1.0),
                reads=(mk + "A",), writes=(mk + "A",))
            S.emit("gpsimd", lambda h, m=m: h.dma_start(
                out=wxp[:], in_=W["w_xprojT"][m].rearrange("(kt p) i -> p kt i", p=128)),
                writes=(mk + "wxp",), dma=True)
            S.emit("gpsimd", lambda h, m=m: h.dma_start(out=wdt[:], in_=W["w_dtT"][m]),
                   writes=(mk + "wdt",), dma=True)
            S.emit("vector", lambda h: h.memset(carry[:], 0.0), writes=(mk + "carry",))
            def load_x(ch, m=m):
                t0 = ch * tc
                xzr = xz_dram[m].rearrange("(j p) t -> p j t", p=128)
                if ch == 0:
                    S.emit("vector", lambda h: h.memset(xh[:, 0, :, 0:3], 0.0),
                           writes=(mk + "xh",))
                    S.emit("sync", lambda h, xzr=xzr: h.dma_start(
                        out=xh[:, 0, :, 3:3 + tc], in_=xzr[:, 0:16, 0:tc]),
                        reads=(f"{tag}_xzd",), writes=(mk + "xh",), dma=True)
                else:
                    S.emit("sync", lambda h, xzr=xzr, t0=t0: h.dma_start(
                        out=xh[:, 0, :, 0:3 + tc], in_=xzr[:, 0:16, t0 - 3:t0 + tc]),
                        reads=(f"{tag}_xzd",), writes=(mk + "xh",), dma=True)

            def load_z(ch, m=m):
                t0 = ch * tc
                sl = ch % 2
                xzr = xz_dram[m].rearrange("(j p) t -> p j t", p=128)
                S.emit("sync", lambda h, xzr=xzr, t0=t0, sl=sl: h.dma_start(
                    out=zb[:, sl, :, :], in_=xzr[:, 16:32, t0:t0 + tc]),
                    reads=(f"{tag}_xzd",), writes=(f"{mk}z{sl}",), dma=True)

            BSTEP = int(os.environ.get("MAMBA_BSTEP", "99"))
            nch_l = min(nch, int(os.environ.get("MAMBA_NCH", "99")))
            load_x(0)
            load_z(0)
            for ch in range(nch_l):
                t0 = ch * tc
                sl = ch % 2
                if ch + 1 < nch_l:
                    load_z(ch + 1)
                for j in range(16):
                    S.emit("vector", lambda h, j=j: h.tensor_scalar_mul(
                        out=ybuf[:, j, :], in0=xh[:, 0, j, 0:tc], scalar1=convw[:, j, 0:1]),
                        reads=(mk + "xh", mk + "cw"), writes=(mk + "y",))
                    for k4 in (1, 2, 3):
                        S.emit("vector", lambda h, j=j, k4=k4: h.scalar_tensor_tensor(
                            out=ybuf[:, j, :], in0=xh[:, 0, j, k4:k4 + tc],
                            scalar=convw[:, j, k4:k4 + 1], in1=ybuf[:, j, :],
                            op0=ALU.mult, op1=ALU.add),
                            reads=(mk + "xh", mk + "cw", mk + "y"), writes=(mk + "y",))
                if BSTEP < 3:
                    continue
                if ch + 1 < nch_l:
                    load_x(ch + 1)
                for j in range(16):
                    S.emit("scalar", lambda h, j=j: h.activation(
                        out=ub[:, j, :], in_=ybuf[:, j, :], func=AF.Silu,
                        bias=convb[:, j:j + 1]),
                        reads=(mk + "y", mk + "cb"), writes=(mk + "u",))
                if BSTEP < 4:
                    continue
                def _xdbl(h):
                    for kk in range(16):
                        last = h.matmul(ps[:, 7, :tc], wxp[:, kk, :], ub[:, kk, :],
                                        start=(kk == 0), stop=(kk == 15))
                    return last
                S.emit("tensor", _xdbl, reads=(mk + "wxp", mk + "u"),
                       writes=("psb7",))
                S.emit("scalar", lambda h: h.copy(out=bcrB[:, :], in_=ps[0:16, 7, :tc]),
                       reads=("psb7",), writes=(mk + "bcr",))
                S.emit("scalar", lambda h: h.copy(out=bcrC[:, :], in_=ps[32:48, 7, :tc]),
                       reads=("psb7",), writes=(mk + "bcr",))
                S.emit("scalar", lambda h: h.copy(out=dtr[:, :], in_=ps[64:128, 7, :tc]),
                       reads=("psb7",), writes=(mk + "dtr",))
                if BSTEP < 5:
                    continue
                for half in range(2):
                    def _dmm(h, half=half):
                        for jj8 in range(8):
                            last = h.matmul(
                                ps[:, jj8, 0:tc], wdt[:, ts(half * 8 + jj8)],
                                dtr[:, :], start=True, stop=True)
                        return last
                    S.emit("tensor", _dmm, reads=(mk + "dtr", mk + "wdt"),
                           writes=tuple(f"psb{b}" for b in range(8)))

                    def _devac(h, half=half):
                        for jj8 in range(8):
                            last = h.activation(
                                out=delta[:, half * 8 + jj8, :], in_=ps[:, jj8, 0:tc],
                                func=AF.Identity,
                                bias=bdt[:, half * 8 + jj8:half * 8 + jj8 + 1])
                        return last
                    S.emit("scalar", _devac,
                           reads=tuple(f"psb{b}" for b in range(8)) + (mk + "bdt",),
                           writes=(mk + "dl",))
                S.emit("vector", lambda h: h.tensor_scalar_max(
                    out=ybuf[:], in0=delta[:], scalar1=0.0),
                    reads=(mk + "dl",), writes=(mk + "y",))
                S.emit("scalar", lambda h: h.activation(
                    out=delta[:], in_=delta[:], func=AF.Abs),
                    reads=(mk + "dl",), writes=(mk + "dl",))
                S.emit("scalar", lambda h: h.activation(
                    out=delta[:], in_=delta[:], func=AF.Exp, scale=-1.0),
                    reads=(mk + "dl",), writes=(mk + "dl",))
                S.emit("scalar", lambda h: h.activation(
                    out=delta[:], in_=delta[:], func=AF.Ln, bias=1.0),
                    reads=(mk + "dl",), writes=(mk + "dl",))
                S.emit("vector", lambda h: h.tensor_add(
                    out=delta[:], in0=delta[:], in1=ybuf[:]),
                    reads=(mk + "dl", mk + "y"), writes=(mk + "dl",))
                if BSTEP < 6:
                    continue
                for j in range(16):
                    S.emit("vector", lambda h, j=j: h.tensor_scalar_mul(
                        out=ybuf[:, j, :], in0=ub[:, j, :], scalar1=Dsb[:, j:j + 1]),
                        reads=(mk + "u", mk + "D"), writes=(mk + "y",))
                S.emit("vector", lambda h: h.tensor_mul(out=dub[:], in0=delta[:], in1=ub[:]),
                       reads=(mk + "dl", mk + "u"), writes=(mk + "dub",))
                for n in range(min(16, int(os.environ.get("MAMBA_NN", "16")))):
                    bb = n % 2
                    cb = 2 + n % 2
                    S.emit("tensor", lambda h, n=n, bb=bb: h.matmul(
                        ps[:, bb, :tc], C.esel[:, n, :], bcrB[:, :],
                        start=True, stop=True),
                        reads=(mk + "bcr", "esel"), writes=(f"psb{bb}",))
                    S.emit("tensor", lambda h, n=n, cb=cb: h.matmul(
                        ps[:, cb, :tc], C.esel[:, n, :], bcrC[:, :],
                        start=True, stop=True),
                        reads=(mk + "bcr", "esel"), writes=(f"psb{cb}",))
                    par = n % 2
                    S.emit("scalar", lambda h, bb=bb: h.copy(
                        out=bcBs[:], in_=ps[:, bb, :tc]),
                        reads=(f"psb{bb}",), writes=(mk + "bcBs",))
                    S.emit("scalar", lambda h, cb=cb, par=par: h.copy(
                        out=bcCs[:, par, :], in_=ps[:, cb, :tc]),
                        reads=(f"psb{cb}",), writes=(f"{mk}bcCs{par}",))
                    for g4 in range(4):
                        s0 = (g4 % 2) * 4

                        def _dag(h, n=n, g4=g4, s0=s0):
                            for q in range(4):
                                j = 4 * g4 + q
                                last = h.activation(
                                    out=dAs[:, s0 + q, :], in_=delta[:, j, :],
                                    func=AF.Exp,
                                    scale=Asb[:, j * 16 + n:j * 16 + n + 1])
                            return last
                        S.emit("scalar", _dag, reads=(mk + "dl", mk + "A"),
                               writes=(f"{mk}dAg{g4 % 2}",))

                        def _xg(h, g4=g4, s0=s0):
                            for q in range(4):
                                j = 4 * g4 + q
                                last = h.tensor_tensor(
                                    out=Xs[:, s0 + q, :], in0=dub[:, j, :],
                                    in1=bcBs[:], op=ALU.mult)
                            return last
                        S.emit("vector", _xg, reads=(mk + "dub", mk + "bcBs"),
                               writes=(f"{mk}Xg{g4 % 2}",))

                        def _sg(h, n=n, g4=g4, par=par, s0=s0):
                            for q in range(4):
                                j = 4 * g4 + q
                                last = h.tensor_tensor_scan(
                                    out=hs[:, par, j, :], data0=dAs[:, s0 + q, :],
                                    data1=Xs[:, s0 + q, :],
                                    initial=carry[:, j * 16 + n:j * 16 + n + 1],
                                    op0=ALU.mult, op1=ALU.add)
                            return last
                        S.emit("vector", _sg,
                               reads=(f"{mk}dAg{g4 % 2}", f"{mk}Xg{g4 % 2}",
                                      mk + "carry"),
                               writes=(f"{mk}hp{par}",))
                    S.emit("vector", lambda h, n=n, par=par: h.tensor_copy(
                        out=carry.ap()[:, n:256:16], in_=hs.ap()[:, par, :, tc - 1]),
                        reads=(f"{mk}hp{par}",),
                        writes=(mk + "carry",))
                    S.emit("vector", lambda h, par=par: h.tensor_tensor(
                        out=hs[:, par, :, :], in0=hs[:, par, :, :],
                        in1=bcCs.ap()[:, par:par + 1, :].broadcast_to([128, 16, tc]),
                        op=ALU.mult),
                        reads=(f"{mk}hp{par}", f"{mk}bcCs{par}"),
                        writes=(f"{mk}hp{par}",))
                    S.emit("gpsimd", lambda h, par=par: h.tensor_add(
                        out=ybuf[:], in0=ybuf[:], in1=hs[:, par, :, :]),
                        reads=(f"{mk}hp{par}", mk + "y"),
                        writes=(mk + "y", f"{mk}hp{par}"))
                S.emit("scalar", lambda h, sl=sl: h.activation(
                    out=zb[:, sl, :, :], in_=zb[:, sl, :, :], func=AF.Silu),
                    reads=(f"{mk}z{sl}",), writes=(f"{mk}z{sl}",))
                S.emit("vector", lambda h, sl=sl: h.tensor_mul(
                    out=ub[:], in0=ybuf[:], in1=zb[:, sl, :, :]),
                    reads=(mk + "y", f"{mk}z{sl}"), writes=(mk + "u",))
                for ihalf in range(2):
                    S.emit("gpsimd", lambda h, m=m, ihalf=ihalf: h.dma_start(
                        out=wout[:], in_=W["w_outT"][m].rearrange(
                            "(kt p) i -> p kt i", p=128)[:, :, ihalf * 512:(ihalf + 1) * 512]),
                        writes=(mk + "wout",), dma=True)
                    for i4 in range(4):
                        i = ihalf * 4 + i4
                        pb = 4 + i % 4

                        def _outp(h, i4=i4, pb=pb):
                            for kk in range(16):
                                last = h.matmul(
                                    ps[:, pb, :tc], wout[:, kk, ts(i4)], ub[:, kk, :],
                                    start=(kk == 0), stop=(kk == 15))
                            return last
                        S.emit("tensor", _outp,
                               reads=(mk + "wout", mk + "u"), writes=(f"psb{pb}",))
                        if n_mamba == 1:
                            xsl = (slice(None), i, slice(t0, t0 + tc))
                        else:
                            xsl = (slice(None), i,
                                   slice(m * t_sub + t0, m * t_sub + t0 + tc))
                        S.emit("vector", lambda h, xsl=xsl, pb=pb: h.tensor_tensor(
                            out=C.x_res[xsl], in0=C.x_res[xsl], in1=ps[:, pb, :tc],
                            op=ALU.add),
                            reads=(f"psb{pb}", "x_res"), writes=("x_res",))
        S.barrier()
        S.flush()


def stage_l0a(C, es):
    emit_mamba1(C, "l0a", 1, T_FULL)


def stage_l0b(C, es):
    emit_mamba1(C, "l0b", 4, 512)



def emit_l0ffn(C, tag):
    import contextlib
    nc, S = C.nc, C.S
    W = C.dram_w["l0f"]
    with contextlib.ExitStack() as es:
        alloc_esel(C, es, tag)
        ps = es.enter_context(nc.psum_tensor(f"{tag}_ps", [128, 8, 512], F32))
        x2 = es.enter_context(nc.sbuf_tensor(f"{tag}_x2", [128, 8, 256], F32))
        rows = es.enter_context(nc.sbuf_tensor(f"{tag}_rows", [1, 6, 256], F32))
        gw = es.enter_context(nc.sbuf_tensor(f"{tag}_gw", [128, 8, 16], F32))
        gbb = es.enter_context(nc.sbuf_tensor(f"{tag}_gbb", [128, 16], F32))
        gbr = es.enter_context(nc.sbuf_tensor(f"{tag}_gbr", [1, 16], F32))
        Gs = es.enter_context(nc.sbuf_tensor(f"{tag}_Gs", [1, 16], F32))
        xn = es.enter_context(nc.sbuf_tensor(f"{tag}_xn", [128, 8, 1024], BF16))
        keep = es.enter_context(nc.sbuf_tensor(f"{tag}_keep", [1, 2, 1024], F32))
        comb = es.enter_context(nc.sbuf_tensor(f"{tag}_comb", [16, 1024], F32))
        yacc = es.enter_context(nc.sbuf_tensor(f"{tag}_yacc", [128, 8, 1024], F32))
        w13b = es.enter_context(nc.sbuf_tensor(f"{tag}_w13", [128, 2, 8, 1024], BF16))
        w2b = es.enter_context(nc.sbuf_tensor(f"{tag}_w2", [128, 2, 4, 1024], BF16))
        gst = es.enter_context(nc.sbuf_tensor(f"{tag}_gst", [128, 512], F32))
        hbf = es.enter_context(nc.sbuf_tensor(f"{tag}_hbf", [128, 4, 512], BF16))
        rt = es.enter_context(nc.sbuf_tensor(f"{tag}_rt", [128, 8, 16], F32))
        rc = es.enter_context(nc.sbuf_tensor(f"{tag}_rc", [128, 4], F32))
        S.emit("sync", lambda h: h.dma_start(
            out=gw[:], in_=W["gate_wT"].rearrange("(kt p) e -> p kt e", p=128),
            allow_slow_non_contiguous=True),
            writes=(tag + "gw",), dma=True)
        S.emit("sync", lambda h: h.dma_start(out=gbr[:], in_=W["gate_b"]),
               writes=(tag + "gbr",), dma=True)
        S.emit("tensor", lambda h: h.matmul(
            ps[:, 0, 0:16], C.ones_row[:], gbr[:], start=True, stop=True),
            reads=(tag + "gbr", "ones_row"), writes=("psb0",))
        S.emit("scalar", lambda h: h.copy(out=gbb[:], in_=ps[:, 0, 0:16]),
               reads=("psb0",), writes=(tag + "gbb",))
        def _colsum(h):
            for kk in range(8):
                last = h.matmul(ps[0:1, 1, 0:16], C.ones_col[:], gw[:, kk, :],
                                start=(kk == 0), stop=(kk == 7))
            return last
        S.emit("tensor", _colsum, reads=(tag + "gw", "ones_col"), writes=("psb1",))
        S.emit("scalar", lambda h: h.copy(out=Gs[:], in_=ps[0:1, 1, 0:16]),
               reads=("psb1",), writes=(tag + "Gs",))
        for mc in range(2):
            t0mc = mc * 1024
            for half in range(4):
                emit_ln(C, xn, tag + "xn", t0mc + half * 256, 256,
                        f"{tag}{mc}{half}", ps, x2, rows,
                        keep_ab=keep, keep_off=half * 256, keep_key=tag + "keep",
                        out_t0=half * 256)
            for tt in range(8):
                gsl = slice(t0mc + tt * 128, t0mc + (tt + 1) * 128)
                lsl = slice(tt * 128, (tt + 1) * 128)
                def _s1(h, gsl=gsl):
                    for kk in range(8):
                        last = h.matmul(ps[:, 4, 0:16], C.x_res[:, kk, gsl],
                                        gw[:, kk, :],
                                        start=(kk == 0), stop=(kk == 7))
                    return last
                S.emit("tensor", _s1, reads=("x_res", tag + "gw"), writes=("psb4",))
                S.emit("scalar", lambda h: h.copy(out=rt[:, 0, :], in_=ps[:, 4, 0:16]),
                       reads=("psb4",), writes=(tag + "rt",))
                S.emit("tensor", lambda h, lsl=lsl: h.matmul(
                    ps[:, 5, 0:1], keep[0:1, 0, lsl], C.ones_row[0:1, 0:1],
                    start=True, stop=True),
                    reads=(tag + "keep", "ones_row"), writes=("psb5",))
                S.emit("scalar", lambda h: h.copy(out=rc[:, 0:1], in_=ps[:, 5, 0:1]),
                       reads=("psb5",), writes=(tag + "rc",))
                S.emit("tensor", lambda h, lsl=lsl: h.matmul(
                    ps[:, 6, 0:16], keep[0:1, 1, lsl], Gs[:],
                    start=True, stop=True),
                    reads=(tag + "keep", tag + "Gs", "ident_f"), writes=("psb6",))
                S.emit("vector", lambda h: h.scalar_tensor_tensor(
                    out=rt[:, 1, :], in0=rt[:, 0, :], scalar=rc[:, 0:1],
                    in1=ps[:, 6, 0:16], op0=ALU.mult, op1=ALU.add),
                    reads=(tag + "rt", tag + "rc", "psb6"), writes=(tag + "rt",))
                S.emit("scalar", lambda h: h.activation(
                    out=rt[:, 2, :], in_=rt[:, 1, :], func=AF.Sigmoid),
                    reads=(tag + "rt",), writes=(tag + "rt",))
                S.emit("vector", lambda h: h.tensor_add(
                    out=rt[:, 3, :], in0=rt[:, 2, :], in1=gbb[:]),
                    reads=(tag + "rt", tag + "gbb"), writes=(tag + "rt",))
                for k4 in range(4):
                    S.emit("vector", lambda h: h.reduce_max(
                        out=rc[:, 1:2], in_=rt[:, 3, :], axis=AX.X),
                        reads=(tag + "rt",), writes=(tag + "rc",))
                    S.emit("vector", lambda h: h.tensor_scalar(
                        out=rt[:, 5, :], in0=rt[:, 3, :], scalar1=rc[:, 1:2],
                        scalar2=None, op0=ALU.is_equal),
                        reads=(tag + "rt", tag + "rc"), writes=(tag + "rt",))
                    if k4 == 0:
                        S.emit("vector", lambda h: h.tensor_copy(
                            out=rt[:, 4, :], in_=rt[:, 5, :]),
                            reads=(tag + "rt",), writes=(tag + "rt",))
                    else:
                        S.emit("vector", lambda h: h.tensor_add(
                            out=rt[:, 4, :], in0=rt[:, 4, :], in1=rt[:, 5, :]),
                            reads=(tag + "rt",), writes=(tag + "rt",))
                    S.emit("vector", lambda h: h.scalar_tensor_tensor(
                        out=rt[:, 3, :], in0=rt[:, 5, :], scalar=-1e9,
                        in1=rt[:, 3, :], op0=ALU.mult, op1=ALU.add),
                        reads=(tag + "rt",), writes=(tag + "rt",))
                S.emit("vector", lambda h: h.tensor_mul(
                    out=rt[:, 6, :], in0=rt[:, 4, :], in1=rt[:, 2, :]),
                    reads=(tag + "rt",), writes=(tag + "rt",))
                S.emit("vector", lambda h: h.reduce_sum(
                    out=rc[:, 2:3], in_=rt[:, 6, :], axis=AX.X),
                    reads=(tag + "rt",), writes=(tag + "rc",))
                S.emit("vector", lambda h: h.reciprocal(
                    out=rc[:, 3:4], in_=rc[:, 2:3]),
                    reads=(tag + "rc",), writes=(tag + "rc",))
                S.emit("vector", lambda h: h.tensor_scalar(
                    out=rt[:, 7, :], in0=rt[:, 6, :], scalar1=rc[:, 3:4],
                    scalar2=2.5, op0=ALU.mult, op1=ALU.mult),
                    reads=(tag + "rt", tag + "rc"), writes=(tag + "rt",))
                S.emit("tensor", lambda h: h.transpose(
                    ps[0:16, 7, 0:128], rt[:, 7, :], C.ident_f[:]),
                    reads=(tag + "rt", "ident_f"), writes=("psb7",))
                S.emit("scalar", lambda h, lsl=lsl: h.copy(
                    out=comb[:, lsl], in_=ps[0:16, 7, 0:128]),
                    reads=("psb7",), writes=(tag + "comb",))
            for ei in range(17):
                shared = ei == 0
                e = ei - 1
                wsl = ei % 2
                if shared:
                    w13src = W["w13sT"]
                    w2src = W["w2sT"]
                else:
                    w13src = W["w13T"][e]
                    w2src = W["w2T"][e]
                S.emit("gpsimd", lambda h, w13src=w13src, wsl=wsl: h.dma_start(
                    out=w13b[:, wsl, :, :],
                    in_=w13src.rearrange("(kt p) f -> p kt f", p=128)),
                    writes=(f"{tag}w13{wsl}",), dma=True)
                S.emit("gpsimd", lambda h, w2src=w2src, wsl=wsl: h.dma_start(
                    out=w2b[:, wsl, :, :],
                    in_=w2src.rearrange("(kt p) f -> p kt f", p=128)),
                    writes=(f"{tag}w2{wsl}",), dma=True)
                for sub in range(2):
                    ts0 = sub * 512
                    if not shared:
                        S.emit("tensor", lambda h, e=e, ts0=ts0: h.matmul(
                            ps[:, 6, :], C.esel[:, e, :], comb[:, ts0:ts0 + 512],
                            start=True, stop=True),
                            reads=(tag + "comb", "esel"), writes=("psb6",))
                    for ft in range(4):
                        gb = (4, 0)[ft % 2]
                        ub_ = (5, 1)[ft % 2]

                        def _gmm(h, ft=ft, wsl=wsl, ts0=ts0, gb=gb):
                            for kk in range(8):
                                last = h.matmul(
                                    ps[:, gb, :], w13b[:, wsl, kk, ts(ft)],
                                    xn[:, kk, ts0:ts0 + 512],
                                    start=(kk == 0), stop=(kk == 7))
                            return last
                        S.emit("tensor", _gmm,
                               reads=(f"{tag}w13{wsl}", tag + "xn"), writes=(f"psb{gb}",))

                        def _umm(h, ft=ft, wsl=wsl, ts0=ts0, ub_=ub_):
                            for kk in range(8):
                                last = h.matmul(
                                    ps[:, ub_, :],
                                    w13b[:, wsl, kk, 512 + ft * 128:512 + (ft + 1) * 128],
                                    xn[:, kk, ts0:ts0 + 512],
                                    start=(kk == 0), stop=(kk == 7))
                            return last
                        S.emit("tensor", _umm,
                               reads=(f"{tag}w13{wsl}", tag + "xn"), writes=(f"psb{ub_}",))
                        S.emit("scalar", lambda h, gb=gb: h.activation(
                            out=gst[:], in_=ps[:, gb, :], func=AF.Silu),
                            reads=(f"psb{gb}",), writes=(tag + "gst",))
                        S.emit("vector", lambda h, ft=ft, ub_=ub_: h.tensor_tensor(
                            out=hbf[:, ft, :], in0=gst[:], in1=ps[:, ub_, :],
                            op=ALU.mult),
                            reads=(tag + "gst", f"psb{ub_}"), writes=(tag + "hbf",))
                        if not shared:
                            S.emit("vector", lambda h, ft=ft: h.tensor_tensor(
                                out=hbf[:, ft, :], in0=hbf[:, ft, :], in1=ps[:, 6, :],
                                op=ALU.mult),
                                reads=(tag + "hbf", "psb6"), writes=(tag + "hbf",))
                    for i in range(8):
                        mb2 = (7, 3)[i % 2]

                        def _mm2(h, i=i, wsl=wsl, mb2=mb2):
                            for kk in range(4):
                                last = h.matmul(
                                    ps[:, mb2, :], w2b[:, wsl, kk, ts(i)], hbf[:, kk, :],
                                    start=(kk == 0), stop=(kk == 3))
                            return last
                        S.emit("tensor", _mm2,
                               reads=(f"{tag}w2{wsl}", tag + "hbf"), writes=(f"psb{mb2}",))
                        if shared:
                            S.emit("vector", lambda h, i=i, ts0=ts0, mb2=mb2: h.tensor_copy(
                                out=yacc[:, i, ts0:ts0 + 512], in_=ps[:, mb2, :]),
                                reads=(f"psb{mb2}",), writes=(tag + "yacc",))
                        else:
                            S.emit("vector", lambda h, i=i, ts0=ts0, mb2=mb2: h.tensor_add(
                                out=yacc[:, i, ts0:ts0 + 512],
                                in0=yacc[:, i, ts0:ts0 + 512], in1=ps[:, mb2, :]),
                                reads=(tag + "yacc", f"psb{mb2}"), writes=(tag + "yacc",))
            for i in range(8):
                S.emit("vector", lambda h, i=i, t0mc=t0mc: h.tensor_add(
                    out=C.x_res[:, i, t0mc:t0mc + 1024],
                    in0=C.x_res[:, i, t0mc:t0mc + 1024], in1=yacc[:, i, :]),
                    reads=(tag + "yacc", "x_res"), writes=("x_res",))
        S.barrier()
        S.flush()



def emit_l1ffn(C, tag):
    import contextlib
    nc, S = C.nc, C.S
    W = C.dram_w["l1f"]
    with contextlib.ExitStack() as es:
        alloc_esel(C, es, tag)
        ps = es.enter_context(nc.psum_tensor(f"{tag}_ps", [128, 8, 512], F32))
        x2 = es.enter_context(nc.sbuf_tensor(f"{tag}_x2", [128, 8, 128], F32))
        rows = es.enter_context(nc.sbuf_tensor(f"{tag}_rows", [1, 6, 128], F32))
        gw = es.enter_context(nc.sbuf_tensor(f"{tag}_gw", [128, 8, 8], F32))
        gbr = es.enter_context(nc.sbuf_tensor(f"{tag}_gbr", [1, 8], F32))
        Gs = es.enter_context(nc.sbuf_tensor(f"{tag}_Gs", [1, 8], F32))
        b1s = es.enter_context(nc.sbuf_tensor(f"{tag}_b1", [128, 32, 8], F32))
        b2s = es.enter_context(nc.sbuf_tensor(f"{tag}_b2", [8, 1024], F32))
        xn = es.enter_context(nc.sbuf_tensor(f"{tag}_xn", [128, 8, 1024], BF16))
        keep = es.enter_context(nc.sbuf_tensor(f"{tag}_keep", [1, 2, 1024], F32))
        cfm = es.enter_context(nc.sbuf_tensor(f"{tag}_cfm", [8, 1024], F32))
        yacc = es.enter_context(nc.sbuf_tensor(f"{tag}_yacc", [128, 8, 1024], F32))
        wb = es.enter_context(nc.sbuf_tensor(f"{tag}_wb", [128, 8, 1024], BF16))
        wb2 = es.enter_context(nc.sbuf_tensor(f"{tag}_wb2", [128, 8, 1024], BF16))
        hblk = es.enter_context(nc.sbuf_tensor(f"{tag}_hblk", [128, 8, 1024], BF16))
        bcs = es.enter_context(nc.sbuf_tensor(f"{tag}_bcs", [128, 1024], BF16))
        rt = es.enter_context(nc.sbuf_tensor(f"{tag}_rt", [128, 8, 8], F32))
        rc = es.enter_context(nc.sbuf_tensor(f"{tag}_rc", [128, 4], F32))
        S.emit("sync", lambda h: h.dma_start(
            out=gw[:], in_=W["w_gate"].rearrange("(kt p) e -> p kt e", p=128),
            allow_slow_non_contiguous=True),
            writes=(tag + "gw",), dma=True)
        S.emit("sync", lambda h: h.dma_start(out=gbr[:], in_=W["b_gate"]),
               writes=(tag + "gbr",), dma=True)
        S.emit("sync", lambda h: h.dma_start(
            out=b1s[:], in_=W["b1p"]), writes=(tag + "b1",), dma=True)
        S.emit("sync", lambda h: h.dma_start(out=b2s[:], in_=W["b2"]),
               writes=(tag + "b2",), dma=True)
        def _colsum(h):
            for kk in range(8):
                last = h.matmul(ps[0:1, 1, 0:8], C.ones_col[:], gw[:, kk, :],
                                start=(kk == 0), stop=(kk == 7))
            return last
        S.emit("tensor", _colsum, reads=(tag + "gw", "ones_col"), writes=("psb1",))
        S.emit("scalar", lambda h: h.copy(out=Gs[:], in_=ps[0:1, 1, 0:8]),
               reads=("psb1",), writes=(tag + "Gs",))
        for mc in range(2):
            t0mc = mc * 1024
            for q in range(8):
                emit_ln(C, xn, tag + "xn", t0mc + q * 128, 128,
                        f"{tag}{mc}{q}", ps, x2, rows,
                        keep_ab=keep, keep_off=q * 128, keep_key=tag + "keep",
                        out_t0=q * 128)
            for tt in range(8):
                gsl = slice(t0mc + tt * 128, t0mc + (tt + 1) * 128)
                lsl = slice(tt * 128, (tt + 1) * 128)
                def _s1(h, gsl=gsl):
                    for kk in range(8):
                        last = h.matmul(ps[:, 4, 0:8], C.x_res[:, kk, gsl],
                                        gw[:, kk, :],
                                        start=(kk == 0), stop=(kk == 7))
                    return last
                S.emit("tensor", _s1, reads=("x_res", tag + "gw"), writes=("psb4",))
                S.emit("scalar", lambda h: h.copy(out=rt[:, 0, :], in_=ps[:, 4, 0:8]),
                       reads=("psb4",), writes=(tag + "rt",))
                S.emit("tensor", lambda h, lsl=lsl: h.matmul(
                    ps[:, 5, 0:1], keep[0:1, 0, lsl], C.ones_row[0:1, 0:1],
                    start=True, stop=True),
                    reads=(tag + "keep", "ones_row"), writes=("psb5",))
                S.emit("scalar", lambda h: h.copy(out=rc[:, 0:1], in_=ps[:, 5, 0:1]),
                       reads=("psb5",), writes=(tag + "rc",))
                S.emit("tensor", lambda h, lsl=lsl: h.matmul(
                    ps[:, 6, 0:8], keep[0:1, 1, lsl], Gs[:],
                    start=True, stop=False),
                    reads=(tag + "keep", tag + "Gs"), writes=("psb6",))
                S.emit("tensor", lambda h: h.matmul(
                    ps[:, 6, 0:8], C.ones_row[:], gbr[:],
                    start=False, stop=True),
                    reads=(tag + "gbr", "ones_row"), writes=("psb6",))
                S.emit("vector", lambda h: h.scalar_tensor_tensor(
                    out=rt[:, 1, :], in0=rt[:, 0, :], scalar=rc[:, 0:1],
                    in1=ps[:, 6, 0:8], op0=ALU.mult, op1=ALU.add),
                    reads=(tag + "rt", tag + "rc", "psb6"), writes=(tag + "rt",))
                S.emit("vector", lambda h: h.reduce_max(
                    out=rc[:, 1:2], in_=rt[:, 1, :], axis=AX.X, negate=True),
                    reads=(tag + "rt",), writes=(tag + "rc",))
                S.emit("scalar", lambda h: h.activation(
                    out=rt[:, 2, :], in_=rt[:, 1, :], func=AF.Exp,
                    bias=rc[:, 1:2]),
                    reads=(tag + "rt", tag + "rc"), writes=(tag + "rt",))
                S.emit("vector", lambda h: h.reduce_sum(
                    out=rc[:, 2:3], in_=rt[:, 2, :], axis=AX.X),
                    reads=(tag + "rt",), writes=(tag + "rc",))
                S.emit("vector", lambda h: h.reciprocal(
                    out=rc[:, 3:4], in_=rc[:, 2:3]),
                    reads=(tag + "rc",), writes=(tag + "rc",))
                S.emit("vector", lambda h: h.tensor_scalar_mul(
                    out=rt[:, 3, :], in0=rt[:, 2, :], scalar1=rc[:, 3:4]),
                    reads=(tag + "rt", tag + "rc"), writes=(tag + "rt",))
                S.emit("vector", lambda h: h.reduce_max(
                    out=rc[:, 1:2], in_=rt[:, 3, :], axis=AX.X),
                    reads=(tag + "rt", tag + "rc"), writes=(tag + "rc",))
                S.emit("vector", lambda h: h.tensor_scalar(
                    out=rt[:, 4, :], in0=rt[:, 3, :], scalar1=rc[:, 1:2],
                    scalar2=None, op0=ALU.is_equal),
                    reads=(tag + "rt", tag + "rc"), writes=(tag + "rt",))
                S.emit("vector", lambda h: h.tensor_scalar_mul(
                    out=rt[:, 5, :], in0=rt[:, 4, :], scalar1=rc[:, 1:2]),
                    reads=(tag + "rt", tag + "rc"), writes=(tag + "rt",))
                S.emit("tensor", lambda h: h.transpose(
                    ps[0:8, 7, 0:128], rt[:, 5, :], C.ident_f[:]),
                    reads=(tag + "rt", "ident_f"), writes=("psb7",))
                S.emit("scalar", lambda h, lsl=lsl: h.copy(
                    out=cfm[:, lsl], in_=ps[0:8, 7, 0:128]),
                    reads=("psb7",), writes=(tag + "cfm",))
            for i in range(8):
                for half in range(2):
                    hsl = slice(half * 512, (half + 1) * 512)
                    S.emit("tensor", lambda h, i=i, hsl=hsl: h.matmul(
                        ps[:, 2, :], b2s[:, ts(i)], cfm[:, hsl],
                        start=True, stop=True),
                        reads=(tag + "b2", tag + "cfm"), writes=("psb2",))
                    S.emit("vector", lambda h, i=i, hsl=hsl: h.tensor_copy(
                        out=yacc[:, i, hsl], in_=ps[:, 2, :]),
                        reads=("psb2",), writes=(tag + "yacc",))
            for e in range(8):
                for half in range(2):
                    hsl = slice(half * 512, (half + 1) * 512)
                    S.emit("tensor", lambda h, e=e, hsl=hsl, half=half: h.matmul(
                        ps[:, 2 + half, :], C.esel[0:8, e, :], cfm[:, hsl],
                        start=True, stop=True),
                        reads=(tag + "cfm", "esel"), writes=(f"psb{2 + half}",))
                    S.emit("scalar", lambda h, hsl=hsl, half=half: h.copy(
                        out=bcs[:, hsl], in_=ps[:, 2 + half, :]),
                        reads=(f"psb{2 + half}",), writes=(tag + "bcs",))
                for mb in range(4):
                    S.emit("gpsimd", lambda h, e=e, mb=mb: h.dma_start(
                        out=wb[:], in_=W["W1"][e].rearrange(
                            "(kt p) m -> p kt m", p=128)[:, :, ts(mb, 1024)]),
                        writes=(tag + "wb",), dma=True)
                    S.emit("gpsimd", lambda h, e=e, mb=mb: h.dma_start(
                        out=wb2[:], in_=W["W2"][e].rearrange(
                            "(kt p) d -> p kt d", p=128, kt=32)[:, mb * 8:(mb + 1) * 8, :]),
                        writes=(tag + "wb2",), dma=True)
                    for mt in range(8):
                        for half in range(2):
                            hsl = slice(half * 512, (half + 1) * 512)
                            pb = (2 * mt + half) % 3

                            def _mm1(h, mt=mt, hsl=hsl, pb=pb):
                                for kk in range(8):
                                    last = h.matmul(
                                        ps[:, pb, :], wb[:, kk, ts(mt)],
                                        xn[:, kk, hsl],
                                        start=(kk == 0), stop=(kk == 7))
                                return last
                            S.emit("tensor", _mm1,
                                   reads=(tag + "wb", tag + "xn"),
                                   writes=(f"psb{pb}",))
                            S.emit("scalar", lambda h, mt=mt, mb=mb, e=e, hsl=hsl, pb=pb: h.activation(
                                out=hblk[:, mt, hsl], in_=ps[:, pb, :], func=AF.Relu,
                                bias=b1s[:, mb * 8 + mt, e:e + 1]),
                                reads=(f"psb{pb}", tag + "b1"), writes=(tag + "hblk",))
                        S.emit("vector", lambda h, mt=mt: h.tensor_tensor(
                            out=hblk[:, mt, :], in0=hblk[:, mt, :], in1=bcs[:],
                            op=ALU.mult),
                            reads=(tag + "hblk", tag + "bcs"), writes=(tag + "hblk",))
                    for i in range(8):
                        for half in range(2):
                            hsl = slice(half * 512, (half + 1) * 512)
                            pb = 4 + (2 * i + half) % 3

                            def _mm2(h, i=i, hsl=hsl, pb=pb):
                                for kk in range(8):
                                    last = h.matmul(
                                        ps[:, pb, :], wb2[:, kk, ts(i)],
                                        hblk[:, kk, hsl],
                                        start=(kk == 0), stop=(kk == 7))
                                return last
                            S.emit("tensor", _mm2,
                                   reads=(tag + "wb2", tag + "hblk"),
                                   writes=(f"psb{pb}",))
                            S.emit("vector", lambda h, i=i, hsl=hsl, pb=pb: h.tensor_add(
                                out=yacc[:, i, hsl], in0=yacc[:, i, hsl],
                                in1=ps[:, pb, :]),
                                reads=(tag + "yacc", f"psb{pb}"), writes=(tag + "yacc",))
            for i in range(8):
                S.emit("vector", lambda h, i=i, t0mc=t0mc: h.tensor_add(
                    out=C.x_res[:, i, t0mc:t0mc + 1024],
                    in0=C.x_res[:, i, t0mc:t0mc + 1024], in1=yacc[:, i, :]),
                    reads=(tag + "yacc", "x_res"), writes=("x_res",))
        S.barrier()
        S.flush()



def emit_l1a(C, tag="l1a"):
    import contextlib
    nc, S = C.nc, C.S
    W = C.dram_w["l1a"]
    p_dram = C.dram["l1a_p"]
    r_dram = C.dram["l1a_r"]
    NIT = 41
    with contextlib.ExitStack() as es:
        ps = es.enter_context(nc.psum_tensor(f"{tag}_psA", [128, 8, 512], F32))
        x2 = es.enter_context(nc.sbuf_tensor(f"{tag}_x2", [128, 8, 512], F32))
        rows = es.enter_context(nc.sbuf_tensor(f"{tag}_rows", [1, 6, 512], F32))
        xnorm = es.enter_context(nc.sbuf_tensor(f"{tag}_xn", [128, 8, T_FULL], BF16))
        wblk = es.enter_context(nc.sbuf_tensor(f"{tag}_wblk", [128, 2, 8, 512], BF16))
        evb = es.enter_context(nc.sbuf_tensor(f"{tag}_evb", [128, 2, 4, 512], BF16))
        for ch in range(4):
            emit_ln(C, xnorm, "xnorm", ch * 512, 512, tag + str(ch), ps, x2, rows)
        for proj, src_w, n_i, dst in (("p", W["w_inT"], NIT, p_dram),
                                      ("r", W["w_resT"], 16, r_dram)):
            nblk = (n_i + 3) // 4
            for ib in range(nblk):
                i0 = ib * 4
                ni = min(4, n_i - i0)
                wsl = ib % 2
                S.emit("gpsimd", lambda h, src_w=src_w, i0=i0, ni=ni, wsl=wsl: h.dma_start(
                    out=wblk[:, wsl, :, 0:ni * 128],
                    in_=src_w.rearrange("(kt p) i -> p kt i", p=128)[:, :, i0 * 128:(i0 + ni) * 128]),
                    writes=(f"{tag}_wblk{wsl}",), dma=True)
                for ch in range(4):
                    esl = (ib * 4 + ch) % 2
                    for i in range(ni):
                        pb = (ib * 4 + i) % 8

                        def _inp(h, ch=ch, i=i, wsl=wsl, pb=pb):
                            for kk in range(8):
                                last = h.matmul(
                                    ps[:, pb, :], wblk[:, wsl, kk, ts(i)],
                                    xnorm[:, kk, ch * 512:(ch + 1) * 512],
                                    start=(kk == 0), stop=(kk == 7))
                            return last
                        S.emit("tensor", _inp,
                               reads=(f"{tag}_wblk{wsl}", "xnorm"),
                               writes=(f"psb{pb}",))
                        S.emit("scalar", lambda h, i=i, esl=esl, pb=pb: h.copy(
                            out=evb[:, esl, i, :], in_=ps[:, pb, :]),
                            reads=(f"psb{pb}",), writes=(f"{tag}_evb{esl}",))
                    S.emit("sync", lambda h, dst=dst, i0=i0, ni=ni, ch=ch, esl=esl: h.dma_start(
                        out=dst[i0 * 128:(i0 + ni) * 128, ch * 512:(ch + 1) * 512].rearrange(
                            "(i p) t -> p i t", p=128),
                        in_=evb[:, esl, 0:ni, :]),
                        reads=(f"{tag}_evb{esl}",), writes=(f"{tag}_{proj}d",), dma=True)
        S.barrier()
        S.flush()
    S.barrier()
    tc = 256
    with contextlib.ExitStack() as es:
        alloc_esel(C, es, tag)
        ps = es.enter_context(nc.psum_tensor(f"{tag}_psB", [128, 8, 512], F32))
        wout = es.enter_context(nc.sbuf_tensor(f"{tag}_wout", [128, 16, 512], BF16))
        wdt = es.enter_context(nc.sbuf_tensor(f"{tag}_wdt", [64, 8], BF16))
        convw = es.enter_context(nc.sbuf_tensor(f"{tag}_cw", [128, 16, 4], F32))
        convb = es.enter_context(nc.sbuf_tensor(f"{tag}_cb", [128, 16], F32))
        dtb = es.enter_context(nc.sbuf_tensor(f"{tag}_dtb", [8, 1], F32))
        Afl = es.enter_context(nc.sbuf_tensor(f"{tag}_Afl", [1, 128], F32))
        Abc = es.enter_context(nc.sbuf_tensor(f"{tag}_Abc", [128, 128], F32))
        Dsk = es.enter_context(nc.sbuf_tensor(f"{tag}_Dsk", [128, 16], F32))
        nrmw = es.enter_context(nc.sbuf_tensor(f"{tag}_nw", [128, 16], F32))
        carry = es.enter_context(nc.sbuf_tensor(f"{tag}_carry", [128, 256], F32))
        xh = es.enter_context(nc.sbuf_tensor(f"{tag}_xh", [128, 16, tc + 4], BF16))
        ub = es.enter_context(nc.sbuf_tensor(f"{tag}_u", [128, 16, tc], F32))
        dts = es.enter_context(nc.sbuf_tensor(f"{tag}_dts", [8, tc], F32))
        dtr = es.enter_context(nc.sbuf_tensor(f"{tag}_dtr", [64, tc], BF16))
        Bf = es.enter_context(nc.sbuf_tensor(f"{tag}_Bf", [16, 8, tc], BF16))
        eselb = es.enter_context(nc.sbuf_tensor(f"{tag}_eselb", [16, 16, 128], BF16))
        Cf = es.enter_context(nc.sbuf_tensor(f"{tag}_Cf", [16, 8, tc], BF16))
        du = es.enter_context(nc.sbuf_tensor(f"{tag}_du", [128, 16, tc], F32))
        ybuf = es.enter_context(nc.sbuf_tensor(f"{tag}_y", [128, 16, tc], F32))
        zb = es.enter_context(nc.sbuf_tensor(f"{tag}_z", [128, 16, tc], BF16))
        rbuf = es.enter_context(nc.sbuf_tensor(f"{tag}_rin", [128, 16, tc], BF16))
        gnb = es.enter_context(nc.sbuf_tensor(f"{tag}_gn", [128, 16, tc], BF16))
        dAs = es.enter_context(nc.sbuf_tensor(f"{tag}_dAs", [128, 2, tc], F32))
        Xs = es.enter_context(nc.sbuf_tensor(f"{tag}_Xs", [128, 4, tc], BF16))
        hs = es.enter_context(nc.sbuf_tensor(f"{tag}_hs", [128, 2, 2, tc], F32))
        dubuf = es.enter_context(nc.sbuf_tensor(f"{tag}_dub", [128, 16, tc], BF16))
        bcBs = es.enter_context(nc.sbuf_tensor(f"{tag}_bcBs", [128, 2, tc], BF16))
        bcCs = es.enter_context(nc.sbuf_tensor(f"{tag}_bcCs", [128, 2, tc], BF16))
        rrow = es.enter_context(nc.sbuf_tensor(f"{tag}_rrow", [1, 2, tc], F32))
        mk = tag + "m"
        for name, dst, key in (("conv_w", convw, "cw"), ("conv_b", convb, "cb"),
                               ("dt_bias", dtb, "dtb"), ("A_flat", Afl, "Afl"),
                               ("Dskip", Dsk, "D"), ("norm_w", nrmw, "nw")):
            S.emit("sync", lambda h, name=name, dst=dst: h.dma_start(
                out=dst[:], in_=W[name]), writes=(mk + key,), dma=True)
        S.emit("gpsimd", lambda h: h.dma_start(out=wdt[:], in_=W["w_dtT"]),
               writes=(mk + "wdt",), dma=True)
        S.emit("scalar", lambda h: h.activation(out=Afl[:], in_=Afl[:], func=AF.Exp),
               reads=(mk + "Afl",), writes=(mk + "Afl",))
        S.emit("vector", lambda h: h.tensor_scalar_mul(
            out=Afl[:], in0=Afl[:], scalar1=-1.0),
            reads=(mk + "Afl",), writes=(mk + "Afl",))
        S.emit("tensor", lambda h: h.matmul(
            ps[:, 0, 0:128], C.ones_row[:], Afl[:], start=True, stop=True),
            reads=(mk + "Afl", "ones_row"), writes=("psb0",))
        S.emit("scalar", lambda h: h.copy(out=Abc[:], in_=ps[:, 0, 0:128]),
               reads=("psb0",), writes=(mk + "Abc",))
        S.emit("vector", lambda h: h.memset(carry[:], 0.0), writes=(mk + "carry",))
        S.emit("scalar", lambda h: h.copy(out=eselb[:], in_=C.esel[:]),
               reads=("esel",), writes=(mk + "eselb",))
        pr = p_dram.rearrange("(j p) t -> p j t", p=128)
        rr = r_dram.rearrange("(j p) t -> p j t", p=128)
        for ch in range(8):
            t0 = ch * tc
            if ch == 0:
                S.emit("vector", lambda h: h.memset(xh[:, :, 0:3], 0.0),
                       writes=(mk + "xh",))
                S.emit("sync", lambda h: h.dma_start(
                    out=xh[:, :, 3:3 + tc], in_=pr[:, 16:32, 0:tc]),
                    reads=(f"{tag}_pd",), writes=(mk + "xh",), dma=True)
            else:
                S.emit("sync", lambda h, t0=t0: h.dma_start(
                    out=xh[:, :, 0:3 + tc], in_=pr[:, 16:32, t0 - 3:t0 + tc]),
                    reads=(f"{tag}_pd",), writes=(mk + "xh",), dma=True)
            S.emit("sync", lambda h, t0=t0: h.dma_start(
                out=dtr[:], in_=pr[0:64, 32, t0:t0 + tc]),
                reads=(f"{tag}_pd",), writes=(mk + "dtr",), dma=True)
            S.emit("sync", lambda h, t0=t0: h.dma_start(
                out=gnb[:, 0:4, :], in_=pr[:, 33:37, t0:t0 + tc]),
                reads=(f"{tag}_pd",), writes=(mk + "gn",), dma=True)
            S.emit("sync", lambda h, t0=t0: h.dma_start(
                out=gnb[:, 4:8, :], in_=pr[:, 37:41, t0:t0 + tc]),
                reads=(f"{tag}_pd",), writes=(mk + "gn",), dma=True)
            for j in range(16):
                S.emit("vector", lambda h, j=j: h.tensor_scalar(
                    out=ub[:, j, :], in0=xh[:, j, 0:tc], scalar1=convw[:, j, 0:1],
                    scalar2=convb[:, j:j + 1], op0=ALU.mult, op1=ALU.add),
                    reads=(mk + "xh", mk + "cw", mk + "cb"), writes=(mk + "u",))
                for k4 in (1, 2, 3):
                    S.emit("vector", lambda h, j=j, k4=k4: h.scalar_tensor_tensor(
                        out=ub[:, j, :], in0=xh[:, j, k4:k4 + tc],
                        scalar=convw[:, j, k4:k4 + 1], in1=ub[:, j, :],
                        op0=ALU.mult, op1=ALU.add),
                        reads=(mk + "xh", mk + "cw", mk + "u"), writes=(mk + "u",))
            for g in range(8):
                S.emit("scalar", lambda h, g=g: h.copy(
                    out=Bf[:, g, :], in_=gnb[64 * (g % 2):64 * (g % 2) + 16, g // 2, :]),
                    reads=(mk + "gn",), writes=(mk + "Bf",))
                S.emit("scalar", lambda h, g=g: h.copy(
                    out=Cf[:, g, :], in_=gnb[64 * (g % 2):64 * (g % 2) + 16, 4 + g // 2, :]),
                    reads=(mk + "gn",), writes=(mk + "Cf",))
            S.emit("tensor", lambda h: h.matmul(
                ps[0:8, 6, :tc], wdt[:, :], dtr[:],
                start=True, stop=True),
                reads=(mk + "dtr", mk + "wdt"), writes=("psb6",))
            S.emit("scalar", lambda h: h.activation(
                out=dts[:, :],
                in_=ps[0:8, 6, :tc], func=AF.Identity, bias=dtb[:, 0:1]),
                reads=("psb6", mk + "dtb"), writes=(mk + "dts",))
            S.emit("vector", lambda h: h.tensor_scalar_max(
                out=du[0:8, 0, :], in0=dts[:, :], scalar1=0.0),
                reads=(mk + "dts",), writes=(mk + "duscr",))
            S.emit("scalar", lambda h: h.activation(
                out=dts[:, :], in_=dts[:, :], func=AF.Abs),
                reads=(mk + "dts",), writes=(mk + "dts",))
            S.emit("scalar", lambda h: h.activation(
                out=dts[:, :], in_=dts[:, :], func=AF.Exp, scale=-1.0),
                reads=(mk + "dts",), writes=(mk + "dts",))
            S.emit("scalar", lambda h: h.activation(
                out=dts[:, :], in_=dts[:, :], func=AF.Ln, bias=1.0),
                reads=(mk + "dts",), writes=(mk + "dts",))
            S.emit("vector", lambda h: h.tensor_add(
                out=dts[:, :], in0=dts[:, :], in1=du[0:8, 0, :]),
                reads=(mk + "dts", mk + "duscr"), writes=(mk + "dts",))
            S.emit("vector", lambda h: h.tensor_scalar(
                out=dts[:, :], in0=dts[:, :], scalar1=1e-4, scalar2=1.0,
                op0=ALU.max, op1=ALU.min),
                reads=(mk + "dts",), writes=(mk + "dts",))
            for j in range(16):
                S.emit("vector", lambda h, j=j: h.tensor_scalar_mul(
                    out=ybuf[:, j, :], in0=ub[:, j, :], scalar1=Dsk[:, j:j + 1]),
                    reads=(mk + "u", mk + "D"), writes=(mk + "y",))
            for hh in range(8):
                hb = hh % 2
                S.emit("tensor", lambda h, hh=hh, hb=hb: h.matmul(
                    ps[:, hb, :tc], C.esel[0:8, hh, :], dts[:, :],
                    start=True, stop=True),
                    reads=(mk + "dts", "esel"), writes=(f"psb{hb}",))
                for jo in range(2):
                    j = 2 * hh + jo
                    S.emit("vector", lambda h, j=j, hb=hb: h.tensor_tensor(
                        out=dubuf[:, j, :], in0=ub[:, j, :], in1=ps[:, hb, :tc],
                        op=ALU.mult),
                        reads=(mk + "u", f"psb{hb}"), writes=(mk + "dub",))
                for n in range(16):
                    bb = 2 + n % 2
                    cb2 = 4 + n % 2
                    par = n % 2
                    S.emit("tensor", lambda h, hh=hh, n=n, bb=bb: h.matmul(
                        ps[:, bb, :tc], eselb[:, n, :],
                        Bf[:, hh, :],
                        start=True, stop=True),
                        reads=(mk + "Bf", mk + "eselb"), writes=(f"psb{bb}",))
                    S.emit("tensor", lambda h, hh=hh, n=n, cb2=cb2: h.matmul(
                        ps[:, cb2, :tc], eselb[:, n, :],
                        Cf[:, hh, :],
                        start=True, stop=True),
                        reads=(mk + "Cf", mk + "eselb"), writes=(f"psb{cb2}",))
                    S.emit("scalar", lambda h, bb=bb, par=par: h.copy(
                        out=bcBs[:, par, :], in_=ps[:, bb, :tc]),
                        reads=(f"psb{bb}",), writes=(f"{mk}bcBs{par}",))
                    dsl = n % 2
                    S.emit("scalar", lambda h, hh=hh, n=n, dsl=dsl, hb=hb: h.activation(
                        out=dAs[:, dsl, :], in_=ps[:, hb, :tc], func=AF.Exp,
                        scale=Abc[:, hh * 16 + n:hh * 16 + n + 1]),
                        reads=(f"psb{hb}", mk + "Abc"), writes=(f"{mk}dA{dsl}",))
                    def _xg(h, hh=hh, par=par):
                        for jo in range(2):
                            last = h.tensor_tensor(
                                out=Xs[:, 2 * par + jo, :], in0=dubuf[:, 2 * hh + jo, :],
                                in1=bcBs[:, par, :], op=ALU.mult)
                        return last
                    S.emit("vector", _xg, reads=(mk + "dub", f"{mk}bcBs{par}"),
                           writes=(f"{mk}Xg{par}",))

                    def _sg(h, hh=hh, n=n, par=par, dsl=dsl):
                        for jo in range(2):
                            j = 2 * hh + jo
                            last = h.tensor_tensor_scan(
                                out=hs[:, par, jo, :], data0=dAs[:, dsl, :],
                                data1=Xs[:, 2 * par + jo, :],
                                initial=carry[:, j * 16 + n:j * 16 + n + 1],
                                op0=ALU.mult, op1=ALU.add)
                        return last
                    S.emit("vector", _sg,
                           reads=(f"{mk}dA{dsl}", f"{mk}Xg{par}", mk + "carry"),
                           writes=(f"{mk}h{par}0", f"{mk}h{par}1"))
                    S.emit("vector", lambda h, hh=hh, n=n, par=par: h.tensor_copy(
                        out=carry.ap()[:, 2 * hh * 16 + n:(2 * hh + 2) * 16:16],
                        in_=hs.ap()[:, par, :, tc - 1]),
                        reads=(f"{mk}h{par}0", f"{mk}h{par}1"),
                        writes=(mk + "carry",))
                    S.emit("vector", lambda h, par=par, cb2=cb2: h.tensor_tensor(
                        out=hs[:, par, :, :], in0=hs[:, par, :, :],
                        in1=ps.ap()[:, cb2:cb2 + 1, :tc].broadcast_to([128, 2, tc]),
                        op=ALU.mult),
                        reads=(f"{mk}h{par}0", f"{mk}h{par}1", f"psb{cb2}"),
                        writes=(f"{mk}h{par}0", f"{mk}h{par}1")),
                    S.emit("gpsimd", lambda h, hh=hh, par=par: h.tensor_add(
                        out=ybuf[:, 2 * hh:2 * hh + 2, :], in0=ybuf[:, 2 * hh:2 * hh + 2, :],
                        in1=hs[:, par, :, :]),
                        reads=(f"{mk}h{par}0", f"{mk}h{par}1", mk + "y"),
                        writes=(mk + "y", f"{mk}hsG{par}"))
            S.emit("sync", lambda h, t0=t0: h.dma_start(
                out=zb[:], in_=pr[:, 0:16, t0:t0 + tc]),
                reads=(f"{tag}_pd",), writes=(mk + "z",), dma=True)
            S.emit("sync", lambda h, t0=t0: h.dma_start(
                out=rbuf[:], in_=rr[:, 0:16, t0:t0 + tc]),
                reads=(f"{tag}_rd",), writes=(mk + "rb",), dma=True)
            S.emit("scalar", lambda h: h.activation(
                out=zb[:], in_=zb[:], func=AF.Silu),
                reads=(mk + "z",), writes=(mk + "z",))
            S.emit("vector", lambda h: h.tensor_mul(
                out=ybuf[:], in0=ybuf[:], in1=zb[:]),
                reads=(mk + "y", mk + "z"), writes=(mk + "y",))
            S.emit("vector", lambda h: h.tensor_add(
                out=ybuf[:], in0=ybuf[:], in1=rbuf[:]),
                reads=(mk + "y", mk + "rb"), writes=(mk + "y",))
            S.emit("scalar", lambda h: h.activation(
                out=du[:], in_=ybuf[:], func=AF.Square),
                reads=(mk + "y",), writes=(mk + "du",))
            def _rms(h):
                for j in range(16):
                    last = h.matmul(ps[0:1, 6, :tc], C.ones_col[:], du[:, j, :],
                                    start=(j == 0), stop=(j == 15))
                return last
            S.emit("tensor", _rms, reads=(mk + "du", "ones_col"), writes=("psb6",))
            S.emit("vector", lambda h: h.tensor_scalar_mul(
                out=rrow[:, 0, :], in0=ps[0:1, 6, :tc], scalar1=1.0 / 2048.0),
                reads=("psb6",), writes=(mk + "rrow",))
            S.emit("scalar", lambda h: h.activation(
                out=rrow[:, 1, :], in_=rrow[:, 0, :], func=AF.Sqrt,
                bias=C.cpool[0:1, 1:2]),
                reads=(mk + "rrow", "cpool"), writes=(mk + "rrow",))
            S.emit("vector", lambda h: h.reciprocal(
                out=rrow[:, 0, :], in_=rrow[:, 1, :]),
                reads=(mk + "rrow",), writes=(mk + "rrow",))
            S.emit("tensor", lambda h: h.matmul(
                ps[:, 7, :tc], C.ones_row[:], rrow[:, 0, :], start=True, stop=True),
                reads=(mk + "rrow", "ones_row"), writes=("psb7",))
            for j in range(16):
                S.emit("vector", lambda h, j=j: h.tensor_scalar_mul(
                    out=du[:, j, :], in0=ybuf[:, j, :], scalar1=nrmw[:, j:j + 1]),
                    reads=(mk + "y", mk + "nw"), writes=(mk + "du",))
            S.emit("vector", lambda h: h.tensor_tensor(
                out=gnb[:], in0=du[:],
                in1=ps.ap()[:, 7:8, :tc].broadcast_to([128, 16, tc]), op=ALU.mult),
                reads=(mk + "du", "psb7"), writes=(mk + "gn",))
            for ihalf in range(2):
                S.emit("gpsimd", lambda h, ihalf=ihalf: h.dma_start(
                    out=wout[:], in_=W["w_outT"].rearrange(
                        "(kt p) i -> p kt i", p=128)[:, :, ihalf * 512:(ihalf + 1) * 512]),
                    writes=(mk + "wout",), dma=True)
                for i in range(4):
                    ig = ihalf * 4 + i
                    pb = 2 + i % 4

                    def _outp(h, i=i, pb=pb):
                        for kk in range(16):
                            last = h.matmul(
                                ps[:, pb, :tc], wout[:, kk, ts(i)], gnb[:, kk, :],
                                start=(kk == 0), stop=(kk == 15))
                        return last
                    S.emit("tensor", _outp, reads=(mk + "wout", mk + "gn"),
                           writes=(f"psb{pb}",))
                    S.emit("vector", lambda h, ig=ig, t0=t0, pb=pb: h.tensor_add(
                        out=C.x_res[:, ig, t0:t0 + tc], in0=C.x_res[:, ig, t0:t0 + tc],
                        in1=ps[:, pb, :tc]),
                        reads=(f"psb{pb}", "x_res"), writes=("x_res",))
        S.barrier()
        S.flush()



def emit_l1b(C, tag="l1b"):
    import contextlib
    nc, S = C.nc, C.S
    W = C.dram_w["l1b"]
    tc = 512
    with contextlib.ExitStack() as es:
        ps_ = es.enter_context(nc.psum_tensor(f"{tag}_ps", [128, 8, 512], F32))
        ps = ps_
        psb16 = ps_.bitcast(BF16)
        x2 = es.enter_context(nc.sbuf_tensor(f"{tag}_x2", [128, 8, 128], F32))
        rows = es.enter_context(nc.sbuf_tensor(f"{tag}_rows", [1, 6, 128], F32))
        xn = es.enter_context(nc.sbuf_tensor(f"{tag}_xn", [128, 8, T_FULL], BF16))
        kb = es.enter_context(nc.sbuf_tensor(f"{tag}_kb", [128, 8, T_FULL], BF16))
        vt = es.enter_context(nc.sbuf_tensor(f"{tag}_vt", [128, 16, 1040], BF16))
        obuf = es.enter_context(nc.sbuf_tensor(f"{tag}_ob", [128, 8, 512], BF16))
        pt = es.enter_context(nc.sbuf_tensor(f"{tag}_pt", [128, 2, 512], BF16))
        wdk = es.enter_context(nc.sbuf_tensor(f"{tag}_wdksb", [128, 8, 128], BF16))
        wdv = es.enter_context(nc.sbuf_tensor(f"{tag}_wdvsb", [128, 8, 128], BF16))
        wuk = es.enter_context(nc.sbuf_tensor(f"{tag}_wuksb", [128, 1024], BF16))
        wuv = es.enter_context(nc.sbuf_tensor(f"{tag}_wuvsb", [128, 1024], BF16))
        wo = es.enter_context(nc.sbuf_tensor(f"{tag}_wosb", [128, 8, 512], BF16))
        bdk = es.enter_context(nc.sbuf_tensor(f"{tag}_bdksb", [128, 2], F32))
        buk = es.enter_context(nc.sbuf_tensor(f"{tag}_buksb", [128, 8, 2], F32))
        bo = es.enter_context(nc.sbuf_tensor(f"{tag}_bosb", [128, 8], F32))
        cosb = es.enter_context(nc.sbuf_tensor(f"{tag}_cossb", [128, 512], BF16))
        sinb = es.enter_context(nc.sbuf_tensor(f"{tag}_sinsb", [128, 512], BF16))
        perm = es.enter_context(nc.sbuf_tensor(f"{tag}_permsb", [128, 128], BF16))
        ckb = es.enter_context(nc.sbuf_tensor(f"{tag}_ckb", [128, 512], BF16))
        cvb = es.enter_context(nc.sbuf_tensor(f"{tag}_cvb", [128, 512], BF16))
        vtmp = es.enter_context(nc.sbuf_tensor(f"{tag}_vtmp", [128, 512], BF16))
        rtmp = es.enter_context(nc.sbuf_tensor(f"{tag}_rtmp", [128, 2, 512], BF16))
        osc = es.enter_context(nc.sbuf_tensor(f"{tag}_osc", [64, 512], BF16))
        den = es.enter_context(nc.sbuf_tensor(f"{tag}_den", [1, 512], F32))
        mk = tag
        for name, dst, key, cast in (
                ("w_dk", wdk, "wdk", True), ("w_dv", wdv, "wdv", True),
                ("w_uk", wuk, "wuk", True), ("w_uv", wuv, "wuv", True),
                ("perm", perm, "perm", True)):
            if name in ("w_dk", "w_dv"):
                inap = W[name].rearrange("(kt p) c -> p kt c", p=128)
            else:
                inap = W[name]
            S.emit("gpsimd", lambda h, inap=inap, dst=dst: h.dma_start(
                out=dst[:], in_=inap), writes=(mk + key,), dma=True)
        for name, dst, key in (("b_dkv", bdk, "bdk"), ("b_ukv", buk, "buk"),
                               ("b_o", bo, "bo")):
            S.emit("sync", lambda h, name=name, dst=dst: h.dma_start(
                out=dst[:], in_=W[name]), writes=(mk + key,), dma=True)
        for q in range(16):
            emit_ln(C, xn, "xnorm", q * 128, 128, tag + str(q), ps, x2, rows)
        S.emit("vector", lambda h: h.memset(vt[:, :, 64::65], 1.0),
               writes=(mk + "vt",))
        for ch in range(4):
            t0 = ch * tc
            csl = slice(t0, t0 + tc)
            def _ck(h, csl=csl):
                for kk in range(8):
                    last = h.matmul(ps[:, 0, :], wdk[:, kk, :], xn[:, kk, csl],
                                    start=(kk == 0), stop=(kk == 7))
                return last
            S.emit("tensor", _ck, reads=(mk + "wdk", "xnorm"), writes=("psb0",))
            S.emit("scalar", lambda h: h.activation(
                out=ckb[:], in_=ps[:, 0, :], func=AF.Identity, bias=bdk[:, 0:1]),
                reads=("psb0", mk + "bdk"), writes=(mk + "ckb",))
            def _cv(h, csl=csl):
                for kk in range(8):
                    last = h.matmul(ps[:, 1, :], wdv[:, kk, :], xn[:, kk, csl],
                                    start=(kk == 0), stop=(kk == 7))
                return last
            S.emit("tensor", _cv, reads=(mk + "wdv", "xnorm"), writes=("psb1",))
            S.emit("scalar", lambda h: h.activation(
                out=cvb[:], in_=ps[:, 1, :], func=AF.Identity, bias=bdk[:, 1:2]),
                reads=("psb1", mk + "bdk"), writes=(mk + "cvb",))
            for i in range(8):
                S.emit("tensor", lambda h, i=i: h.matmul(
                    ps[:, 2, :], wuk[:, ts(i)], ckb[:],
                    start=True, stop=True),
                    reads=(mk + "wuk", mk + "ckb"), writes=("psb2",))
                S.emit("scalar", lambda h, i=i, csl=csl: h.activation(
                    out=kb[:, i, csl], in_=ps[:, 2, :], func=AF.Identity,
                    bias=buk[:, i, 0:1]),
                    reads=("psb2", mk + "buk"), writes=(mk + "kb",))
                S.emit("tensor", lambda h, i=i: h.matmul(
                    ps[:, 3, :], wuv[:, ts(i)], cvb[:],
                    start=True, stop=True),
                    reads=(mk + "wuv", mk + "cvb"), writes=("psb3",))
                S.emit("scalar", lambda h, i=i: h.activation(
                    out=vtmp[:], in_=ps[:, 3, :], func=AF.Identity,
                    bias=buk[:, i, 1:2]),
                    reads=("psb3", mk + "buk"), writes=(mk + "vtmp",))
                for tt4 in range(4):
                    pb = 6 + tt4 % 2
                    S.emit("tensor", lambda h, tt4=tt4, pb=pb: h.transpose(
                        psb16[:, pb, 0:128], vtmp[:, tt4 * 128:(tt4 + 1) * 128],
                        C.ident_b[:]),
                        reads=(mk + "vtmp", "ident_b"), writes=(f"psb{pb}",))
                    ttg = ch * 4 + tt4
                    S.emit("scalar", lambda h, i=i, ttg=ttg, pb=pb: h.copy(
                        out=vt[:, ttg, (2 * i) * 65:(2 * i) * 65 + 64],
                        in_=psb16[:, pb, 0:64]),
                        reads=(f"psb{pb}",), writes=(mk + "vt",))
                    S.emit("scalar", lambda h, i=i, ttg=ttg, pb=pb: h.copy(
                        out=vt[:, ttg, (2 * i + 1) * 65:(2 * i + 1) * 65 + 64],
                        in_=psb16[:, pb, 64:128]),
                        reads=(f"psb{pb}",), writes=(mk + "vt",))
            S.emit("gpsimd", lambda h, csl=csl: h.dma_start(
                out=cosb[:], in_=W["cos"][:, csl]),
                writes=(mk + "cos",), dma=True)
            S.emit("gpsimd", lambda h, csl=csl: h.dma_start(
                out=sinb[:], in_=W["sin"][:, csl]),
                writes=(mk + "sin",), dma=True)
            for buf, bkey in ((xn, "xnorm"), (kb, mk + "kb")):
                for dt in range(8):
                    pb = 4 + dt % 2
                    S.emit("tensor", lambda h, buf=buf, dt=dt, csl=csl, pb=pb: h.matmul(
                        ps[:, pb, :], perm[:], buf[:, dt, csl],
                        start=True, stop=True),
                        reads=(bkey, mk + "perm"), writes=(f"psb{pb}",))
                    S.emit("vector", lambda h, buf=buf, dt=dt, csl=csl: h.tensor_tensor(
                        out=rtmp[:, 0, :], in0=buf[:, dt, csl], in1=cosb[:],
                        op=ALU.mult),
                        reads=(bkey, mk + "cos"), writes=(mk + "rtmp",))
                    S.emit("vector", lambda h, dt=dt, csl=csl, pb=pb: h.tensor_tensor(
                        out=rtmp[:, 1, :], in0=ps[:, pb, :], in1=sinb[:],
                        op=ALU.mult),
                        reads=(f"psb{pb}", mk + "sin"), writes=(mk + "rtmp",))
                    S.emit("vector", lambda h, buf=buf, dt=dt, csl=csl: h.tensor_add(
                        out=buf[:, dt, csl], in0=rtmp[:, 0, :], in1=rtmp[:, 1, :]),
                        reads=(mk + "rtmp",), writes=(bkey,)),
        for ch in range(4):
            for hh in range(16):
                hb = 64 * (hh % 2)
                hdt = hh // 2
                qsl = slice(ch * 512, (ch + 1) * 512)
                pvb = 4 + hh % 2
                for tkt in range(16):
                    sb = tkt % 3
                    S.emit("tensor", lambda h, hb=hb, hdt=hdt, tkt=tkt, qsl=qsl, sb=sb: h.matmul(
                        ps[:, sb, :], kb[hb:hb + 64, hdt, ts(tkt)],
                        xn[hb:hb + 64, hdt, qsl], start=True, stop=True),
                        reads=(mk + "kb", "xnorm"), writes=(f"psb{sb}",))
                    psl = tkt % 2
                    S.emit("scalar", lambda h, tkt=tkt, sb=sb, psl=psl: h.activation(
                        out=pt[:, psl, :], in_=ps[:, sb, :],
                        func=AF.Exp, scale=0.125),
                        reads=(f"psb{sb}",), writes=(f"{mk}pt{psl}",))
                    S.emit("tensor", lambda h, hh=hh, tkt=tkt, psl=psl, pvb=pvb: h.matmul(
                        ps[0:65, pvb, :], vt[:, tkt, hh * 65:(hh + 1) * 65],
                        pt[:, psl, :],
                        start=(tkt == 0), stop=(tkt == 15)),
                        reads=(mk + "vt", f"{mk}pt{psl}"), writes=(f"psb{pvb}",))
                S.emit("scalar", lambda h, pvb=pvb: h.copy(
                    out=osc[:], in_=ps[0:64, pvb, :]),
                    reads=(f"psb{pvb}",), writes=(mk + "osc",))
                S.emit("scalar", lambda h, pvb=pvb: h.copy(
                    out=den[:], in_=ps[64:65, pvb, :]),
                    reads=(f"psb{pvb}",), writes=(mk + "den",))
                S.emit("vector", lambda h: h.reciprocal(
                    out=den[:], in_=den[:]),
                    reads=(mk + "den",), writes=(mk + "den",))
                S.emit("tensor", lambda h: h.matmul(
                    ps[0:64, 6, :], C.ones_row[0:1, 0:64], den[:],
                    start=True, stop=True),
                    reads=(mk + "den", "ones_row"), writes=("psb6",))
                S.emit("vector", lambda h, hb=hb, hdt=hdt: h.tensor_tensor(
                    out=obuf[hb:hb + 64, hdt, :], in0=osc[:],
                    in1=ps[0:64, 6, :], op=ALU.mult),
                    reads=(mk + "osc", "psb6"), writes=(mk + "ob",))
            for wh in range(2):
                S.emit("gpsimd", lambda h, wh=wh: h.dma_start(
                    out=wo[:], in_=W["w_o"].rearrange(
                        "(kt p) m -> p kt m", p=128)[:, :, wh * 512:(wh + 1) * 512]),
                    writes=(mk + "wo",), dma=True)
                for i in range(4):
                    ig = wh * 4 + i
                    pb = 6 + i % 2

                    def _wo(h, i=i, pb=pb):
                        for kk in range(8):
                            last = h.matmul(
                                ps[:, pb, :], wo[:, kk, ts(i)], obuf[:, kk, :],
                                start=(kk == 0), stop=(kk == 7))
                        return last
                    S.emit("tensor", _wo, reads=(mk + "wo", mk + "ob"),
                           writes=(f"psb{pb}",))
                    S.emit("scalar", lambda h, ig=ig, pb=pb: h.activation(
                        out=rtmp[:, 0, :], in_=ps[:, pb, :], func=AF.Identity,
                        bias=bo[:, ig:ig + 1]),
                        reads=(f"psb{pb}", mk + "bo"), writes=(mk + "rtmp",))
                    S.emit("vector", lambda h, ig=ig, ch=ch: h.tensor_add(
                        out=C.x_res[:, ig, ch * 512:(ch + 1) * 512],
                        in0=C.x_res[:, ig, ch * 512:(ch + 1) * 512],
                        in1=rtmp[:, 0, :]),
                        reads=(mk + "rtmp", "x_res"), writes=("x_res",))
        S.barrier()
        S.flush()


def build_program(stages, io_mode):
    nc = bass.Bass("TRN2", target_bir_lowering=False)
    C = Ctx()
    C.nc = nc
    C.dram = {}
    C.dram_w = {}
    declare_dram(C, stages, io_mode)
    with contextlib.ExitStack() as es0:
        sems = {}
        for e in ENGINES:
            for p in range(Sched.NPOOL[e]):
                sems[f"{e}:c{p}"] = es0.enter_context(
                    nc.semaphore(f"sem_{e}_c{p}"))
            for p in range(Sched.NDMA.get(e, 0)):
                sems[f"{e}:d{p}"] = es0.enter_context(
                    nc.semaphore(f"sem_{e}_d{p}"))
        block = es0.enter_context(nc.Block())
        C.S = Sched(nc, block, sems)
        C.x_res = es0.enter_context(nc.sbuf_tensor("x_res", [128, NDT, T_FULL], F32))
        build_consts(C, es0)
        C.S.barrier()
        if io_mode == "full":
            with contextlib.ExitStack() as es:
                stage_pro(C, es)
                C.S.barrier()
                C.S.flush()
        else:
            with contextlib.ExitStack() as es:
                stage_load_res(C, es)
                C.S.barrier()
                C.S.flush()
        for st in stages:
            with contextlib.ExitStack() as es:
                STAGE_FNS[st](C, es)
                C.S.flush()
        if io_mode == "full":
            with contextlib.ExitStack() as es:
                C.S.barrier()
                stage_epi(C, es)
                C.S.flush()
        else:
            with contextlib.ExitStack() as es:
                C.S.barrier()
                stage_store_res(C, es)
                C.S.flush()
        C.S.emit("sync", lambda h: h.nop(), reads=("out_done",))
        C.S.flush()
    return C


def declare_dram(C, stages, io_mode):
    nc = C.nc
    if io_mode == "full":
        C.dram["x_tm"] = nc.dram_tensor("x_tm", [T_FULL, D_MODEL], F32,
                                        kind="ExternalInput").ap()
        C.dram["out_tm"] = nc.dram_tensor("out_tm", [T_FULL, D_MODEL], F32,
                                          kind="ExternalOutput").ap()
    else:
        C.dram["xin_fm"] = nc.dram_tensor("xin_fm", [D_MODEL, T_FULL], F32,
                                          kind="ExternalInput").ap()
        C.dram["out_tm"] = nc.dram_tensor("xout_fm", [D_MODEL, T_FULL], F32,
                                          kind="ExternalOutput").ap()
        C.dram["xout_fm"] = C.dram["out_tm"]
    for st in stages:
        if st in ("l0a", "l0b"):
            nm = 1 if st == "l0a" else 4
            tsub = T_FULL if st == "l0a" else 512
            W = {}
            W["w_inT"] = nc.dram_tensor(f"{st}_w_inT", [nm, 1024, 4096], F32,
                                        kind="ExternalInput").ap()
            W["conv_w"] = nc.dram_tensor(f"{st}_conv_w", [nm, 128, 16, 4], F32,
                                         kind="ExternalInput").ap()
            W["conv_b"] = nc.dram_tensor(f"{st}_conv_b", [nm, 128, 16], F32,
                                         kind="ExternalInput").ap()
            W["w_xprojT"] = nc.dram_tensor(f"{st}_w_xprojT", [nm, 2048, 128], F32,
                                           kind="ExternalInput").ap()
            W["w_dtT"] = nc.dram_tensor(f"{st}_w_dtT", [nm, 64, 2048], F32,
                                        kind="ExternalInput").ap()
            W["b_dt"] = nc.dram_tensor(f"{st}_b_dt", [nm, 128, 16], F32,
                                       kind="ExternalInput").ap()
            W["A_log"] = nc.dram_tensor(f"{st}_A_log", [nm, 128, 256], F32,
                                        kind="ExternalInput").ap()
            W["D"] = nc.dram_tensor(f"{st}_D", [nm, 128, 16], F32,
                                    kind="ExternalInput").ap()
            W["w_outT"] = nc.dram_tensor(f"{st}_w_outT", [nm, 2048, 1024], F32,
                                         kind="ExternalInput").ap()
            C.dram_w[st] = W
            C.dram[f"{st}_xz"] = nc.dram_tensor(f"{st}_xz", [nm, 4096, tsub], BF16,
                                                kind="Internal").ap()
        elif st in ("l0f1", "l0f2") and "l0f" not in C.dram_w:
            W = {}
            W["gate_wT"] = nc.dram_tensor("l0f_gate_wT", [1024, 16], F32,
                                          kind="ExternalInput").ap()
            W["gate_b"] = nc.dram_tensor("l0f_gate_b", [1, 16], F32,
                                         kind="ExternalInput").ap()
            W["w13T"] = nc.dram_tensor("l0f_w13T", [16, 1024, 1024], F32,
                                       kind="ExternalInput").ap()
            W["w2T"] = nc.dram_tensor("l0f_w2T", [16, 512, 1024], F32,
                                      kind="ExternalInput").ap()
            W["w13sT"] = nc.dram_tensor("l0f_w13sT", [1024, 1024], F32,
                                        kind="ExternalInput").ap()
            W["w2sT"] = nc.dram_tensor("l0f_w2sT", [512, 1024], F32,
                                       kind="ExternalInput").ap()
            C.dram_w["l0f"] = W
        elif st in ("l1f1", "l1f2") and "l1f" not in C.dram_w:
            W = {}
            W["w_gate"] = nc.dram_tensor("l1f_w_gate", [1024, 8], F32,
                                         kind="ExternalInput").ap()
            W["b_gate"] = nc.dram_tensor("l1f_b_gate", [1, 8], F32,
                                         kind="ExternalInput").ap()
            W["W1"] = nc.dram_tensor("l1f_W1", [8, 1024, 4096], F32,
                                     kind="ExternalInput").ap()
            W["W2"] = nc.dram_tensor("l1f_W2", [8, 4096, 1024], F32,
                                     kind="ExternalInput").ap()
            W["b1p"] = nc.dram_tensor("l1f_b1p", [128, 32, 8], F32,
                                      kind="ExternalInput").ap()
            W["b2"] = nc.dram_tensor("l1f_b2", [8, 1024], F32,
                                     kind="ExternalInput").ap()
            C.dram_w["l1f"] = W
        elif st == "l1a":
            W = {}
            W["w_inT"] = nc.dram_tensor("l1a_w_inT", [1024, 5248], F32,
                                        kind="ExternalInput").ap()
            W["w_resT"] = nc.dram_tensor("l1a_w_resT", [1024, 2048], F32,
                                         kind="ExternalInput").ap()
            W["w_outT"] = nc.dram_tensor("l1a_w_outT", [2048, 1024], F32,
                                         kind="ExternalInput").ap()
            W["w_dtT"] = nc.dram_tensor("l1a_w_dtT", [64, 8], F32,
                                        kind="ExternalInput").ap()
            W["conv_w"] = nc.dram_tensor("l1a_conv_w", [128, 16, 4], F32,
                                         kind="ExternalInput").ap()
            W["conv_b"] = nc.dram_tensor("l1a_conv_b", [128, 16], F32,
                                         kind="ExternalInput").ap()
            W["dt_bias"] = nc.dram_tensor("l1a_dt_bias", [8, 1], F32,
                                          kind="ExternalInput").ap()
            W["A_flat"] = nc.dram_tensor("l1a_A_flat", [1, 128], F32,
                                         kind="ExternalInput").ap()
            W["Dskip"] = nc.dram_tensor("l1a_Dskip", [128, 16], F32,
                                        kind="ExternalInput").ap()
            W["norm_w"] = nc.dram_tensor("l1a_norm_w", [128, 16], F32,
                                         kind="ExternalInput").ap()
            C.dram_w["l1a"] = W
            C.dram["l1a_p"] = nc.dram_tensor("l1a_p", [5248, 2048], BF16,
                                             kind="Internal").ap()
            C.dram["l1a_r"] = nc.dram_tensor("l1a_r", [2048, 2048], BF16,
                                             kind="Internal").ap()
        elif st == "l1b":
            W = {}
            for nm, shp in (("w_dk", [1024, 128]), ("w_dv", [1024, 128]),
                            ("w_uk", [128, 1024]), ("w_uv", [128, 1024]),
                            ("w_o", [1024, 1024]), ("cos", [128, 2048]),
                            ("sin", [128, 2048]), ("perm", [128, 128]),
                            ("b_dkv", [128, 2]), ("b_ukv", [128, 8, 2]),
                            ("b_o", [128, 8])):
                W[nm] = nc.dram_tensor(f"l1b_{nm}", shp, F32,
                                       kind="ExternalInput").ap()
            C.dram_w["l1b"] = W


def stage_l0f1(C, es):
    emit_l0ffn(C, "l0f1")


def stage_l0f2(C, es):
    emit_l0ffn(C, "l0f2")


def stage_l1f1(C, es):
    emit_l1ffn(C, "l1f1")


def stage_l1f2(C, es):
    emit_l1ffn(C, "l1f2")


def stage_l1a(C, es):
    emit_l1a(C)


def stage_l1b(C, es):
    emit_l1b(C)


STAGE_FNS = {
    "l1b": stage_l1b,
    "l1a": stage_l1a,
    "l0a": stage_l0a,
    "l0b": stage_l0b,
    "l0f1": stage_l0f1,
    "l0f2": stage_l0f2,
    "l1f1": stage_l1f1,
    "l1f2": stage_l1f2,
}


def prep_weights(I):
    out = {}

    def mamba1_pack(pfx, w_in, conv_w, conv_b, w_xproj, w_dt, b_dt, a_log, d_skip,
                    w_out):
        nm = w_in.shape[0]
        o = {}
        o["w_inT"] = np.ascontiguousarray(np.swapaxes(w_in, 1, 2))
        o["conv_w"] = np.ascontiguousarray(
            conv_w.reshape(nm, 16, 128, 4).transpose(0, 2, 1, 3))
        o["conv_b"] = np.ascontiguousarray(
            conv_b.reshape(nm, 16, 128).transpose(0, 2, 1))
        xp = np.zeros((nm, 2048, 128), np.float32)
        xp[:, :, 0:16] = np.swapaxes(w_xproj[:, 64:80, :], 1, 2)
        xp[:, :, 32:48] = np.swapaxes(w_xproj[:, 80:96, :], 1, 2)
        xp[:, :, 64:128] = np.swapaxes(w_xproj[:, 0:64, :], 1, 2)
        o["w_xprojT"] = xp
        o["w_dtT"] = np.ascontiguousarray(np.swapaxes(w_dt, 1, 2))
        o["b_dt"] = np.ascontiguousarray(b_dt.reshape(nm, 16, 128).transpose(0, 2, 1))
        o["A_log"] = np.ascontiguousarray(
            a_log.reshape(nm, 16, 128, 16).transpose(0, 2, 1, 3).reshape(nm, 128, 256))
        o["D"] = np.ascontiguousarray(d_skip.reshape(nm, 16, 128).transpose(0, 2, 1))
        o["w_outT"] = np.ascontiguousarray(np.swapaxes(w_out, 1, 2))
        return {f"{pfx}_{k}": v for k, v in o.items()}

    out.update(mamba1_pack(
        "l0a", I["L0_attn_a_W_in"][None], I["L0_attn_a_conv_w"][None],
        I["L0_attn_a_conv_b"][None], I["L0_attn_a_W_xproj"][None],
        I["L0_attn_a_W_dt"][None], I["L0_attn_a_b_dt"][None],
        I["L0_attn_a_A_log"][None], I["L0_attn_a_D_skip"][None],
        I["L0_attn_a_W_out"][None]))
    out.update(mamba1_pack(
        "l0b", I["L0_attn_b_in_w"], I["L0_attn_b_conv_w"], I["L0_attn_b_conv_b"],
        I["L0_attn_b_xproj_w"], I["L0_attn_b_dt_w"], I["L0_attn_b_dt_b"],
        I["L0_attn_b_A_log"], I["L0_attn_b_D_skip"], I["L0_attn_b_out_w"]))
    out["l0f_gate_wT"] = np.swapaxes(I["L0_ffn_gate_w"], 0, 1)
    out["l0f_gate_b"] = np.asarray(I["L0_ffn_gate_b"])[None, :]
    out["l0f_w13T"] = np.swapaxes(I["L0_ffn_w13"], 1, 2)
    out["l0f_w2T"] = np.swapaxes(I["L0_ffn_w2"], 1, 2)
    out["l0f_w13sT"] = np.swapaxes(I["L0_ffn_w13_s"], 0, 1)
    out["l0f_w2sT"] = np.swapaxes(I["L0_ffn_w2_s"], 0, 1)
    out["l1f_w_gate"] = np.asarray(I["L1_ffn_w_gate"])
    out["l1f_b_gate"] = np.asarray(I["L1_ffn_b_gate"])[None, :]
    out["l1f_W1"] = np.asarray(I["L1_ffn_W1"])
    out["l1f_W2"] = np.asarray(I["L1_ffn_W2"])
    out["l1f_b1p"] = np.asarray(I["L1_ffn_b1"]).reshape(8, 32, 128).transpose(2, 1, 0)
    out["l1f_b2"] = np.asarray(I["L1_ffn_b2"])
    win = np.asarray(I["L1_attn_a_W_in"])
    wp = np.zeros((5248, 1024), np.float32)
    wp[0:4160] = win[0:4160]
    for hh in range(8):
        wp[4224 + 64 * hh: 4224 + 64 * hh + 16] = win[4160 + 16 * hh: 4160 + 16 * (hh + 1)]
        wp[4736 + 64 * hh: 4736 + 64 * hh + 16] = win[4288 + 16 * hh: 4288 + 16 * (hh + 1)]
    out["l1a_w_inT"] = np.ascontiguousarray(wp.T)
    out["l1a_w_resT"] = np.ascontiguousarray(np.asarray(I["L1_attn_a_W_res"]).T)
    out["l1a_w_outT"] = np.ascontiguousarray(np.asarray(I["L1_attn_a_W_out"]).T)
    out["l1a_w_dtT"] = np.ascontiguousarray(np.asarray(I["L1_attn_a_W_dt"]).T)
    out["l1a_conv_w"] = np.ascontiguousarray(
        np.asarray(I["L1_attn_a_conv_w"]).reshape(16, 128, 4).transpose(1, 0, 2))
    out["l1a_conv_b"] = np.ascontiguousarray(
        np.asarray(I["L1_attn_a_conv_b"]).reshape(16, 128).T)
    out["l1a_dt_bias"] = np.asarray(I["L1_attn_a_dt_bias"])[:, None]
    out["l1a_A_flat"] = np.asarray(I["L1_attn_a_A_log"]).reshape(1, 128)
    out["l1a_Dskip"] = np.ascontiguousarray(
        np.asarray(I["L1_attn_a_Dskip"]).reshape(2048).reshape(16, 128).T)
    out["l1a_norm_w"] = np.ascontiguousarray(
        np.asarray(I["L1_attn_a_norm_w"]).reshape(16, 128).T)
    out["l1b_w_dk"] = np.asarray(I["L1_attn_b_w_dk"])
    out["l1b_w_dv"] = np.asarray(I["L1_attn_b_w_dv"])
    out["l1b_w_uk"] = np.asarray(I["L1_attn_b_w_uk"])
    out["l1b_w_uv"] = np.asarray(I["L1_attn_b_w_uv"])
    out["l1b_w_o"] = np.asarray(I["L1_attn_b_w_o"])
    out["l1b_b_dkv"] = np.stack([np.asarray(I["L1_attn_b_b_dk"]),
                                 np.asarray(I["L1_attn_b_b_dv"])], axis=1)
    out["l1b_b_ukv"] = np.stack(
        [np.asarray(I["L1_attn_b_b_uk"]).reshape(8, 128).T,
         np.asarray(I["L1_attn_b_b_uv"]).reshape(8, 128).T], axis=2)
    out["l1b_b_o"] = np.ascontiguousarray(
        np.asarray(I["L1_attn_b_b_o"]).reshape(8, 128).T)
    t = np.arange(2048, dtype=np.float64)
    invf = 10000.0 ** (-(np.arange(128) % 32) / 32.0)
    ang = t[None, :] * invf[:, None]
    out["l1b_cos"] = np.cos(ang).astype(np.float32)
    out["l1b_sin"] = np.sin(ang).astype(np.float32)
    pm = np.zeros((128, 128), np.float32)
    for i in range(128):
        j = i % 64
        if j < 32:
            pm[i + 32, i] = -1.0
        else:
            pm[i - 32, i] = 1.0
    out["l1b_perm"] = pm
    return {k: np.ascontiguousarray(np.asarray(v, np.float32)) for k, v in out.items()}


def input_names_for(stages):
    pref = {"l0a": "l0a_", "l0b": "l0b_", "l0f1": "l0f_", "l0f2": "l0f_",
            "l1a": "l1a_", "l1b": "l1b_", "l1f1": "l1f_", "l1f2": "l1f_"}
    return sorted({p for st in stages for p in [pref[st]]})


def run_chain(inputs, launch_groups=None):
    launch_groups = launch_groups or LAUNCH_GROUPS
    x = np.asarray(inputs["L0_attn_a_hidden_states"], np.float32)
    wmap = prep_weights(inputs)
    n_groups = len(launch_groups)
    cur = None
    for gi, stages in enumerate(launch_groups):
        first = gi == 0
        last = gi == n_groups - 1
        io_mode = "full" if n_groups == 1 else ("chain" if not (first or last) else
                                                ("full" if n_groups == 1 else "chain"))
        C = build_program(stages, "chain")
        names = needed_weight_names(stages)
        in_maps = []
        for c in range(N_CORES):
            m = {"xin_fm": cur[c]}
            for nmname in names:
                m[nmname] = wmap[nmname]
            in_maps.append(m)
        res = run_bass_kernel_spmd(C.nc, in_maps, core_ids=list(range(N_CORES)))
        cur = np.stack([r["xout_fm"] for r in res.results])
    return cur


def needed_weight_names(stages):
    names = []
    for st in stages:
        if st in ("l0a", "l0b"):
            names += [f"{st}_{k}" for k in ("w_inT", "conv_w", "conv_b", "w_xprojT",
                                            "w_dtT", "b_dt", "A_log", "D", "w_outT")]
        elif st in ("l0f1", "l0f2"):
            names += [f"l0f_{k}" for k in ("gate_wT", "gate_b", "w13T", "w2T",
                                           "w13sT", "w2sT")]
        elif st in ("l1f1", "l1f2"):
            names += [f"l1f_{k}" for k in ("w_gate", "b_gate", "W1", "W2",
                                           "b1p", "b2")]
        elif st == "l1a":
            names += [f"l1a_{k}" for k in ("w_inT", "w_resT", "w_outT", "w_dtT",
                                           "conv_w", "conv_b", "dt_bias",
                                           "A_flat", "Dskip", "norm_w")]
        elif st == "l1b":
            names += [f"l1b_{k}" for k in ("w_dk", "w_dv", "w_uk", "w_uv", "w_o",
                                           "cos", "sin", "perm", "b_dkv",
                                           "b_ukv", "b_o")]
    return sorted(set(names))


def kernel(**inputs):
    x = np.ascontiguousarray(np.asarray(inputs["L0_attn_a_hidden_states"],
                                        np.float32))
    wmap = prep_weights(inputs)
    C = build_program(ALL_STAGES, "full")
    names = needed_weight_names(ALL_STAGES)
    in_maps = []
    for c in range(N_CORES):
        m = {"x_tm": np.ascontiguousarray(x[c])}
        for nm in names:
            m[nm] = wmap[nm]
        in_maps.append(m)
    res = run_bass_kernel_spmd(C.nc, in_maps, core_ids=list(range(N_CORES)))
    return np.stack([np.asarray(r["out_tm"], np.float32) for r in res.results])
